# version 10
# baseline (speedup 1.0000x reference)
import math
import numpy as np
from contextlib import ExitStack

import concourse.bass as bass
import concourse.mybir as mybir
from concourse.bass_utils import run_bass_kernel_spmd

F32 = mybir.dt.float32
BF16 = mybir.dt.bfloat16
AF = mybir.ActivationFunctionType
ALU = mybir.AluOpType
AX = mybir.AxisListType

NCORES = 8
S = 2048
D = 1024
NCH = D // 128
TT = 512
NTT = S // TT
EPS = 1e-5
C_FULL = 40.0
OH_SCALE = 64.0


class Prog:
    ENGINES = ("pe", "dve", "act", "pool", "sp")

    def __init__(self, nc, es):
        self.nc = nc
        self.es = es
        self.eng = {"pe": "tensor", "dve": "vector", "act": "scalar",
                    "pool": "gpsimd", "sp": "sync"}
        self.sem = {}
        self.cnt = {}
        self.step = {}
        self.stream_q = {}
        self.frozen = set()
        self.max_waited = {}
        self.eclock = {e: {} for e in self.ENGINES}
        self.reg = {}
        self.queue = {e: [] for e in self.ENGINES}
        self.envp = None
        self.cur_phase = "init"
        self.pe_phases = []
        for e in ("pe", "dve", "act", "pool"):
            self._mksem("s_" + e, 1)

    def _mksem(self, name, step):
        self.sem[name] = self.es.enter_context(self.nc.semaphore(name))
        self.cnt[name] = 0
        self.step[name] = step

    def freeze_stream(self, name):
        self.frozen.add(name)

    def stream(self, name, qeng):
        assert qeng in ("sp", "pool", "act")
        self._mksem(name, 16)
        self.stream_q[name] = qeng
        return name

    def _region(self, key):
        r = self.reg.get(key)
        if r is None:
            r = {"w": {}, "r": {}}
            if "@" in key:
                parent = self.reg.get(key.split("@", 1)[0])
                if parent is not None:
                    r["w"] = dict(parent["w"])
            self.reg[key] = r
        return r

    def op(self, e, fn, reads=(), writes=(), stream=None):
        need = {}
        wr = [self._region(k) for k in writes]
        for k in reads:
            for s, c in self._region(k)["w"].items():
                if need.get(s, 0) < c:
                    need[s] = c
        for r in wr:
            for m in (r["w"], r["r"]):
                for s, c in m.items():
                    if need.get(s, 0) < c:
                        need[s] = c
        ec = self.eclock[e]
        waits = []
        for s in sorted(need):
            c = need[s]
            if s in self.stream_q:
                c = self.cnt[s]
            if ec.get(s, 0) < c:
                waits.append((s, c))
                ec[s] = c
                if s in self.stream_q and self.max_waited.get(s, 0) < c:
                    self.max_waited[s] = c
        if stream is not None:
            w = self.max_waited.get(stream, 0)
            if ec.get(stream, 0) < w:
                waits.append((stream, w))
                ec[stream] = w
        for k in reads:
            for s, c in self._region(k)["w"].items():
                if ec.get(s, 0) < c:
                    ec[s] = c
        if stream is None:
            sname = "s_" + e
        else:
            sname = stream
            assert self.stream_q[stream] == e, (stream, e)
            assert stream not in self.frozen, f"issue on frozen stream {stream}"
        self.cnt[sname] += self.step[sname]
        ident = self.cnt[sname]
        env = self.envp.env() if self.envp is not None else None
        self.queue[e].append((waits, fn, sname, env))
        if e == "pe":
            self.pe_phases.append(self.cur_phase)
        wclock = dict(ec)
        wclock[sname] = max(wclock.get(sname, 0), ident)
        for r in wr:
            r["w"] = wclock
            r["r"] = {}
        for k in reads:
            rr = self._region(k)["r"]
            if rr.get(sname, 0) < ident:
                rr[sname] = ident
        return (sname, ident)

    def wait_all(self, e, items):
        ec = self.eclock[e]
        waits = []
        for s, c in items:
            if ec.get(s, 0) < c:
                waits.append((s, c))
                ec[s] = c
        if waits:
            self.queue[e].append((waits, None, None, None))

    def wait_region(self, e, keys):
        items = {}
        for k in keys:
            for s, c in self._region(k)["w"].items():
                if s in self.stream_q:
                    c = self.cnt[s]
                if items.get(s, 0) < c:
                    items[s] = c
        self.wait_all(e, sorted(items.items()))

    def finalize(self):
        block = self.es.enter_context(self.nc.Block())
        for e in self.ENGINES:
            q = self.queue[e]
            if not q:
                continue
            bass_eng = getattr(self.nc, self.eng[e])

            def run(engine, q=q, prog=self, e=e):
                if e == "pe":
                    cnt = [0]
                    class _Px:
                        def __getattr__(px, k):
                            a = getattr(engine, k)
                            if k in ("matmul", "transpose"):
                                def w(*ar, **kw):
                                    cnt[0] += 1
                                    return a(*ar, **kw)
                                return w
                            return a
                    px = _Px()
                    prog.pe_mm_cum = []
                    for waits, fn, sname, env in q:
                        for s, c in waits:
                            engine.wait_ge(prog.sem[s], c)
                        if fn is not None:
                            if env is not None:
                                prog.envp.restore(env)
                            instr = fn(px)
                            instr.then_inc(prog.sem[sname], prog.step[sname])
                        prog.pe_mm_cum.append(cnt[0])
                    return
                for waits, fn, sname, env in q:
                    for s, c in waits:
                        engine.wait_ge(prog.sem[s], c)
                    if fn is not None:
                        if env is not None:
                            prog.envp.restore(env)
                        instr = fn(engine)
                        instr.then_inc(prog.sem[sname], prog.step[sname])

            getattr(block, self.eng[e])(run)


class Ctx:
    def __init__(self, nc, es, prog):
        self.nc = nc
        self.es = es
        self.p = prog
        self.t = {}
        self.xs_off = 0
        self._scopes = []
        self._dead = []
        self._snaps = []
        self._dirty = True
        self._cur_env = -1
        prog.envp = self

    def env(self):
        if self._dirty:
            self._snaps.append(dict(self.t))
            self._dirty = False
        return len(self._snaps) - 1

    def restore(self, env):
        if env != self._cur_env:
            self.t.clear()
            self.t.update(self._snaps[env])
            self._cur_env = env

    def _addr(self, name):
        for f in self.nc.m.functions:
            for al in f.allocations:
                if isinstance(al, mybir.MemoryLocationSet):
                    ml = al.memorylocations[0]
                    if ml.name == name:
                        return ml.addr, ml.addr + ml.size()
        raise KeyError(name)

    _uid = 0

    def _alloc(self, kind, name, shape, dtype):
        es = self._scopes[-1][0] if self._scopes else self.es
        mk = self.nc.sbuf_tensor if kind == "sb" else self.nc.psum_tensor
        real = name
        if self._scopes:
            Ctx._uid += 1
            real = f"{name}_s{Ctx._uid}"
        h = es.enter_context(mk(real, shape, dtype))
        self.t[name] = h
        self._dirty = True
        lo, hi = self._addr(real)
        space = "sb" if kind == "sb" else "ps"
        if self._scopes:
            self._scopes[-1][1].append((name, space, lo, hi))
        clk = {}
        keep = []
        for (sp2, lo2, hi2, c2) in self._dead:
            if sp2 == space and lo < hi2 and lo2 < hi:
                for s, c in c2.items():
                    if clk.get(s, 0) < c:
                        clk[s] = c
            keep.append((sp2, lo2, hi2, c2))
        self._dead = keep
        if clk:
            r = self.p._region(name)
            for s, c in clk.items():
                if r["w"].get(s, 0) < c:
                    r["w"][s] = c
        return h

    def sb(self, name, shape, dtype):
        return self._alloc("sb", name, shape, dtype)

    def ps(self, name, shape, dtype):
        return self._alloc("ps", name, shape, dtype)

    def wload(self, wk, src_ap, cols=None):
        t = self.t
        ncols = cols if cols is not None else src_ap.shape[-1]
        rkey = "hbm:" + src_ap.tensor.name
        if src_ap.dtype == BF16:
            self.p.op("sp", lambda g, wk=wk, src_ap=src_ap, ncols=ncols:
                      g.dma_start(out=t[wk][:, 0:ncols], in_=src_ap),
                      reads=(rkey,), writes=(wk,), stream="d_" + wk)
        else:
            self.p.op("pool", lambda g, wk=wk, src_ap=src_ap, ncols=ncols:
                      g.dma_start(out=t[wk][:, 0:ncols], in_=src_ap),
                      reads=(rkey,), writes=(wk,), stream="dp_" + wk)

    def scope(self):
        from contextlib import contextmanager

        @contextmanager
        def _sc():
            es = ExitStack()
            self._scopes.append((es, []))
            try:
                yield
            finally:
                es2, names = self._scopes.pop()
                for (name, space, lo, hi) in names:
                    clk = {}
                    for key in [name] + [k for k in self.p.reg
                                         if k.startswith(name + "@")]:
                        r = self.p._region(key)
                        for m in (r["w"], r["r"]):
                            for s, c in m.items():
                                if clk.get(s, 0) < c:
                                    clk[s] = c
                        self.p.reg.pop(key, None)
                    self._dead.append((space, lo, hi, clk))
                    self.t.pop(name, None)
                self._dirty = True
                es2.close()
        return _sc()


def pack_w(w):
    Kd, Nd = w.shape
    nk, no = Kd // 128, Nd // 128
    return np.ascontiguousarray(
        w.reshape(nk, 128, no, 128).transpose(2, 1, 0, 3).reshape(no, 128, nk * 128))


def host_constants():
    c = {}
    c["CONST_ONES"] = np.ones((128, 128), np.float32)
    c["CONST_IDENT"] = np.eye(128, dtype=np.float32)
    c["CONST_IDENT2"] = np.vstack([np.eye(64, dtype=np.float32)] * 2)
    kk, qq = np.meshgrid(np.arange(128), np.arange(128), indexing="ij")
    c["CONST_TRIU"] = (qq >= kk).astype(np.float32)
    c["CONST_TRIL"] = (qq <= kk).astype(np.float32)
    DH, WIN = 64, 256
    half = np.arange(0, DH, 2, dtype=np.float32)
    inv_freq = 1.0 / 10000.0 ** (half / DH)
    tpos = np.arange(S, dtype=np.float32)
    freqs = tpos[:, None] * inv_freq[None, :]
    freqs = np.concatenate([freqs, freqs], -1)
    base_scale = (half + 0.4 * DH) / (1.4 * DH)
    power = (tpos - S // 2) / (WIN // 2)
    scale = base_scale[None, :] ** power[:, None]
    scale = np.concatenate([scale, scale], -1)
    cosf, sinf = np.cos(freqs), np.sin(freqs)
    st2 = lambda a: np.ascontiguousarray(np.vstack([a.T, a.T]).astype(np.float32))
    c["CONST_TBQC"] = st2(cosf * scale)
    c["CONST_TBQS"] = st2(sinf * scale)
    c["CONST_TBKC"] = st2(cosf / scale)
    c["CONST_TBKS"] = st2(sinf / scale)
    Rh = np.zeros((128, 128), np.float32)
    for blk in range(2):
        for m in range(64):
            if m < 32:
                Rh[blk * 64 + m, blk * 64 + m + 32] = -1.0
            else:
                Rh[blk * 64 + m, blk * 64 + m - 32] = 1.0
    c["CONST_RHT"] = np.ascontiguousarray(Rh.T)
    e2 = np.zeros((128, 2), np.float32)
    e2[0:64, 0] = 1.0
    e2[64:128, 1] = 1.0
    c["CONST_EHD2"] = e2
    c["CONST_E2T"] = np.ascontiguousarray(e2.T)
    return c


def make_std(ctx, n_w_tiles=8, w_cols=1024, psum=True, xs_cols=S):
    sb, ps = ctx.sb, ctx.ps
    for i in range(NCH):
        sb(f"XS{i}", [128, xs_cols], F32)
        sb(f"XN{i}", [128, S], BF16)
    sb("ONESb", [128, 128], BF16)
    sb("ONESf", [128, 128], F32)
    sb("IDENTb", [128, 128], BF16)
    sb("IDENT2b", [128, 64], BF16)
    sb("U0", [128, TT], BF16)
    sb("U1", [128, TT], BF16)
    for i in range(4):
        sb(f"SQ{i}", [128, TT], BF16)
    sb("CONSTS", [128, 8], F32)
    sb("AROW", [1, TT], F32)
    sb("BROW0", [1, TT], F32)
    sb("BROW1", [1, TT], F32)
    sb("MU", [1, TT], F32)
    sb("VAR", [1, TT], F32)
    for i in range(n_w_tiles):
        sb(f"W{i}", [128, w_cols], BF16)
    if psum:
        for i in range(8):
            ps(f"P{i}", [128, 512], F32)


CONST_COLS = {"eps": 0, "negC": 1, "zero": 2, "negBIGC": 4, "negCfull": 5}


def load_constants(ctx, streams):
    p = ctx.p
    nc = ctx.nc
    cc = ctx.t["CONSTS"]
    p.op("dve", lambda v: v.memset(cc[:, 0:1], EPS), writes=("CONSTS",))
    p.op("dve", lambda v: v.memset(cc[:, 1:2], -C_FULL), writes=("CONSTS",))
    p.op("dve", lambda v: v.memset(cc[:, 2:3], 0.0), writes=("CONSTS",))
    p.op("dve", lambda v: v.memset(cc[:, 4:5], -(OH_SCALE * OH_SCALE + C_FULL)), writes=("CONSTS",))
    p.op("dve", lambda v: v.memset(cc[:, 5:6], -C_FULL), writes=("CONSTS",))
    p.op("dve", lambda v: v.memset(cc[:, 6:7], 1e-24), writes=("CONSTS",))
    ones = nc.dram_tensor("CONST_ONES", [128, 128], F32, kind="ExternalInput").ap()
    ident = nc.dram_tensor("CONST_IDENT", [128, 128], F32, kind="ExternalInput").ap()
    t = ctx.t
    p.op("pool", lambda g: g.dma_start(out=t["ONESb"][:], in_=ones),
         reads=("c:ONES",), writes=("ONESb",), stream=streams["wg"])
    p.op("sp", lambda g: g.dma_start(out=t["ONESf"][:], in_=ones),
         reads=("c:ONES",), writes=("ONESf",), stream=streams["setup"])
    p.op("pool", lambda g: g.dma_start(out=t["IDENTb"][:], in_=ident),
         reads=("c:IDENT",), writes=("IDENTb",), stream=streams["wg"])
    ident2 = nc.dram_tensor("CONST_IDENT2", [128, 64], F32, kind="ExternalInput").ap()
    p.op("pool", lambda g: g.dma_start(out=t["IDENT2b"][:], in_=ident2),
         reads=("c:IDENT2",), writes=("IDENT2b",), stream=streams["wg"])


def emit_load_x(ctx, x_hbm, seq, streams, xs_prefix="XS", xs_off=None):
    p, t = ctx.p, ctx.t
    if xs_off is None:
        xs_off = ctx.xs_off
    for c in range(NCH):
        dst = t[f"{xs_prefix}{c}"]
        src = x_hbm[seq, c * 128:(c + 1) * 128, :]
        p.op("sp", lambda g, dst=dst, src=src, xs_off=xs_off: g.dma_start(
            out=dst[:, xs_off:xs_off + S], in_=src),
            reads=(f"hbm:x{seq}",), writes=(f"{xs_prefix}{c}",), stream=streams["xs"])


def emit_store_x(ctx, x_hbm, seq, streams, xs_prefix="XS", xs_off=None):
    p, t = ctx.p, ctx.t
    if xs_off is None:
        xs_off = ctx.xs_off
    for c in range(NCH):
        src = t[f"{xs_prefix}{c}"]
        dst = x_hbm[seq, c * 128:(c + 1) * 128, :]
        p.op("sp", lambda g, dst=dst, src=src, xs_off=xs_off: g.dma_start(
            out=dst, in_=src[:, xs_off:xs_off + S]),
            reads=(f"{xs_prefix}{c}",), writes=(f"hbm:x{seq}",), stream=streams["xs"])


def emit_cast_x(ctx, xs="XS", xn="XN"):
    p, t = ctx.p, ctx.t
    off = ctx.xs_off
    for c in range(NCH):
        src, dst = t[f"{xs}{c}"], t[f"{xn}{c}"]
        eng = "pool" if c % 2 == 0 else "dve"
        p.op(eng, lambda v, src=src, dst=dst, off=off: v.tensor_copy(
            dst[:, 0:S], src[:, off:off + S]),
            reads=(f"{xs}{c}",),
            writes=tuple(f"{xn}{c}@{i}" for i in range(NTT)))


def emit_ln(ctx, xn="XN", g_col=None, b_col=None, out=None, out_f32=None,
           srow=False, tile_hook=None):
    p, t = ctx.p, ctx.t
    banks = getattr(ctx, "ln_banks", ("P6", "P7"))
    out = out or xn
    def _banks(tt):
        if len(banks) >= 4:
            return banks[2 * (tt % 2)], banks[2 * (tt % 2) + 1]
        return banks[0], banks[1]

    def stage1(tt):
        LB0, LB1 = _banks(tt)
        sl = slice(tt * TT, (tt + 1) * TT)
        def mm_s1(eng, sl=sl, LB0=LB0):
            last = None
            for c in range(NCH):
                last = eng.matmul(t[LB0][0:1, 0:TT], t["ONESb"][:, 0:1],
                                  t[f"{xn}{c}"][:, sl],
                                  start=(c == 0), stop=(c == NCH - 1))
            return last
        p.op("pe", mm_s1,
             reads=tuple(f"{xn}{c}@{tt}" for c in range(NCH)) + ("ONESb",),
             writes=(LB0,))
        for half2 in range(2):
            for cc in range(4):
                c = 4 * half2 + cc
                p.op("dve", lambda v, c=c, cc=cc, sl=sl: v.tensor_tensor(
                    t[f"SQ{cc}"][:, 0:TT], t[f"{xn}{c}"][:, sl], t[f"{xn}{c}"][:, sl],
                    op=ALU.mult),
                    reads=(f"{xn}{c}@{tt}",), writes=(f"SQ{cc}",))
            def mm_s2(eng, half2=half2, LB1=LB1):
                last = None
                for cc in range(4):
                    last = eng.matmul(t[LB1][0:1, 0:TT], t["ONESb"][:, 0:1],
                                      t[f"SQ{cc}"][:, 0:TT],
                                      start=(half2 == 0 and cc == 0),
                                      stop=(half2 == 1 and cc == 3))
                return last
            p.op("pe", mm_s2, reads=tuple(f"SQ{cc}" for cc in range(4)) + ("ONESb",),
                 writes=(LB1,))

    def stage2(tt):
        LB0, LB1 = _banks(tt)
        sl = slice(tt * TT, (tt + 1) * TT)
        p.op("dve", lambda v, LB0=LB0: v.tensor_scalar(
            t["MU"][:], t[LB0][0:1, 0:TT], 1.0 / D, None, op0=ALU.mult),
            reads=(LB0,), writes=("MU",))
        p.op("dve", lambda v: v.tensor_tensor(
            t["AROW"][:], t["MU"][:], t["MU"][:], op=ALU.mult),
            reads=("MU",), writes=("AROW",))
        p.op("dve", lambda v, LB1=LB1: v.scalar_tensor_tensor(
            t["VAR"][:], t[LB1][0:1, 0:TT], 1.0 / D, t["AROW"][:],
            op0=ALU.mult, op1=ALU.subtract),
            reads=(LB1, "AROW"), writes=("VAR",))
        p.op("act", lambda s_: s_.activation(
            t["VAR"][:], t["VAR"][:], AF.Ln, bias=t["CONSTS"][0:1, 0:1]),
            reads=("VAR", "CONSTS"), writes=("VAR",))
        p.op("act", lambda s_: s_.activation(
            t["AROW"][:], t["VAR"][:], AF.Exp, scale=-0.5),
            reads=("VAR",), writes=("AROW",))
        if srow:
            p.op("act", lambda s_: s_.activation(
                t["SROW"][:], t["VAR"][:], AF.Exp, scale=0.5),
                reads=("VAR",), writes=("SROW",))
        p.op("dve", lambda v: v.scalar_tensor_tensor(
            t["BROW0"][:, 0:TT], t["MU"][:], -1.0, t["AROW"][:],
            op0=ALU.mult, op1=ALU.mult),
            reads=("MU", "AROW"), writes=("BROW0",))
        p.op("pe", lambda eng, LB0=LB0: eng.matmul(
            t[LB0][:, 0:TT], t["ONESf"][0:1, 0:128], t["AROW"][:],
            start=True, stop=True), reads=("AROW", "ONESf"), writes=(LB0,))
        p.op("pe", lambda eng, LB1=LB1: eng.matmul(
            t[LB1][:, 0:TT], t["ONESf"][0:1, 0:128], t["BROW0"][:, 0:TT],
            start=True, stop=True), reads=("BROW0", "ONESf"), writes=(LB1,))

    def stage2b(tt):
        LB0, LB1 = _banks(tt)
        sl = slice(tt * TT, (tt + 1) * TT)
        for c in range(NCH):
            u = t["U0"] if c % 2 == 0 else t["U1"]
            uk = "U0" if c % 2 == 0 else "U1"
            p.op("dve", lambda v, c=c, u=u, sl=sl, LB0=LB0: v.tensor_tensor(
                u[:, 0:TT], t[f"{xn}{c}"][:, sl], t[LB0][:, 0:TT], op=ALU.mult),
                reads=(f"{xn}{c}@{tt}", LB0), writes=(uk,))
            if out_f32 is not None:
                osl = slice(ctx.xs_off + sl.start, ctx.xs_off + sl.stop)
                p.op("dve", lambda v, c=c, u=u, osl=osl, LB1=LB1: v.tensor_tensor(
                    t[f"{out_f32}{c}"][:, osl], u[:, 0:TT], t[LB1][:, 0:TT], op=ALU.add),
                    reads=(uk, LB1), writes=(f"{out_f32}{c}",))
            if g_col is None:
                p.op("dve", lambda v, c=c, u=u, sl=sl, LB1=LB1: v.tensor_tensor(
                    t[f"{out}{c}"][:, sl], u[:, 0:TT], t[LB1][:, 0:TT], op=ALU.add),
                    reads=(uk, LB1), writes=(f"{out}{c}@{tt}",))
            else:
                p.op("dve", lambda v, c=c, u=u, sl=sl, LB1=LB1: v.tensor_tensor(
                    u[:, 0:TT], u[:, 0:TT], t[LB1][:, 0:TT], op=ALU.add),
                    reads=(uk, LB1), writes=(uk,))
                gc, bc, preg = g_col(c), b_col(c), getattr(ctx, "ln_param_regions", ())
                p.op("dve", lambda v, c=c, u=u, gc=gc, bc=bc, sl=sl: v.tensor_scalar(
                    t[f"{out}{c}"][:, sl], u[:, 0:TT], gc, bc, op0=ALU.mult, op1=ALU.add),
                    reads=(uk,) + tuple(preg), writes=(f"{out}{c}@{tt}",))
        if tile_hook is not None:
            tile_hook(tt)

    if len(banks) >= 4:
        for k2 in range(NTT + 2):
            if k2 >= 2:
                stage2b(k2 - 2)
            if 1 <= k2 <= NTT:
                stage2(k2 - 1)
            if k2 < NTT:
                stage1(k2)
    else:
        for tt in range(NTT):
            stage1(tt)
            stage2(tt)
            stage2b(tt)


def emit_ffn_l0(ctx, w1_hbm, b1_col, w2_hbm, b2h_col, streams, scale=0.5,
                bias_regions=(), tag=""):
    p, t = ctx.p, ctx.t
    NH1 = 4096 // 128
    HW = S // 2
    for half in range(2):
        hsl = slice(half * HW, (half + 1) * HW)
        for c4 in range(NH1):
            wk = f"W{c4 % 4}"
            wt = t[wk]
            ctx.wload(wk, w1_hbm[c4], 1024)
            for tt in range(2):
                sl = slice(half * HW + tt * TT, half * HW + (tt + 1) * TT)
                pb = f"P{(2 * c4 + tt) % 4}"
                def mm(eng, wt=wt, sl=sl, pb=pb):
                    last = None
                    for k in range(NCH):
                        last = eng.matmul(t[pb][:, 0:TT], wt[:, k * 128:(k + 1) * 128],
                                          t[f"XN{k}"][:, sl],
                                          start=(k == 0), stop=(k == NCH - 1))
                    return last
                p.op("pe", mm,
                     reads=tuple(f"XN{k}@{sl.start // TT}" for k in range(NCH)) + (wk,),
                     writes=(pb,))
                bc = b1_col(c4)
                p.op("act", lambda s_, c4=c4, tt=tt, pb=pb, bc=bc: s_.activation(
                    t[f"H{c4}"][:, tt * TT:(tt + 1) * TT], t[pb][:, 0:TT],
                    AF.Gelu, bias=bc),
                    reads=(pb,) + bias_regions, writes=(f"H{c4}",))
        for c in range(NCH):
            for hw2 in range(2):
                ctx.wload(f"W{4 + hw2}", w2_hbm[c][:, hw2 * 2048:(hw2 + 1) * 2048], 2048)
            for tt in range(2):
                sl = slice(half * HW + tt * TT, half * HW + (tt + 1) * TT)
                pb = f"P{(2 * c + tt) % 2 + 4}" if False else f"P{2 + tt % 2}"
                def mm(eng, tt=tt, pb=pb):
                    last = None
                    for k in range(NH1):
                        wt2 = t[f"W{4 + k // 16}"]
                        last = eng.matmul(t[pb][:, 0:TT],
                                          wt2[:, (k % 16) * 128:(k % 16 + 1) * 128],
                                          t[f"H{k}"][:, tt * TT:(tt + 1) * TT],
                                          start=(k == 0), stop=(k == NH1 - 1))
                    return last
                p.op("pe", mm, reads=tuple(f"H{k}" for k in range(NH1)) + ("W4", "W5"),
                     writes=(pb,))
                bc = b2h_col(c)
                u = t["U0"] if tt % 2 == 0 else t["U1"]
                uk = "U0" if tt % 2 == 0 else "U1"
                p.op("act", lambda s_, pb=pb, u=u, bc=bc: s_.activation(
                    u[:, 0:TT], t[pb][:, 0:TT], AF.Identity, bias=bc, scale=scale),
                    reads=(pb,) + bias_regions, writes=(uk,))
                xsl = slice(ctx.xs_off + sl.start, ctx.xs_off + sl.stop)
                p.op("pool", lambda g, c=c, u=u, xsl=xsl: g.tensor_tensor(
                    t[f"XS{c}"][:, xsl], t[f"XS{c}"][:, xsl], u[:, 0:TT], op=ALU.add),
                    reads=(f"XS{c}", uk), writes=(f"XS{c}",))




def emit_qr_onehot(ctx, tile, streams):
    p, t = ctx.p, ctx.t
    xo = ctx.xs_off
    for sub in range(4):
        ts0 = tile * TT + sub * 128
        for side, pcol in (("Q", 0), ("K", 128)):
            def mm(eng, ts0=ts0, side=side, pcol=pcol, sub=sub):
                last = None
                for c in range(NCH):
                    last = eng.matmul(
                        t["PS1"][:, pcol:pcol + 128],
                        t[f"XS{c}"][:, xo + ts0:xo + ts0 + 128],
                        t[f"WR{side}"][:, c * 128:(c + 1) * 128],
                        start=(c == 0), stop=False)
                eng.matmul(t["PS1"][:, pcol:pcol + 128],
                           t["MU"][0:1, sub * 128:sub * 128 + 128],
                           t[f"NCS{side}"][0:1, 0:128], start=False, stop=False)
                last = eng.matmul(t["PS1"][:, pcol:pcol + 128],
                                  t["SROW"][0:1, sub * 128:sub * 128 + 128],
                                  t[f"B{side}R"][0:1, 0:128], start=False, stop=True)
                return last
            p.op("pe", mm,
                 reads=tuple(f"XS{c}" for c in range(NCH)) +
                       (f"WR{side}", "MU", "SROW", f"NCS{side}", f"B{side}R"),
                 writes=("PS1",))
            p.op("dve", lambda v, pcol=pcol: v.tensor_reduce(
                t["M16"][:, 0:16],
                bass.AP(t["PS1"], pcol, [[512, 128], [8, 16], [1, 8]]),
                axis=AX.X, op=ALU.max),
                reads=("PS1",), writes=("M16",))
            p.op("dve", lambda v, pcol=pcol: v.tensor_tensor(
                t["OHSB"][:, 0:128],
                bass.AP(t["PS1"], pcol, [[512, 128], [8, 16], [1, 8]]),
                bass.AP(t["M16"], 0, [[16, 128], [1, 16], [0, 8]]),
                op=ALU.is_ge),
                reads=("PS1", "M16"), writes=("OHSB",))
            p.op("dve", lambda v: v.tensor_scalar(
                t["OHSB"][:, 0:128], t["OHSB"][:, 0:128], OH_SCALE, None, op0=ALU.mult),
                reads=("OHSB",), writes=("OHSB",))
            p.op("sp", lambda g, ts0=ts0, side=side: g.dma_start(
                out=t[f"TRANS{side}"][:, ts0:ts0 + 128], in_=t["OHSB"][:, 0:128],
                transpose=True),
                reads=("OHSB",), writes=(f"TRANS{side}",), stream=streams["tr"])


def emit_full_attn(ctx, streams, *, bucket, causal, wq_hbm, wk_hbm, wv_hbm,
                   wo_hbm, bq8_col=None, bk_col=None, bv_col=None, bo_col=None,
                   param_regions=(), tag=""):
    p, t = ctx.p, ctx.t
    KAUG = 72 if bucket else 64
    eshift = 4 if bucket else 5
    nsp = 0
    for ch in range(NCH):
        for wi, (w_hbm, dst, key) in enumerate(
                ((wq_hbm, "QP", "q"), (wk_hbm, "KP", "k"), (wv_hbm, "VP", "v"))):
            wk_ = f"W{2 * wi + ch % 2}"
            wt = t[wk_]
            ctx.wload(wk_, w_hbm[ch], 1024)
            for tt in range(NTT):
                sl = slice(tt * TT, (tt + 1) * TT)
                pb = f"PS{(wi * NTT + tt) % 3}"
                def mm(eng, wt=wt, sl=sl, pb=pb):
                    last = None
                    for k in range(NCH):
                        last = eng.matmul(t[pb][:, 0:TT], wt[:, k * 128:(k + 1) * 128],
                                          t[f"XN{k}"][:, sl],
                                          start=(k == 0), stop=(k == NCH - 1))
                    return last
                p.op("pe", mm,
                     reads=tuple(f"XN{k}@{sl.start // TT}" for k in range(NCH)) + (wk_,),
                     writes=(pb,))
                if key == "q":
                    bc = bq8_col(ch) if bq8_col is not None else None
                    if bc is not None:
                        p.op("act", lambda s_, pb=pb, sl=sl, bc=bc: s_.activation(
                            t["QP"][:, sl], t[pb][:, 0:TT], AF.Identity,
                            bias=bc, scale=0.125),
                            reads=(pb,) + param_regions, writes=("QP",))
                    else:
                        p.op("act", lambda s_, pb=pb, sl=sl: s_.activation(
                            t["QP"][:, sl], t[pb][:, 0:TT], AF.Identity,
                            bias=0.0, scale=0.125),
                            reads=(pb,), writes=("QP",))
                else:
                    dst_ = "KP" if key == "k" else "VP"
                    bc = bk_col(ch) if (key == "k" and bk_col) else (
                        bv_col(ch) if (key == "v" and bv_col) else None)
                    if bc is not None:
                        p.op("act", lambda s_, pb=pb, sl=sl, bc=bc, dst_=dst_:
                             s_.activation(t[dst_][:, sl], t[pb][:, 0:TT],
                                           AF.Identity, bias=bc),
                             reads=(pb,) + param_regions, writes=(dst_,))
                    else:
                        p.op("act", lambda s_, pb=pb, sl=sl, dst_=dst_: s_.copy(
                            t[dst_][:, sl], t[pb][:, 0:TT]),
                            reads=(pb,), writes=(dst_,))
        for hh in range(2):
            h = 2 * ch + hh
            b = 64 * hh
            vt = "VTM0"
            for j0 in range(4):
                def trs(eng, b=b, j0=j0):
                    last = None
                    for jj in range(4):
                        j = 4 * j0 + jj
                        last = eng.transpose(t["PTB"][:, jj * 64:(jj + 1) * 64],
                                             t["VP"][b:b + 64, j * 128:(j + 1) * 128],
                                             t["IDENT2b"][b:b + 64, 0:64])
                    return last
                p.op("pe", trs, reads=("VP", "IDENT2b"), writes=("PTB",))
                p.op("dve", lambda v, vt=vt, j0=j0: v.tensor_copy(
                    bass.AP(t[vt], 4 * j0 * 65, [[1040, 128], [65, 4], [1, 64]]),
                    bass.AP(t["PTB"], 0, [[1024, 128], [64, 4], [1, 64]])),
                    reads=("PTB",), writes=(vt,))
            if bucket:
                p.op("sp", lambda g, b=b: g.dma_start(
                    out=t["AUGQ"][0:64, :], in_=t["QP"][b:b + 64, :]),
                    reads=("QP",), writes=("AUGQ",), stream=streams["tr"])
                p.op("sp", lambda g, b=b: g.dma_start(
                    out=t["AUGK"][0:64, :], in_=t["KP"][b:b + 64, :]),
                    reads=("KP",), writes=("AUGK",), stream=streams["tr"])
                p.op("sp", lambda g, h=h: g.dma_start(
                    out=t["AUGQ"][64:72, :], in_=t["TRANSQ"][8 * h:8 * h + 8, :]),
                    reads=("TRANSQ",), writes=("AUGQ",), stream=streams["tr"])
                p.op("sp", lambda g, h=h: g.dma_start(
                    out=t["AUGK"][64:72, :], in_=t["TRANSK"][8 * h:8 * h + 8, :]),
                    reads=("TRANSK",), writes=("AUGK",), stream=streams["tr"])
            iters = [(j, qb) for j in range(16) for qb in range(4)
                     if not (causal and j > 4 * qb + 3)]

            def emit_s(idx):
                j, qb = iters[idx]
                qsl = slice(qb * TT, (qb + 1) * TT)
                pb = f"PS{idx % 3}"
                if bucket:
                    p.op("pe", lambda eng, j=j, qsl=qsl, pb=pb: eng.matmul(
                        t[pb][:, 0:TT], t["AUGK"][0:KAUG, j * 128:(j + 1) * 128],
                        t["AUGQ"][0:KAUG, qsl], start=True, stop=True),
                        reads=("AUGK", "AUGQ"), writes=(pb,))
                else:
                    p.op("pe", lambda eng, j=j, qsl=qsl, pb=pb, b=b: eng.matmul(
                        t[pb][:, 0:TT], t["KP"][b:b + 64, j * 128:(j + 1) * 128],
                        t["QP"][b:b + 64, qsl], start=True, stop=True),
                        reads=("KP", "QP"), writes=(pb,))

            def emit_rest(idx, nsp):
                j, qb = iters[idx]
                pb = f"PS{idx % 3}"
                sp = f"SQ{nsp % 3}"
                p.op("act", lambda s_, pb=pb, sp=sp: s_.activation(
                    t[sp][:, 0:TT], t[pb][:, 0:TT], AF.Exp,
                    bias=t["CONSTS"][:, eshift:eshift + 1]),
                    reads=(pb, "CONSTS"), writes=(sp,))
                if causal and j >= 4 * qb:
                    c0 = j * 128 - qb * TT
                    if c0 > 0:
                        p.op("pool", lambda g, sp=sp, c0=c0: g.memset(
                            t[sp][:, 0:c0], 0.0), reads=(), writes=(sp,))
                    p.op("pool", lambda g, sp=sp, c0=c0: g.tensor_tensor(
                        t[sp][:, c0:c0 + 128], t[sp][:, c0:c0 + 128],
                        t["TRIUb"][:, 0:128], op=ALU.mult),
                        reads=("TRIUb", sp), writes=(sp,))
                jlast = (4 * qb + 3) if causal else 15
                p.op("pe", lambda eng, vt=vt, j=j, qb=qb, sp=sp,
                     jlast=jlast: eng.matmul(
                         t["PZ"][:, qb * TT:(qb + 1) * TT],
                         t[vt][:, j * 65:j * 65 + 65], t[sp][:, 0:TT],
                         start=(j == 0), stop=(j == jlast)),
                     reads=(vt, sp), writes=("PZ",))

            for idx in range(len(iters) + 2):
                if idx < len(iters):
                    emit_s(idx)
                if idx >= 2:
                    emit_rest(idx - 2, nsp)
                    nsp += 1
            for qb in range(4):
                qsl = slice(qb * TT, (qb + 1) * TT)
                br = f"BROW{qb % 2}"
                pbq = f"PS{qb % 2}"
                p.op("act", lambda s_, qsl=qsl, br=br: s_.activation(
                    t[br][:, 0:TT], t["PZ"][64:65, qsl], AF.Ln),
                    reads=("PZ",), writes=(br,))
                p.op("act", lambda s_, br=br: s_.activation(
                    t[br][:, 0:TT], t[br][:, 0:TT], AF.Exp, scale=-1.0),
                    reads=(br,), writes=(br,))
                p.op("pe", lambda eng, br=br, pbq=pbq: eng.matmul(
                    t[pbq][0:64, 0:TT], t["ONESf"][0:1, 0:64],
                    t[br][0:1, 0:TT], start=True, stop=True),
                    reads=(br, "ONESf"), writes=(pbq,))
                sp = f"SQ{nsp % 3}"
                nsp += 1
                p.op("dve", lambda v, sp=sp, pbq=pbq: v.tensor_copy(
                    t[sp][0:64, 0:TT], t[pbq][0:64, 0:TT]),
                    reads=(pbq,), writes=(sp,))
                p.op("dve", lambda v, ch=ch, b=b, qsl=qsl, sp=sp:
                     v.tensor_tensor(
                         t[f"ZB{ch}"][b:b + 64, qsl], t["PZ"][0:64, qsl],
                         t[sp][0:64, 0:TT], op=ALU.mult),
                     reads=("PZ", sp), writes=(f"ZB{ch}",))
    for c in range(NCH):
        wk_ = f"W{c % 2}"
        wt = t[wk_]
        ctx.wload(wk_, wo_hbm[c], 1024)
        for tt in range(NTT):
            sl = slice(tt * TT, (tt + 1) * TT)
            pb = f"PS{tt % 2}"
            def mm(eng, wt=wt, sl=sl, pb=pb):
                last = None
                for k in range(NCH):
                    last = eng.matmul(t[pb][:, 0:TT], wt[:, k * 128:(k + 1) * 128],
                                      t[f"ZB{k}"][:, sl],
                                      start=(k == 0), stop=(k == NCH - 1))
                return last
            p.op("pe", mm, reads=tuple(f"ZB{k}" for k in range(NCH)) + (wk_,),
                 writes=(pb,))
            xsl = slice(ctx.xs_off + sl.start, ctx.xs_off + sl.stop)
            if bo_col is not None:
                bc = bo_col(c)
                p.op("dve", lambda v, pb=pb, bc=bc, c=c, xsl=xsl:
                     v.scalar_tensor_tensor(
                         t[f"XS{c}"][:, xsl], t[pb][:, 0:TT], bc,
                         t[f"XS{c}"][:, xsl], op0=ALU.add, op1=ALU.add),
                     reads=(pb, f"XS{c}") + param_regions, writes=(f"XS{c}",))
            else:
                p.op("dve", lambda v, pb=pb, c=c, xsl=xsl: v.tensor_tensor(
                    t[f"XS{c}"][:, xsl], t[f"XS{c}"][:, xsl], t[pb][:, 0:TT],
                    op=ALU.add),
                    reads=(f"XS{c}", pb), writes=(f"XS{c}",))


def build_l0ffn(nseq=2):
    nc = bass.Bass("TRN2", target_bir_lowering=False)
    x_hbm = nc.dram_tensor("x", [nseq, D, S], F32, kind="ExternalInput").ap()
    y_hbm = nc.dram_tensor("y", [nseq, D, S], F32, kind="ExternalOutput").ap()
    w1 = nc.dram_tensor("w1", [32, 128, D], F32, kind="ExternalInput").ap()
    b1T = nc.dram_tensor("b1T", [128, 4096 // 128], F32, kind="ExternalInput").ap()
    w2 = nc.dram_tensor("w2", [NCH, 128, 4096], F32, kind="ExternalInput").ap()
    b2T = nc.dram_tensor("b2T", [128, NCH], F32, kind="ExternalInput").ap()

    es = ExitStack()
    with es:
        prog = Prog(nc, es)
        ctx = Ctx(nc, es, prog)
        streams = {"xs": prog.stream("d_xs", "sp"),
                   "setup": prog.stream("d_setup", "sp"),
                   "wg": prog.stream("d_wg", "pool")}
        for i in range(6):
            prog.stream(f"d_W{i}", "sp")
            prog.stream(f"dp_W{i}", "pool")
        make_std(ctx, n_w_tiles=0, w_cols=0)
        for i in range(4):
            ctx.sb(f"W{i}", [128, 1024], BF16)
        for i in range(4, 6):
            ctx.sb(f"W{i}", [128, 2048], BF16)
        for i in range(32):
            ctx.sb(f"H{i}", [128, 1024], BF16)
        bt = ctx.sb("B1T", [128, 4096 // 128], F32)
        prog.op("sp", lambda g: g.dma_start(out=bt[:], in_=b1T),
                reads=("hbm:b1",), writes=("B1T",), stream=streams["setup"])
        b2t = ctx.sb("B2T", [128, NCH], F32)
        prog.op("sp", lambda g: g.dma_start(out=b2t[:], in_=b2T),
                reads=("hbm:b2",), writes=("B2T",), stream=streams["setup"])
        b2h = ctx.sb("B2H", [128, NCH], F32)
        load_constants(ctx, streams)
        prog.freeze_stream("d_setup")
        prog.op("dve", lambda v: v.tensor_scalar(
            b2h[:], b2t[:], 0.5, None, op0=ALU.mult),
            reads=("B2T",), writes=("B2H",))

        for seq in range(nseq):
            emit_load_x(ctx, x_hbm, seq, streams)
            emit_cast_x(ctx)
            emit_ln(ctx)
            emit_ffn_l0(ctx, w1, lambda c4: bt[:, c4:c4 + 1], w2,
                        lambda c: b2h[:, c:c + 1], streams,
                        bias_regions=("B1T", "B2H"))
            emit_store_x(ctx, y_hbm, seq, streams)
        prog.wait_region("sp", tuple(f"hbm:x{s_}" for s_ in range(nseq)))
        prog.finalize()
    return nc




def build_l0attn(nseq=2):
    nc = bass.Bass("TRN2", target_bir_lowering=False)
    x_hbm = nc.dram_tensor("x", [nseq, D, S], F32, kind="ExternalInput").ap()
    y_hbm = nc.dram_tensor("y", [nseq, D, S], F32, kind="ExternalOutput").ap()
    wq = nc.dram_tensor("wq", [NCH, 128, D], F32, kind="ExternalInput").ap()
    wk = nc.dram_tensor("wk", [NCH, 128, D], F32, kind="ExternalInput").ap()
    wv = nc.dram_tensor("wv", [NCH, 128, D], F32, kind="ExternalInput").ap()
    wo = nc.dram_tensor("wo", [NCH, 128, D], F32, kind="ExternalInput").ap()
    wqT = nc.dram_tensor("wqT", [NCH, 128, D], F32, kind="ExternalInput").ap()
    wkT = nc.dram_tensor("wkT", [NCH, 128, D], F32, kind="ExternalInput").ap()
    rexp = nc.dram_tensor("rexp", [128, 1024], F32, kind="ExternalInput").ap()
    bqT = nc.dram_tensor("bqT", [128, NCH], F32, kind="ExternalInput").ap()
    bkT = nc.dram_tensor("bkT", [128, NCH], F32, kind="ExternalInput").ap()
    bvT = nc.dram_tensor("bvT", [128, NCH], F32, kind="ExternalInput").ap()
    boT = nc.dram_tensor("boT", [128, NCH], F32, kind="ExternalInput").ap()

    es = ExitStack()
    with es:
        prog = Prog(nc, es)
        ctx = Ctx(nc, es, prog)
        streams = {"xs": prog.stream("d_xs", "sp"),
                   "setup": prog.stream("d_setup", "sp"),
                   "tr": prog.stream("d_tr", "sp"),
                   "wg": prog.stream("d_wg", "pool")}
        for i in range(6):
            prog.stream(f"d_W{i}", "sp")
            prog.stream(f"dp_W{i}", "pool")
        make_std(ctx, n_w_tiles=6, w_cols=1024, psum=False)
        sb, ps = ctx.sb, ctx.ps
        sb("SROW", [1, TT], F32)
        sb("WRQ", [128, 1024], F32)
        sb("WRK", [128, 1024], F32)
        sb("NCSQ", [1, 128], F32)
        sb("NCSK", [1, 128], F32)
        sb("BQR", [1, 128], F32)
        sb("BKR", [1, 128], F32)
        sb("M16", [128, 16], F32)
        sb("OHSB", [128, 128], BF16)
        sb("TRANSQ", [128, S], BF16)
        sb("TRANSK", [128, S], BF16)
        sb("QP", [128, S], BF16)
        sb("KP", [128, S], BF16)
        sb("VP", [128, S], BF16)
        sb("AUGQ", [72, S], BF16)
        sb("AUGK", [72, S], BF16)
        sb("VTM0", [128, 16 * 65], BF16)
        sb("WTMP", [128, 1024], F32)
        for ch in range(NCH):
            sb(f"ZB{ch}", [128, S], BF16)
        for nm in ("BQ8", "BKC", "BVC", "BOC"):
            sb(nm, [128, NCH], F32)
        sb("REXP", [128, 1024], F32)
        ps("PS0", [128, 512], F32)
        ps("PS1", [128, 512], F32)
        ps("PS2", [128, 512], F32)
        ps("PZ", [65, 2048], F32)
        ps("PTB", [128, 1024], BF16)

        t = ctx.t
        prog.op("sp", lambda g: g.dma_start(out=t["REXP"][:], in_=rexp),
                reads=("hbm:rexp",), writes=("REXP",), stream=streams["setup"])
        for nm, src in (("BQ8", bqT), ("BKC", bkT), ("BVC", bvT), ("BOC", boT)):
            prog.op("sp", lambda g, nm=nm, src=src: g.dma_start(out=t[nm][:], in_=src),
                    reads=(f"hbm:{nm}",), writes=(nm,), stream=streams["setup"])
        load_constants(ctx, streams)
        prog.freeze_stream("d_setup")
        prog.op("dve", lambda v: v.tensor_scalar(
            t["BQ8"][:], t["BQ8"][:], 0.125, None, op0=ALU.mult),
            reads=("BQ8",), writes=("BQ8",))
        ctx.ln_banks = ("PS0", "PS1")
        emit_wr_setup_inline(ctx, wqT, wkT, streams)

        for seq in range(nseq):
            emit_load_x(ctx, x_hbm, seq, streams)
            emit_cast_x(ctx)
            prog.op("dve", lambda v: v.memset(
                bass.AP(ctx.t["VTM0"], 64, [[1040, 128], [65, 16], [1, 1]]), 1.0),
                writes=("VTM0",))
            emit_ln(ctx, srow=True,
                    tile_hook=lambda tt: emit_qr_onehot(ctx, tt, streams))
            emit_full_attn(ctx, streams, bucket=True, causal=False,
                           wq_hbm=wq, wk_hbm=wk, wv_hbm=wv, wo_hbm=wo,
                           bq8_col=lambda c: t["BQ8"][:, c:c + 1],
                           bk_col=lambda c: t["BKC"][:, c:c + 1],
                           bv_col=lambda c: t["BVC"][:, c:c + 1],
                           bo_col=lambda c: t["BOC"][:, c:c + 1],
                           param_regions=("BQ8", "BKC", "BVC", "BOC"), tag="l0a")
            emit_store_x(ctx, y_hbm, seq, streams)
        prog.wait_region("sp", tuple(f"hbm:x{s_}" for s_ in range(nseq)))
        prog.finalize()
    return nc


def emit_wr_setup_inline(ctx, wqT, wkT, streams):
    p, t = ctx.p, ctx.t
    for side, wT, bTk in (("Q", wqT, "BQ8"), ("K", wkT, "BKC")):
        wr = t[f"WR{side}"]
        for m in range(NCH):
            p.op("sp", lambda g, src=wT[m]: g.dma_start(
                out=t["WTMP"][:, 0:1024], in_=src),
                reads=(f"hbm:w{side}T",), writes=("WTMP",), stream=streams["tr"])
            def mm(eng):
                last = None
                for k in range(NCH):
                    last = eng.matmul(t["PS0"][:, 0:128],
                                      t["WTMP"][:, k * 128:(k + 1) * 128],
                                      t["REXP"][:, k * 128:(k + 1) * 128],
                                      start=(k == 0), stop=(k == NCH - 1))
                return last
            p.op("pe", mm, reads=("WTMP", "REXP"), writes=("PS0",))
            p.op("dve", lambda v, m=m, wr=wr, side=side: v.tensor_copy(
                wr[:, m * 128:(m + 1) * 128], t["PS0"][:, 0:128]),
                reads=("PS0",), writes=(f"WR{side}",))
        def mmc(eng, wr=wr):
            last = None
            for k in range(NCH):
                last = eng.matmul(t["PS1"][0:1, 0:128], t["ONESf"][:, 0:1],
                                  wr[:, k * 128:(k + 1) * 128],
                                  start=(k == 0), stop=(k == NCH - 1))
            return last
        p.op("pe", mmc, reads=(f"WR{side}", "ONESf"), writes=("PS1",))
        p.op("dve", lambda v, side=side: v.tensor_scalar(
            t[f"NCS{side}"][:, 0:128], t["PS1"][0:1, 0:128], -1.0, None,
            op0=ALU.mult),
            reads=("PS1",), writes=(f"NCS{side}",))
        bf = 8.0 if side == "Q" else 1.0
        def mmb(eng, bTk=bTk, bf=bf):
            last = None
            for k in range(NCH):
                last = eng.matmul(t["PS1"][0:1, 128:256], t[bTk][:, k:k + 1],
                                  t["REXP"][:, k * 128:(k + 1) * 128],
                                  start=(k == 0), stop=(k == NCH - 1))
            return last
        p.op("pe", mmb, reads=(bTk, "REXP"), writes=("PS1",))
        p.op("dve", lambda v, side=side, bf=bf: v.tensor_scalar(
            t[f"B{side}R"][:, 0:128], t["PS1"][0:1, 128:256], bf, None,
            op0=ALU.mult),
            reads=("PS1",), writes=(f"B{side}R",))


def build_l1attn(nseq=2):
    nc = bass.Bass("TRN2", target_bir_lowering=False)
    x_hbm = nc.dram_tensor("x", [nseq, D, S], F32, kind="ExternalInput").ap()
    y_hbm = nc.dram_tensor("y", [nseq, D, S], F32, kind="ExternalOutput").ap()
    wq = nc.dram_tensor("wq", [NCH, 128, D], F32, kind="ExternalInput").ap()
    wk = nc.dram_tensor("wk", [NCH, 128, D], F32, kind="ExternalInput").ap()
    wv = nc.dram_tensor("wv", [NCH, 128, D], F32, kind="ExternalInput").ap()
    wo = nc.dram_tensor("wo", [NCH, 128, D], F32, kind="ExternalInput").ap()
    triu = nc.dram_tensor("CONST_TRIU", [128, 128], F32, kind="ExternalInput").ap()

    es = ExitStack()
    with es:
        prog = Prog(nc, es)
        ctx = Ctx(nc, es, prog)
        streams = {"xs": prog.stream("d_xs", "sp"),
                   "setup": prog.stream("d_setup", "sp"),
                   "tr": prog.stream("d_tr", "sp"),
                   "wg": prog.stream("d_wg", "pool")}
        for i in range(6):
            prog.stream(f"d_W{i}", "sp")
            prog.stream(f"dp_W{i}", "pool")
        make_std(ctx, n_w_tiles=6, w_cols=1024, psum=False)
        sb, ps = ctx.sb, ctx.ps
        sb("QP", [128, S], BF16)
        sb("KP", [128, S], BF16)
        sb("VP", [128, S], BF16)
        sb("AUGQ", [72, S], BF16)
        sb("AUGK", [72, S], BF16)
        sb("VTM0", [128, 16 * 65], BF16)
        sb("TRIUb", [128, 128], BF16)
        for ch in range(NCH):
            sb(f"ZB{ch}", [128, S], BF16)
        ps("PS0", [128, 512], F32)
        ps("PS1", [128, 512], F32)
        ps("PS2", [128, 512], F32)
        ps("PZ", [65, 2048], F32)
        ps("PTB", [128, 1024], BF16)
        ctx.ln_banks = ("PS0", "PS1")

        t = ctx.t
        prog.op("pool", lambda g: g.dma_start(out=t["TRIUb"][:], in_=triu),
                reads=("c:TRIU",), writes=("TRIUb",), stream=streams["wg"])
        load_constants(ctx, streams)
        prog.freeze_stream("d_setup")

        for seq in range(nseq):
            emit_load_x(ctx, x_hbm, seq, streams)
            emit_cast_x(ctx)
            prog.op("dve", lambda v: v.memset(
                bass.AP(ctx.t["VTM0"], 64, [[1040, 128], [65, 16], [1, 1]]), 1.0),
                writes=("VTM0",))
            emit_ln(ctx)
            emit_full_attn(ctx, streams, bucket=False, causal=True,
                           wq_hbm=wq, wk_hbm=wk, wv_hbm=wv, wo_hbm=wo, tag="l1a")
            emit_store_x(ctx, y_hbm, seq, streams)
        prog.wait_region("sp", tuple(f"hbm:x{s_}" for s_ in range(nseq)))
        prog.finalize()
    return nc


def emit_cast_shift(ctx, off=1):
    p, t = ctx.p, ctx.t
    for c in range(NCH):
        so = off if c < 4 else off - 1
        eng = "pool" if c % 2 == 0 else "dve"
        p.op(eng, lambda v, c=c, so=so: v.tensor_copy(
            t[f"XN{c}"][:, 0:S], t[f"XS{c}"][:, so:so + S]),
            reads=(f"XS{c}",),
            writes=tuple(f"XN{c}@{i}" for i in range(NTT)))


def emit_ffn_gated(ctx, w1_hbm, b1_col, w2_hbm, b2_col, streams, lng_col,
                   lnb_col, param_regions=(), tag=""):
    p, t = ctx.p, ctx.t
    NA = 22
    HW = S // 2
    for half in range(2):
        for c4 in range(NA):
            for part, oc in (("g", 22 + c4), ("a", c4)):
                wk_ = f"W{(2 * c4 + (0 if part == "g" else 1)) % 4}"
                wt = t[wk_]
                ctx.wload(wk_, w1_hbm[oc], 1024)
                for tt in range(2):
                    sl = slice(half * HW + tt * TT, half * HW + (tt + 1) * TT)
                    pb = f"P{tt % 2}" if part == "g" else f"P{2 + tt % 2}"
                    def mm(eng, wt=wt, sl=sl, pb=pb):
                        last = None
                        for k in range(NCH):
                            last = eng.matmul(t[pb][:, 0:TT],
                                              wt[:, k * 128:(k + 1) * 128],
                                              t[f"XN{k}"][:, sl],
                                              start=(k == 0), stop=(k == NCH - 1))
                        return last
                    p.op("pe", mm,
                         reads=tuple(f"XN{k}@{sl.start // TT}" for k in range(NCH)) + (wk_,),
                         writes=(pb,))
                    bc = b1_col(oc)
                    if part == "g":
                        sq = f"SQ{tt % 2}"
                        p.op("act", lambda s_, pb=pb, bc=bc, sq=sq: s_.activation(
                            t[sq][:, 0:TT], t[pb][:, 0:TT], AF.Gelu, bias=bc),
                            reads=(pb,) + param_regions, writes=(sq,))
                    else:
                        sq = f"SQ{tt % 2}"
                        p.op("dve", lambda v, pb=pb, bc=bc, sq=sq, c4=c4, tt=tt:
                             v.scalar_tensor_tensor(
                                 t[f"H{c4}"][:, tt * TT:(tt + 1) * TT],
                                 t[pb][:, 0:TT], bc, t[sq][:, 0:TT],
                                 op0=ALU.add, op1=ALU.mult),
                             reads=(pb, sq) + param_regions, writes=(f"H{c4}",))
        for c in range(NCH):
            wk_ = f"W{4 + c % 2}"
            wt = t[wk_]
            ctx.wload(wk_, w2_hbm[c], 2816)
            for tt in range(2):
                sl = slice(half * HW + tt * TT, half * HW + (tt + 1) * TT)
                xsl = slice(ctx.xs_off + half * HW + tt * TT,
                            ctx.xs_off + half * HW + (tt + 1) * TT)
                pb = f"P{2 + tt % 2}"
                def mm(eng, wt=wt, tt=tt, pb=pb):
                    last = None
                    for k in range(NA):
                        last = eng.matmul(t[pb][:, 0:TT], wt[:, k * 128:(k + 1) * 128],
                                          t[f"H{k}"][:, tt * TT:(tt + 1) * TT],
                                          start=(k == 0), stop=(k == NA - 1))
                    return last
                p.op("pe", mm, reads=tuple(f"H{k}" for k in range(NA)) + (wk_,),
                     writes=(pb,))
                bc = b2_col(c)
                p.op("dve", lambda v, pb=pb, bc=bc, c=c, xsl=xsl:
                     v.scalar_tensor_tensor(
                         t[f"XS{c}"][:, xsl], t[pb][:, 0:TT], bc,
                         t[f"XS{c}"][:, xsl], op0=ALU.add, op1=ALU.add),
                     reads=(pb, f"XS{c}") + param_regions, writes=(f"XS{c}",))


def build_l1ffn_fsub(nseq=2):
    nc = bass.Bass("TRN2", target_bir_lowering=False)
    x_hbm = nc.dram_tensor("x", [nseq, D, S], F32, kind="ExternalInput").ap()
    y_hbm = nc.dram_tensor("y", [nseq, D, S], F32, kind="ExternalOutput").ap()
    w1 = nc.dram_tensor("w1", [44, 128, D], F32, kind="ExternalInput").ap()
    w2 = nc.dram_tensor("w2", [NCH, 128, 2816], F32, kind="ExternalInput").ap()
    b1T = nc.dram_tensor("b1T", [128, 44], F32, kind="ExternalInput").ap()
    b2T = nc.dram_tensor("b2T", [128, NCH], F32, kind="ExternalInput").ap()
    lngT = nc.dram_tensor("lngT", [128, NCH], F32, kind="ExternalInput").ap()
    lnbT = nc.dram_tensor("lnbT", [128, NCH], F32, kind="ExternalInput").ap()

    es = ExitStack()
    with es:
        prog = Prog(nc, es)
        ctx = Ctx(nc, es, prog)
        streams = {"xs": prog.stream("d_xs", "sp"),
                   "setup": prog.stream("d_setup", "sp"),
                   "wg": prog.stream("d_wg", "pool")}
        for i in range(6):
            prog.stream(f"d_W{i}", "sp")
            prog.stream(f"dp_W{i}", "pool")
        make_std(ctx, n_w_tiles=0, w_cols=0, psum=True, xs_cols=S + 1)
        ctx.xs_off = 1
        for i in range(4):
            ctx.sb(f"W{i}", [128, 1024], BF16)
        for i in range(4, 6):
            ctx.sb(f"W{i}", [128, 2816], BF16)
        for i in range(22):
            ctx.sb(f"H{i}", [128, 1024], BF16)
        t = ctx.t
        for nm, src_ in (("B1C", b1T), ("B2C", b2T), ("LNG", lngT), ("LNB", lnbT)):
            dd = ctx.sb(nm, [128, src_.shape[-1]], F32)
            prog.op("sp", lambda g, dd=dd, src_=src_: g.dma_start(out=dd[:], in_=src_),
                    reads=(f"hbm:{nm}",), writes=(nm,), stream=streams["setup"])
        load_constants(ctx, streams)
        prog.freeze_stream("d_setup")

        for seq in range(nseq):
            emit_load_x(ctx, x_hbm, seq, streams)
            for c in range(NCH):
                prog.op("dve", lambda v, c=c: v.memset(t[f"XS{c}"][:, 0:1], 0.0),
                        writes=(f"XS{c}",))
            emit_cast_shift(ctx)
            ctx.ln_param_regions = ("LNG", "LNB")
            emit_ln(ctx, g_col=lambda c: t["LNG"][:, c:c + 1],
                    b_col=lambda c: t["LNB"][:, c:c + 1])
            emit_ffn_gated(ctx, w1, lambda oc: t["B1C"][:, oc:oc + 1], w2,
                           lambda c: t["B2C"][:, c:c + 1], streams,
                           None, None, param_regions=("B1C", "B2C"))
            emit_store_x(ctx, y_hbm, seq, streams)
        prog.wait_region("sp", tuple(f"hbm:x{s_}" for s_ in range(nseq)))
        prog.finalize()
    return nc


def emit_tables(ctx, streams, tb_aps, qs_col, qsp_col, ks_col, ksp_col,
                param_regions=()):
    p, t = ctx.p, ctx.t
    for dst, base, col, sc in (("TQ1", "TBQC", qs_col, 8.0),
                               ("TQ2", "TBQS", qsp_col, 8.0),
                               ("TK1", "TBKC", ks_col, 1.0),
                               ("TK2", "TBKS", ksp_col, 1.0)):
        ap = tb_aps[base]
        if ap.dtype == BF16:
            p.op("sp", lambda g, dst=dst, ap=ap: g.dma_start(
                out=t[dst][:], in_=ap),
                reads=(f"c:{base}",), writes=(dst,), stream=streams["tr"])
        else:
            p.op("pool", lambda g, dst=dst, ap=ap: g.dma_start(
                out=t[dst][:], in_=ap),
                reads=(f"c:{base}",), writes=(dst,), stream=streams["wg"])
        p.op("dve", lambda v, dst=dst, col=col, sc=sc: v.tensor_scalar(
            t[dst][:], t[dst][:], col, sc, op0=ALU.mult, op1=ALU.mult),
            reads=(dst,) + param_regions, writes=(dst,))


def emit_attn_sublayer(ctx, streams, wqkv_hbm, wo_hbm, qs_col, qsp_col, ks_col,
                       ksp_col, param_regions=(), tag=""):
    p, t = ctx.p, ctx.t
    W = 256
    for ch in range(NCH):
        for wi, dst in ((0, "QP"), (1, "KP"), (2, "VP")):
            wk_ = f"W{2 * wi + ch % 2}"
            ctx.wload(wk_, wqkv_hbm[8 * wi + ch], 1024)

        def proj(tt, wi, dst):
            sl = slice(tt * TT, (tt + 1) * TT)
            wk_ = f"W{2 * wi + ch % 2}"
            wt = t[wk_]
            pb = f"PS{(wi + 3 * tt) % 2}"
            def mm(eng, wt=wt, sl=sl, pb=pb):
                last = None
                for k in range(NCH):
                    last = eng.matmul(t[pb][:, 0:TT], wt[:, k * 128:(k + 1) * 128],
                                      t[f"XN{k}"][:, sl],
                                      start=(k == 0), stop=(k == NCH - 1))
                return last
            p.op("pe", mm,
                 reads=tuple(f"XN{k}@{tt}" for k in range(NCH)) + (wk_,),
                 writes=(pb,))
            p.op("act", lambda s_, pb=pb, sl=sl, dst=dst: s_.copy(
                t[dst][:, sl], t[pb][:, 0:TT]),
                reads=(pb,), writes=(f"{dst}@{tt}",))

        rit = 0

        def rotary(tt, side):
            nonlocal rit
            src_, t1, t2 = ("QP", "TQ1", "TQ2") if side == "q" else ("KP", "TK1", "TK2")
            sl = slice(tt * TT, (tt + 1) * TT)
            par = rit % 2
            rit += 1
            pA = "PS2"
            pR = "PS3"
            s0, s1, s2, s3 = (f"SQ{4 * par}", f"SQ{4 * par + 1}",
                              f"SQ{4 * par + 2}", f"SQ{4 * par + 3}")
            rn = f"RN2{par}"
            p.op("pool", lambda g, src_=src_, sl=sl, s0=s0, tt=tt: g.tensor_tensor(
                t[s0][:, 0:TT], t[src_][:, sl], t[src_][:, sl], op=ALU.mult),
                reads=(f"{src_}@{tt}",), writes=(s0,))
            p.op("pe", lambda eng, pA=pA, s0=s0: eng.matmul(
                t[pA][0:2, 0:TT], t["EHD2b"][:, 0:2], t[s0][:, 0:TT],
                start=True, stop=True),
                reads=(s0, "EHD2b"), writes=(pA,))
            p.op("act", lambda s_, pA=pA, rn=rn: s_.activation(
                t[rn][:, 0:TT], t[pA][0:2, 0:TT], AF.Ln,
                bias=t["CONSTS"][0:2, 6:7]),
                reads=(pA, "CONSTS"), writes=(rn,))
            p.op("act", lambda s_, rn=rn: s_.activation(
                t[rn][:, 0:TT], t[rn][:, 0:TT], AF.Exp, scale=-0.5),
                reads=(rn,), writes=(rn,))
            p.op("pe", lambda eng, pA=pA, rn=rn: eng.matmul(
                t[pA][:, 0:TT], t["E2Tf"][:, 0:128], t[rn][:, 0:TT],
                start=True, stop=True),
                reads=(rn, "E2Tf"), writes=(pA,))
            p.op("pe", lambda eng, pR=pR, src_=src_, sl=sl, tt=tt: eng.matmul(
                t[pR][:, 0:TT], t["RHTb"][:, 0:128], t[src_][:, sl],
                start=True, stop=True),
                reads=(f"{src_}@{tt}", "RHTb"), writes=(pR,))
            p.op("pool", lambda g, t1=t1, sl=sl, src_=src_, s2=s2, tt=tt:
                 g.tensor_tensor(
                     t[s2][:, 0:TT], t[src_][:, sl], t[t1][:, sl], op=ALU.mult),
                 reads=(f"{src_}@{tt}", t1), writes=(s2,))
            p.op("dve", lambda v, t2=t2, sl=sl, pR=pR, s3=s3: v.tensor_tensor(
                t[s3][:, 0:TT], t[pR][:, 0:TT], t[t2][:, sl], op=ALU.mult),
                reads=(pR, t2), writes=(s3,))
            p.op("pool", lambda g, s1=s1, s2=s2, s3=s3: g.tensor_tensor(
                t[s1][:, 0:TT], t[s2][:, 0:TT], t[s3][:, 0:TT], op=ALU.add),
                reads=(s2, s3), writes=(s1,))
            p.op("dve", lambda v, src_=src_, sl=sl, pA=pA, s1=s1, tt=tt:
                 v.tensor_tensor(
                     t[src_][:, sl], t[s1][:, 0:TT], t[pA][:, 0:TT], op=ALU.mult),
                 reads=(s1, pA), writes=(f"{src_}@{tt}",))

        for tt in range(NTT + 1):
            if tt < NTT:
                proj(tt, 0, "QP")
                proj(tt, 1, "KP")
                proj(tt, 2, "VP")
            if tt >= 1:
                rotary(tt - 1, "q")
                rotary(tt - 1, "k")
        for hh in range(2):
            h = 2 * ch + hh
            b = 64 * hh
            vt = "VTM0"
            for j0 in range(4):
                def trs(eng, b=b, j0=j0):
                    last = None
                    for jj in range(4):
                        j = 4 * j0 + jj
                        last = eng.transpose(t["PTB"][:, jj * 64:(jj + 1) * 64],
                                             t["VP"][b:b + 64, j * 128:(j + 1) * 128],
                                             t["IDENT2b"][b:b + 64, 0:64])
                    return last
                p.op("pe", trs, reads=(f"VP@{j0}", "IDENT2b"), writes=("PTB",))
                p.op("dve", lambda v, vt=vt, j0=j0: v.tensor_copy(
                    bass.AP(t[vt], 4 * j0 * 65, [[1040, 128], [65, 4], [1, 64]]),
                    bass.AP(t["PTB"], 0, [[1024, 128], [64, 4], [1, 64]])),
                    reads=("PTB",), writes=(vt,))
            nsp = 0
            for grp in range(2):
              qts = list(range(8 * grp, 8 * grp + 8))

              def emit_s_l(qt):
                c_lo = max(0, qt - 2)
                bw = (qt - c_lo + 1) * 128
                pb = f"PS{qt % 3}"
                kt0, kt1 = c_lo // 4, qt // 4
                p.op("pe", lambda eng, b=b, qt=qt, c_lo=c_lo, bw=bw, pb=pb:
                     eng.matmul(t[pb][:, 0:bw],
                                t["QP"][b:b + 64, qt * 128:(qt + 1) * 128],
                                t["KP"][b:b + 64, c_lo * 128:c_lo * 128 + bw],
                                start=True, stop=True),
                     reads=(f"QP@{qt // 4}", f"KP@{kt0}", f"KP@{kt1}"),
                     writes=(pb,))

              def emit_m_l(qt):
                c_lo = max(0, qt - 2)
                nb = qt - c_lo + 1
                bw = nb * 128
                pb = f"PS{qt % 3}"
                nm = f"NM{qt % 2}"
                p.op("dve", lambda v, pb=pb, bw=bw, nm=nm: v.tensor_reduce(
                    t[nm][:, 0:1], t[pb][:, 0:bw], axis=AX.X, op=ALU.max,
                    negate=True),
                    reads=(pb,), writes=(nm,))
                sq = f"SQ{4 + qt % 4}"
                p.op("act", lambda s_, pb=pb, sq=sq, bw=bw, nm=nm: s_.activation(
                    t[sq][:, 0:bw], t[pb][:, 0:bw], AF.Exp, bias=t[nm][:, 0:1]),
                    reads=(pb, nm), writes=(sq,))
                if qt >= 2:
                    p.op("pool", lambda g, sq=sq: g.tensor_tensor(
                        t[sq][:, 0:128], t[sq][:, 0:128], t["TRIUb"][:, 0:128],
                        op=ALU.mult),
                        reads=("TRIUb", sq), writes=(sq,))
                p.op("pool", lambda g, sq=sq, bw=bw: g.tensor_tensor(
                    t[sq][:, bw - 128:bw], t[sq][:, bw - 128:bw],
                    t["TRILQb" if False else "TRILb"][:, 0:128], op=ALU.mult),
                    reads=("TRILb", sq), writes=(sq,))

              def emit_t_l(qt):
                c_lo = max(0, qt - 2)
                nb = qt - c_lo + 1
                sq = f"SQ{4 + qt % 4}"
                pcol = 512 * (qt % 2)
                def trs(eng, sq=sq, nb=nb, pcol=pcol):
                    last = None
                    for cc in range(nb):
                        last = eng.transpose(
                            t["PTB"][:, pcol + cc * 128:pcol + (cc + 1) * 128],
                            t[sq][:, cc * 128:(cc + 1) * 128], t["IDENTb"][:])
                    return last
                p.op("pe", trs, reads=(sq, "IDENTb"), writes=("PTB",))
                ptt = f"PTT{qt % 2}"
                p.op("dve", lambda v, ptt=ptt, nb=nb, pcol=pcol: v.tensor_copy(
                    t[ptt][:, 0:nb * 128], t["PTB"][:, pcol:pcol + nb * 128]),
                    reads=("PTB",), writes=(ptt,))

              def emit_pv_l(qt):
                c_lo = max(0, qt - 2)
                nb = qt - c_lo + 1
                pz = "PZ0"
                zsl = slice((qt % 8) * 128, (qt % 8) * 128 + 128)
                ptt = f"PTT{qt % 2}"
                def pv(eng, vt=vt, c_lo=c_lo, nb=nb, pz=pz, zsl=zsl, ptt=ptt):
                    last = None
                    for cc in range(nb):
                        last = eng.matmul(
                            t[pz][:, zsl],
                            t[vt][:, (c_lo + cc) * 65:(c_lo + cc) * 65 + 65],
                            t[ptt][:, cc * 128:(cc + 1) * 128],
                            start=(cc == 0), stop=(cc == nb - 1))
                    return last
                p.op("pe", pv, reads=(vt, ptt), writes=(pz,))

              for k_ in range(len(qts) + 3):
                if k_ < len(qts):
                    emit_s_l(qts[k_])
                if 1 <= k_ and k_ - 1 < len(qts):
                    emit_m_l(qts[k_ - 1])
                if 2 <= k_ and k_ - 2 < len(qts):
                    emit_t_l(qts[k_ - 2])
                if k_ >= 3:
                    emit_pv_l(qts[k_ - 3])
              if True:
                pz = "PZ0"
                for half2 in range(2):
                    psl = slice(half2 * TT, (half2 + 1) * TT)
                    qsl = slice(grp * 1024 + half2 * TT, grp * 1024 + (half2 + 1) * TT)
                    br = f"BROW{half2 % 2}"
                    p.op("act", lambda s_, pz=pz, psl=psl, br=br: s_.activation(
                        t[br][:, 0:TT], t[pz][64:65, psl], AF.Ln),
                        reads=(pz,), writes=(br,))
                    p.op("act", lambda s_, br=br: s_.activation(
                        t[br][:, 0:TT], t[br][:, 0:TT], AF.Exp, scale=-1.0),
                        reads=(br,), writes=(br,))
                    p.op("pe", lambda eng, br=br: eng.matmul(
                        t["PRB"][:, 0:TT], t["ONESf"][0:1, 0:64],
                        t[br][0:1, 0:TT], start=True, stop=True),
                        reads=(br, "ONESf"), writes=("PRB",))
                    sq = f"SQ{nsp % 2}"
                    nsp += 1
                    p.op("dve", lambda v, sq=sq: v.tensor_copy(
                        t[sq][0:64, 0:TT], t["PRB"][0:64, 0:TT]),
                        reads=("PRB",), writes=(sq,))
                    p.op("dve", lambda v, ch=ch, b=b, qsl=qsl, sq=sq, pz=pz, psl=psl:
                         v.tensor_tensor(
                             t[f"ZB{ch}"][b:b + 64, qsl], t[pz][0:64, psl],
                             t[sq][0:64, 0:TT], op=ALU.mult),
                         reads=(pz, sq), writes=(f"ZB{ch}",))
    for c in range(NCH):
        wk_ = f"W{c % 2}"
        wt = t[wk_]
        ctx.wload(wk_, wo_hbm[c], 1024)
        for tt in range(NTT):
            sl = slice(tt * TT, (tt + 1) * TT)
            xsl = slice(ctx.xs_off + tt * TT, ctx.xs_off + (tt + 1) * TT)
            pb = f"PS{tt % 2}"
            def mm(eng, wt=wt, sl=sl, pb=pb):
                last = None
                for k in range(NCH):
                    last = eng.matmul(t[pb][:, 0:TT], wt[:, k * 128:(k + 1) * 128],
                                      t[f"ZB{k}"][:, sl],
                                      start=(k == 0), stop=(k == NCH - 1))
                return last
            p.op("pe", mm, reads=tuple(f"ZB{k}" for k in range(NCH)) + (wk_,),
                 writes=(pb,))
            p.op("dve", lambda v, c=c, pb=pb, xsl=xsl: v.tensor_tensor(
                t[f"XS{c}"][:, xsl], t[f"XS{c}"][:, xsl], t[pb][:, 0:TT], op=ALU.add),
                reads=(f"XS{c}", pb), writes=(f"XS{c}",))


def build_l1ffn_asub(nseq=2):
    nc = bass.Bass("TRN2", target_bir_lowering=False)
    x_hbm = nc.dram_tensor("x", [nseq, D, S], F32, kind="ExternalInput").ap()
    y_hbm = nc.dram_tensor("y", [nseq, D, S], F32, kind="ExternalOutput").ap()
    wqkv = nc.dram_tensor("wqkv", [24, 128, D], F32, kind="ExternalInput").ap()
    wo = nc.dram_tensor("wo", [NCH, 128, D], F32, kind="ExternalInput").ap()
    qsT = nc.dram_tensor("qsT", [128, 1], F32, kind="ExternalInput").ap()
    qspT = nc.dram_tensor("qspT", [128, 1], F32, kind="ExternalInput").ap()
    ksT = nc.dram_tensor("ksT", [128, 1], F32, kind="ExternalInput").ap()
    kspT = nc.dram_tensor("kspT", [128, 1], F32, kind="ExternalInput").ap()

    es = ExitStack()
    with es:
        prog = Prog(nc, es)
        ctx = Ctx(nc, es, prog)
        streams = {"xs": prog.stream("d_xs", "sp"),
                   "setup": prog.stream("d_setup", "sp"),
                   "tr": prog.stream("d_tr", "sp"),
                   "wg": prog.stream("d_wg", "pool")}
        for i in range(6):
            prog.stream(f"d_W{i}", "sp")
            prog.stream(f"dp_W{i}", "pool")
        make_std(ctx, n_w_tiles=6, w_cols=1024, psum=False, xs_cols=S + 1)
        ctx.xs_off = 1
        sb, ps = ctx.sb, ctx.ps
        alloc_asub(ctx)
        t = ctx.t
        emit_asub_setup(ctx, streams, qsT, qspT, ksT, kspT)
        load_constants(ctx, streams)
        prog.freeze_stream("d_setup")

        for seq in range(nseq):
            emit_load_x(ctx, x_hbm, seq, streams)
            for c in range(NCH):
                prog.op("dve", lambda v, c=c: v.memset(t[f"XS{c}"][:, 0:1], 0.0),
                        writes=(f"XS{c}",))
            prog.op("dve", lambda v: v.memset(
                bass.AP(t["VTM0"], 64, [[1040, 128], [65, 16], [1, 1]]), 1.0),
                writes=("VTM0",))
            emit_tables(ctx, streams, ctx.tb_aps,
                        t["QSC"][:, 0:1], t["QSPC"][:, 0:1],
                        t["KSC"][:, 0:1], t["KSPC"][:, 0:1],
                        param_regions=("QSC", "QSPC", "KSC", "KSPC"))
            emit_cast_shift(ctx)
            emit_attn_sublayer(ctx, streams, wqkv, wo,
                               t["QSC"][:, 0:1], t["QSPC"][:, 0:1],
                               t["KSC"][:, 0:1], t["KSPC"][:, 0:1],
                               param_regions=("QSC", "QSPC", "KSC", "KSPC"))
            emit_store_x(ctx, y_hbm, seq, streams)
        prog.wait_region("sp", tuple(f"hbm:x{s_}" for s_ in range(nseq)))
        prog.finalize()
    return nc


def alloc_asub(ctx):
    sb, ps = ctx.sb, ctx.ps
    sb("QP", [128, S], BF16)
    sb("KP", [128, S], BF16)
    sb("VP", [128, S], BF16)
    sb("VTM0", [128, 16 * 65], BF16)
    sb("TRIUb", [128, 128], BF16)
    sb("TRILb", [128, 128], BF16)
    sb("RHTb", [128, 128], BF16)
    sb("EHD2b", [128, 2], BF16)
    sb("E2Tf", [2, 128], F32)
    sb("RN20", [2, TT], F32)
    sb("RN21", [2, TT], F32)
    sb("NM0", [128, 1], F32)
    sb("NM1", [128, 1], F32)
    for i in range(4, 8):
        sb(f"SQ{i}", [128, TT], BF16)
    for i in range(2):
        sb(f"PTT{i}", [128, 384], BF16)
    for nm in ("TQ1", "TQ2", "TK1", "TK2"):
        sb(nm, [128, S], BF16)
    for ch in range(NCH):
        sb(f"ZB{ch}", [128, S], BF16)
    for nm in ("QSC", "QSPC", "KSC", "KSPC"):
        sb(nm, [128, 6], F32)
    ps("PS0", [128, 512], F32)
    ps("PS1", [128, 512], F32)
    ps("PS2", [128, 512], F32)
    ps("PS3", [128, 512], F32)
    ps("PZ0", [65, 1024], F32)
    ps("PRB", [64, 512], F32)
    ps("PTB", [128, 1024], BF16)


def emit_asub_setup(ctx, streams, qsT, qspT, ksT, kspT):
    p, t = ctx.p, ctx.t
    nc = ctx.nc
    for nm, src in (("QSC", qsT), ("QSPC", qspT), ("KSC", ksT), ("KSPC", kspT)):
        p.op("sp", lambda g, nm=nm, src=src: g.dma_start(
            out=t[nm][:, 0:src.shape[-1]], in_=src),
            reads=(f"hbm:{nm}",), writes=(nm,), stream=streams["setup"])
    ctx.tb_aps = {}
    for nm in ("TRIL", "RHT", "EHD2"):
        shape = {"TRIL": [128, 128], "RHT": [128, 128], "EHD2": [128, 2]}[nm]
        ap = nc.dram_tensor("CONST_" + nm, shape, F32, kind="ExternalInput").ap()
        dstn = nm + "b" if nm in ("TRIL", "RHT") else "EHD2b"
        p.op("pool", lambda g, ap=ap, dstn=dstn: g.dma_start(out=t[dstn][:], in_=ap),
             reads=(f"c:{nm}",), writes=(dstn,), stream=streams["wg"])
    for nm in ("TBQC", "TBQS", "TBKC", "TBKS"):
        ctx.tb_aps[nm] = nc.dram_tensor("CONST_" + nm, [128, S], F32,
                                        kind="ExternalInput").ap()
    ap = nc.dram_tensor("CONST_E2T", [2, 128], F32, kind="ExternalInput").ap()
    p.op("sp", lambda g, ap=ap: g.dma_start(out=t["E2Tf"][:], in_=ap),
         reads=("c:E2T",), writes=("E2Tf",), stream=streams["setup"])
    ap2 = nc.dram_tensor("CONST_TRIU", [128, 128], F32, kind="ExternalInput").ap()
    p.op("pool", lambda g: g.dma_start(out=t["TRIUb"][:], in_=ap2),
         reads=("c:TRIU",), writes=("TRIUb",), stream=streams["wg"])


def build_ln_out(nseq=2):
    nc = bass.Bass("TRN2", target_bir_lowering=False)
    x_hbm = nc.dram_tensor("x", [nseq, D, S], F32, kind="ExternalInput").ap()
    y_hbm = nc.dram_tensor("y", [nseq, D, S], F32, kind="ExternalOutput").ap()
    es = ExitStack()
    with es:
        prog = Prog(nc, es)
        ctx = Ctx(nc, es, prog)
        streams = {"xs": prog.stream("d_xs", "sp"),
                   "setup": prog.stream("d_setup", "sp"),
                   "wg": prog.stream("d_wg", "pool")}
        make_std(ctx, n_w_tiles=0, w_cols=0, psum=True)
        load_constants(ctx, streams)
        prog.freeze_stream("d_setup")
        for seq in range(nseq):
            emit_load_x(ctx, x_hbm, seq, streams)
            emit_cast_x(ctx)
            emit_ln(ctx, out_f32="XS")
            emit_store_x(ctx, y_hbm, seq, streams)
        prog.wait_region("sp", tuple(f"hbm:x{s_}" for s_ in range(nseq)))
        prog.finalize()
    return nc


def build_axpy(nseq=2, alpha=0.5):
    nc = bass.Bass("TRN2", target_bir_lowering=False)
    x_hbm = nc.dram_tensor("x", [nseq, D, S], F32, kind="ExternalInput").ap()
    yin_hbm = nc.dram_tensor("yin", [nseq, D, S], F32, kind="ExternalInput").ap()
    y_hbm = nc.dram_tensor("y", [nseq, D, S], F32, kind="ExternalOutput").ap()
    es = ExitStack()
    with es:
        prog = Prog(nc, es)
        ctx = Ctx(nc, es, prog)
        streams = {"xs": prog.stream("d_xs", "sp"), "y": prog.stream("d_y", "sp")}
        t = ctx.t
        for i in range(NCH):
            ctx.sb(f"XS{i}", [128, S], F32)
        for i in range(2):
            ctx.sb(f"YT{i}", [128, S], F32)
        for seq in range(nseq):
            emit_load_x(ctx, x_hbm, seq, streams)
            for c in range(NCH):
                yt = f"YT{c % 2}"
                prog.op("sp", lambda g, c=c, yt=yt, seq=seq: g.dma_start(
                    out=t[yt][:], in_=yin_hbm[seq, c * 128:(c + 1) * 128, :]),
                    reads=(f"hbm:yin{seq}",), writes=(yt,), stream=streams["y"])
                for tt in range(NTT):
                    sl = slice(tt * TT, (tt + 1) * TT)
                    prog.op("dve", lambda v, c=c, yt=yt, sl=sl: v.scalar_tensor_tensor(
                        t[f"XS{c}"][:, sl], t[yt][:, sl], alpha, t[f"XS{c}"][:, sl],
                        op0=ALU.mult, op1=ALU.add),
                        reads=(yt, f"XS{c}"), writes=(f"XS{c}",))
            emit_store_x(ctx, y_hbm, seq, streams)
        prog.wait_region("sp", tuple(f"hbm:x{s_}" for s_ in range(nseq)))
        prog.finalize()
    return nc




_LAST_PROG = None


def build_fused(nseq=2):
    global _LAST_PROG
    nc = bass.Bass("TRN2", target_bir_lowering=False)
    gi = lambda n, shp: nc.dram_tensor(n, shp, F32, kind="ExternalInput").ap()
    x_hbm = gi("x", [nseq, D, S])
    y_hbm = nc.dram_tensor("y", [nseq, D, S], F32, kind="ExternalOutput").ap()
    spill = nc.dram_tensor("spill", [D, S], F32).ap()
    fw1_ = gi("l0f_w1", [32, 128, D]); fw2_ = gi("l0f_w2", [NCH, 128, 4096])
    fb1T = gi("l0f_b1T", [128, 32]); fb2T = gi("l0f_b2T", [128, NCH])
    awq = gi("l0a_wq", [NCH, 128, D]); awk = gi("l0a_wk", [NCH, 128, D])
    awv = gi("l0a_wv", [NCH, 128, D]); awo = gi("l0a_wo", [NCH, 128, D])
    awqT = gi("l0a_wqT", [NCH, 128, D]); awkT = gi("l0a_wkT", [NCH, 128, D])
    arexp = gi("l0a_rexp", [128, 1024])
    abqT = gi("l0a_bqT", [128, NCH]); abkT = gi("l0a_bkT", [128, NCH])
    abvT = gi("l0a_bvT", [128, NCH]); aboT = gi("l0a_boT", [128, NCH])
    cwq = gi("l1a_wq", [NCH, 128, D]); cwk = gi("l1a_wk", [NCH, 128, D])
    cwv = gi("l1a_wv", [NCH, 128, D]); cwo = gi("l1a_wo", [NCH, 128, D])
    swqkv = gi("s_wqkv", [6, 24, 128, D]); swo = gi("s_wo", [6, NCH, 128, D])
    sw1 = gi("s_w1", [6, 44, 128, D]); sw2 = gi("s_w2", [6, NCH, 128, 2816])
    sqsT = gi("s_qsT", [128, 6]); sqspT = gi("s_qspT", [128, 6])
    sksT = gi("s_ksT", [128, 6]); skspT = gi("s_kspT", [128, 6])
    sb1T = gi("s_b1T", [128, 6 * 44]); sb2T = gi("s_b2T", [128, 6 * NCH])
    slngT = gi("s_lngT", [128, 6 * NCH]); slnbT = gi("s_lnbT", [128, 6 * NCH])

    es = ExitStack()
    with es:
        prog = Prog(nc, es)
        ctx = Ctx(nc, es, prog)
        streams = {"xs": prog.stream("d_xs", "sp"),
                   "setup": prog.stream("d_setup", "sp"),
                   "tr": prog.stream("d_tr", "sp"),
                   "wg": prog.stream("d_wg", "pool"),
                   "sw": prog.stream("d_sw", "sp")}
        for i in range(6):
            prog.stream(f"d_W{i}", "sp")
            prog.stream(f"dp_W{i}", "pool")
        make_std(ctx, n_w_tiles=4, w_cols=1024, psum=False, xs_cols=S + 1)
        ctx.xs_off = 1
        t = ctx.t
        sb, ps = ctx.sb, ctx.ps
        sb("SROW", [1, TT], F32)
        sb("XOT0", [128, TT], F32)
        sb("XOT1", [128, TT], F32)
        sb("TRIUb", [128, 128], BF16)
        sb("TRILb", [128, 128], BF16)
        sb("RHTb", [128, 128], BF16)
        sb("EHD2b", [128, 2], BF16)
        sb("E2Tf", [2, 128], F32)
        sb("B1T", [128, 32], F32)
        sb("B2T", [128, NCH], F32)
        sb("B2H", [128, NCH], F32)
        for nm in ("QSC", "QSPC", "KSC", "KSPC"):
            sb(nm, [128, 6], F32)
        sb("B1C", [128, 6 * 44], F32)
        sb("B2C", [128, 6 * NCH], F32)
        sb("LNG", [128, 6 * NCH], F32)
        sb("LNB", [128, 6 * NCH], F32)
        for nm, src_ in (("B1T", fb1T), ("B2T", fb2T), ("QSC", sqsT),
                         ("QSPC", sqspT), ("KSC", sksT), ("KSPC", skspT),
                         ("B1C", sb1T), ("B2C", sb2T), ("LNG", slngT),
                         ("LNB", slnbT), ("E2Tf", None)):
            if nm == "E2Tf":
                src_ = nc.dram_tensor("CONST_E2T", [2, 128], F32,
                                      kind="ExternalInput").ap()
            prog.op("sp", lambda g, nm=nm, src_=src_: g.dma_start(
                out=t[nm][:], in_=src_),
                reads=(f"hbm:{nm}",), writes=(nm,), stream=streams["setup"])
        load_constants(ctx, streams)
        for nm in ("TRIL", "RHT", "EHD2", "TRIU"):
            shp = {"TRIL": [128, 128], "RHT": [128, 128], "EHD2": [128, 2],
                   "TRIU": [128, 128]}[nm]
            ap = nc.dram_tensor("CONST_" + nm, shp, F32, kind="ExternalInput").ap()
            dstn = nm + "b"
            prog.op("pool", lambda g, ap=ap, dstn=dstn: g.dma_start(
                out=t[dstn][:], in_=ap),
                reads=(f"c:{nm}",), writes=(dstn,), stream=streams["wg"])
        prog.freeze_stream("d_setup")
        tb_f32 = {nm: nc.dram_tensor("CONST_" + nm, [128, S], F32,
                                     kind="ExternalInput").ap()
                  for nm in ("TBQC", "TBQS", "TBKC", "TBKS")}
        def bfdecl(name, shape):
            return nc.dram_tensor(name + "_bf", shape, BF16).ap()
        fw1b = bfdecl("l0f_w1", [32, 128, D])
        fw2b = bfdecl("l0f_w2", [NCH, 128, 4096])
        awqb, awkb, awvb, awob = (bfdecl(n, [NCH, 128, D]) for n in
                                  ("l0a_wq", "l0a_wk", "l0a_wv", "l0a_wo"))
        cwqb, cwkb, cwvb, cwob = (bfdecl(n, [NCH, 128, D]) for n in
                                  ("l1a_wq", "l1a_wk", "l1a_wv", "l1a_wo"))
        swqkvb = bfdecl("s_wqkv", [6, 24, 128, D])
        swob = bfdecl("s_wo", [6, NCH, 128, D])
        sw1b = bfdecl("s_w1", [6, 44, 128, D])
        sw2b = bfdecl("s_w2", [6, NCH, 128, 2816])
        tb_aps = {nm: bfdecl("TB" + nm, [128, S]) for nm in
                  ("TBQC", "TBQS", "TBKC", "TBKS")}
        groups = [
            ("cv_l0f", [(fw1_[i], fw1b[i]) for i in range(32)] +
                       [(fw2_[i], fw2b[i]) for i in range(NCH)]),
            ("cv_l0a", [(a[i], b_[i]) for a, b_ in
                        ((awq, awqb), (awk, awkb), (awv, awvb), (awo, awob))
                        for i in range(NCH)]),
            ("cv_tb", [(tb_f32[nm], tb_aps[nm]) for nm in tb_f32]),
        ]
        for si in range(6):
            groups.append((f"cv_s{si}",
                           [(swqkv[si][i], swqkvb[si][i]) for i in range(24)] +
                           [(sw1[si][i], sw1b[si][i]) for i in range(44)] +
                           [(sw2[si][i], sw2b[si][i]) for i in range(NCH)] +
                           [(swo[si][i], swob[si][i]) for i in range(NCH)]))
        groups.append(("cv_l1a", [(a[i], b_[i]) for a, b_ in
                                  ((cwq, cwqb), (cwk, cwkb), (cwv, cwvb),
                                   (cwo, cwob)) for i in range(NCH)]))
        with ctx.scope():
            for i in range(4):
                sb(f"CV{i}", [128, 4096], BF16)
            cvi = 0
            for gname, pairs in groups:
                sin = prog.stream("din_" + gname, "pool")
                sout = prog.stream("dout_" + gname, "sp")
                for srcap, dstap in pairs:
                    cv = f"CV{cvi % 4}"
                    cvi += 1
                    ncols = srcap.shape[-1]
                    p_ = prog
                    p_.op("pool", lambda g, cv=cv, srcap=srcap, ncols=ncols:
                          g.dma_start(out=t[cv][:, 0:ncols], in_=srcap),
                          reads=("hbm:" + srcap.tensor.name,), writes=(cv,),
                          stream=sin)
                    p_.op("sp", lambda g, cv=cv, dstap=dstap, ncols=ncols:
                          g.dma_start(out=dstap, in_=t[cv][:, 0:ncols]),
                          reads=(cv,), writes=("hbm:" + dstap.tensor.name,),
                          stream=sout)
        prog.op("dve", lambda v: v.tensor_scalar(
            t["B2H"][:], t["B2T"][:], 0.5, None, op0=ALU.mult),
            reads=("B2T",), writes=("B2H",))

        def l0ffn_scope():
            with ctx.scope():
                for i in range(32):
                    sb(f"H{i}", [128, 1024], BF16)
                sb("W4", [128, 2048], BF16)
                sb("W5", [128, 2048], BF16)
                for i in range(4):
                    ps(f"P{i}", [128, 512], F32)
                ps("P4", [128, 512], F32)
                ps("P5", [128, 512], F32)
                ps("P6", [128, 512], F32)
                ps("P7", [128, 512], F32)
                ctx.ln_banks = ("P6", "P7", "P4", "P5")
                emit_cast_x(ctx)
                emit_ln(ctx)
                emit_ffn_l0(ctx, fw1b, lambda c4: t["B1T"][:, c4:c4 + 1], fw2b,
                            lambda c: t["B2H"][:, c:c + 1], streams,
                            bias_regions=("B1T", "B2H"))

        def attn_scope(*, bucket, tag):
            with ctx.scope():
                sb("QP", [128, S], BF16)
                sb("KP", [128, S], BF16)
                sb("VP", [128, S], BF16)
                sb("VTM0", [128, 16 * 65], BF16)
                sb("W4", [128, 1024], BF16)
                sb("W5", [128, 1024], BF16)
                for ch in range(NCH):
                    sb(f"ZB{ch}", [128, S], BF16)
                ps("PS0", [128, 512], F32)
                ps("PS1", [128, 512], F32)
                ps("PS2", [128, 512], F32)
                ps("PZ", [65, 2048], F32)
                ps("PTB", [128, 1024], BF16)
                ctx.ln_banks = ("PS0", "PS1")
                if bucket:
                    sb("WRQ", [128, 1024], F32)
                    sb("WRK", [128, 1024], F32)
                    sb("NCSQ", [1, 128], F32)
                    sb("NCSK", [1, 128], F32)
                    sb("BQR", [1, 128], F32)
                    sb("BKR", [1, 128], F32)
                    for nm, src_ in (("BQ8", abqT), ("BKC", abkT),
                                     ("BVC", abvT), ("BOC", aboT)):
                        dd = sb(nm, [128, NCH], F32)
                        prog.op("sp", lambda g, dd=dd, src_=src_: g.dma_start(
                            out=dd[:], in_=src_),
                            reads=(f"hbm:{nm}2",), writes=(nm,), stream=streams["tr"])
                    prog.op("dve", lambda v: v.tensor_scalar(
                        t["BQ8"][:], t["BQ8"][:], 0.125, None, op0=ALU.mult),
                        reads=("BQ8",), writes=("BQ8",))
                    with ctx.scope():
                        sb("WTMP", [128, 1024], F32)
                        sb("REXP", [128, 1024], F32)
                        prog.op("sp", lambda g: g.dma_start(
                            out=t["REXP"][:], in_=arexp),
                            reads=("hbm:rexp",), writes=("REXP",), stream=streams["tr"])
                        emit_wr_setup_inline(ctx, awqT, awkT, streams)
                    sb("AUGQ", [72, S], BF16)
                    sb("AUGK", [72, S], BF16)
                    sb("TRANSQ", [128, S], BF16)
                    sb("TRANSK", [128, S], BF16)
                    sb("OHSB", [128, 128], BF16)
                    sb("M16", [128, 16], F32)
                prog.op("dve", lambda v: v.memset(
                    bass.AP(t["VTM0"], 64, [[1040, 128], [65, 16], [1, 1]]), 1.0),
                    writes=("VTM0",))
                emit_cast_x(ctx)
                if bucket:
                    emit_ln(ctx, srow=True,
                            tile_hook=lambda tt: emit_qr_onehot(ctx, tt, streams))
                    emit_full_attn(ctx, streams, bucket=True, causal=False,
                                   wq_hbm=awqb, wk_hbm=awkb, wv_hbm=awvb,
                                   wo_hbm=awob,
                                   bq8_col=lambda c: t["BQ8"][:, c:c + 1],
                                   bk_col=lambda c: t["BKC"][:, c:c + 1],
                                   bv_col=lambda c: t["BVC"][:, c:c + 1],
                                   bo_col=lambda c: t["BOC"][:, c:c + 1],
                                   param_regions=("BQ8", "BKC", "BVC", "BOC"),
                                   tag=tag)
                else:
                    emit_ln(ctx)
                    emit_full_attn(ctx, streams, bucket=False, causal=True,
                                   wq_hbm=cwqb, wk_hbm=cwkb, wv_hbm=cwvb,
                                   wo_hbm=cwob, tag=tag)

        for seq in range(nseq):
            emit_load_x(ctx, x_hbm, seq, streams)
            for c in range(NCH):
                prog.op("dve", lambda v, c=c: v.memset(t[f"XS{c}"][:, 0:1], 0.0),
                        writes=(f"XS{c}",))
            prog.cur_phase = f"l0ffnA{seq}"
            l0ffn_scope()
            prog.cur_phase = f"l0attn{seq}"
            attn_scope(bucket=True, tag=f"l0a{seq}")
            prog.cur_phase = f"l0ffnB{seq}"
            l0ffn_scope()
            for call in range(2):
                for c in range(NCH):
                    prog.op("sp", lambda g, c=c: g.dma_start(
                        out=spill[c * 128:(c + 1) * 128, :],
                        in_=t[f"XS{c}"][:, 1:1 + S]),
                        reads=(f"XS{c}",), writes=("hbm:spill",), stream=streams["sw"])
                prog.cur_phase = f"lnout{seq}_{call}"
                with ctx.scope():
                    ps("P6", [128, 512], F32)
                    ps("P7", [128, 512], F32)
                    ctx.ln_banks = ("P6", "P7")
                    emit_cast_x(ctx)
                    emit_ln(ctx, out_f32="XS")
                with ctx.scope():
                    for nm in ("TQ1", "TQ2", "TK1", "TK2"):
                        sb(nm, [128, S], BF16)
                    for i in range(6):
                        with ctx.scope():
                            sb("QP", [128, S], BF16)
                            sb("KP", [128, S], BF16)
                            sb("VP", [128, S], BF16)
                            sb("VTM0", [128, 16 * 65], BF16)
                            sb("W4", [128, 1024], BF16)
                            sb("W5", [128, 1024], BF16)
                            sb("RN20", [2, TT], F32)
                            sb("RN21", [2, TT], F32)
                            sb("NM0", [128, 1], F32)
                            sb("NM1", [128, 1], F32)
                            for k_ in range(4, 8):
                                sb(f"SQ{k_}", [128, TT], BF16)
                            for k_ in range(2):
                                sb(f"PTT{k_}", [128, 384], BF16)
                            for ch in range(NCH):
                                sb(f"ZB{ch}", [128, S], BF16)
                            for k_ in range(4):
                                ps(f"PS{k_}", [128, 512], F32)
                            ps("PZ0", [65, 1024], F32)
                            ps("PRB", [64, 512], F32)
                            ps("PTB", [128, 1024], BF16)
                            prog.cur_phase = f"asub{seq}_{call}_{i}"
                            prog.op("dve", lambda v: v.memset(
                                bass.AP(t["VTM0"], 64,
                                        [[1040, 128], [65, 16], [1, 1]]), 1.0),
                                writes=("VTM0",))
                            emit_tables(ctx, streams, tb_aps,
                                        t["QSC"][:, i:i + 1], t["QSPC"][:, i:i + 1],
                                        t["KSC"][:, i:i + 1], t["KSPC"][:, i:i + 1],
                                        param_regions=("QSC", "QSPC", "KSC", "KSPC"))
                            emit_cast_shift(ctx)
                            emit_attn_sublayer(
                                ctx, streams, swqkvb[i], swob[i],
                                t["QSC"][:, i:i + 1], t["QSPC"][:, i:i + 1],
                                t["KSC"][:, i:i + 1], t["KSPC"][:, i:i + 1],
                                param_regions=("QSC", "QSPC", "KSC", "KSPC"),
                                tag=f"s{seq}{call}{i}")
                        with ctx.scope():
                            for k_ in range(22):
                                sb(f"H{k_}", [128, 1024], BF16)
                            sb("W4", [128, 2816], BF16)
                            sb("W5", [128, 2816], BF16)
                            for k_ in range(4):
                                ps(f"P{k_}", [128, 512], F32)
                            ps("P4", [128, 512], F32)
                            ps("P5", [128, 512], F32)
                            ps("P6", [128, 512], F32)
                            ps("P7", [128, 512], F32)
                            ctx.ln_banks = ("P6", "P7", "P4", "P5")
                            prog.cur_phase = f"fsub{seq}_{call}_{i}"
                            emit_cast_shift(ctx)
                            ctx.ln_param_regions = ("LNG", "LNB")
                            emit_ln(ctx,
                                    g_col=lambda c, i=i: t["LNG"][:, 8 * i + c:8 * i + c + 1],
                                    b_col=lambda c, i=i: t["LNB"][:, 8 * i + c:8 * i + c + 1])
                            emit_ffn_gated(
                                ctx, sw1b[i],
                                lambda oc, i=i: t["B1C"][:, 44 * i + oc:44 * i + oc + 1],
                                sw2b[i],
                                lambda c, i=i: t["B2C"][:, 8 * i + c:8 * i + c + 1],
                                streams, None, None,
                                param_regions=("B1C", "B2C"), tag=f"f{seq}{call}{i}")
                for c in range(NCH):
                    for tt in range(NTT):
                        sl = slice(tt * TT, (tt + 1) * TT)
                        xsl = slice(1 + tt * TT, 1 + (tt + 1) * TT)
                        xo = "XOT0" if tt % 2 == 0 else "XOT1"
                        prog.op("sp", lambda g, c=c, sl=sl, xo=xo: g.dma_start(
                            out=t[xo][:, 0:TT], in_=spill[c * 128:(c + 1) * 128, sl]),
                            reads=("hbm:spill",), writes=(xo,), stream=streams["sw"])
                        prog.op("dve", lambda v, c=c, xsl=xsl, xo=xo:
                                v.scalar_tensor_tensor(
                                    t[f"XS{c}"][:, xsl], t[f"XS{c}"][:, xsl], 0.5,
                                    t[xo][:, 0:TT], op0=ALU.mult, op1=ALU.add),
                                reads=(f"XS{c}", xo), writes=(f"XS{c}",))
                if call == 0:
                    prog.cur_phase = f"l1attn{seq}"
                    attn_scope(bucket=False, tag=f"l1a{seq}")
            emit_store_x(ctx, y_hbm, seq, streams)
        prog.wait_region("sp", tuple(f"hbm:x{s_}" for s_ in range(nseq)))
        prog.finalize()
    _LAST_PROG = prog
    return nc


def host_pack_all(inputs):
    gi = lambda k: np.ascontiguousarray(np.asarray(inputs[k], np.float32))
    consts = host_constants()
    bT = lambda b, n: np.ascontiguousarray(np.asarray(b, np.float32).reshape(n, 128).T)

    m = {}
    m["l0ffn"] = dict(w1=pack_w(gi("L0_ffn_w1")), w2=pack_w(gi("L0_ffn_w2")),
                      b1T=bT(inputs["L0_ffn_b1"], 32), b2T=bT(inputs["L0_ffn_b2"], 8),
                      **consts)
    Wq, Wk = gi("L0_attn_Wq"), gi("L0_attn_Wk")
    R = gi("L0_attn_R")
    R_exp = np.zeros((1024, 128), np.float32)
    for h in range(16):
        R_exp[64 * h:64 * h + 64, 8 * h:8 * h + 8] = R.T
    rexp_host = np.ascontiguousarray(
        R_exp.reshape(8, 128, 128).transpose(1, 0, 2).reshape(128, 1024))
    m["l0attn"] = dict(
        wq=pack_w(Wq), wk=pack_w(Wk), wv=pack_w(gi("L0_attn_Wv")),
        wo=pack_w(gi("L0_attn_Wo")),
        wqT=pack_w(np.ascontiguousarray(Wq.T)),
        wkT=pack_w(np.ascontiguousarray(Wk.T)), rexp=rexp_host,
        bqT=bT(inputs["L0_attn_bq"], 8), bkT=bT(inputs["L0_attn_bk"], 8),
        bvT=bT(inputs["L0_attn_bv"], 8), boT=bT(inputs["L0_attn_bo"], 8),
        **consts)
    tohm = lambda W: np.ascontiguousarray(
        np.transpose(np.asarray(W, np.float32), (2, 0, 1)).reshape(1024, 1024))
    m["l1attn"] = dict(
        wq=pack_w(tohm(gi("L1_attn_W_Q"))), wk=pack_w(tohm(gi("L1_attn_W_K"))),
        wv=pack_w(tohm(gi("L1_attn_W_V"))),
        wo=pack_w(np.ascontiguousarray(gi("L1_attn_W_O").T)), **consts)
    Wqkv = gi("L1_ffn_Wqkv")
    Wol = gi("L1_ffn_Wo")
    qsc = gi("L1_ffn_q_scale")
    ksc = gi("L1_ffn_k_scale")
    W1 = gi("L1_ffn_W1")
    b1 = gi("L1_ffn_b1")
    W2 = gi("L1_ffn_W2")
    b2 = gi("L1_ffn_b2")
    lng = gi("L1_ffn_ln_g")
    lnb = gi("L1_ffn_ln_b")
    DI = 2730
    m["asub"] = []
    m["fsub"] = []
    st = lambda v: np.ascontiguousarray(np.concatenate([v, v])[:, None])
    for i in range(6):
        qs, ks = qsc[i], ksc[i]
        m["asub"].append(dict(
            wqkv=pack_w(Wqkv[i]), wo=pack_w(Wol[i]),
            qsT=st(qs), qspT=st(np.concatenate([qs[32:], qs[:32]])),
            ksT=st(ks), kspT=st(np.concatenate([ks[32:], ks[:32]])), **consts))
        W1p = np.zeros((1024, 5632), np.float32)
        W1p[:, 0:DI] = W1[i][:, 0:DI]
        W1p[:, 2816:2816 + DI] = W1[i][:, DI:]
        b1p = np.zeros(5632, np.float32)
        b1p[0:DI] = b1[i][:DI]
        b1p[2816:2816 + DI] = b1[i][DI:]
        W2p = np.zeros((2816, 1024), np.float32)
        W2p[0:DI] = W2[i]
        m["fsub"].append(dict(
            w1=pack_w(W1p), w2=pack_w(W2p), b1T=bT(b1p, 44), b2T=bT(b2[i], 8),
            lngT=bT(lng[i], 8), lnbT=bT(lnb[i], 8), **consts))
    m["lnout"] = dict(**consts)
    m["axpy"] = {}
    return m




def host_pack_fused(inputs):
    m = host_pack_all(inputs)
    consts = host_constants()
    out = {}
    out.update({"l0f_w1": m["l0ffn"]["w1"], "l0f_w2": m["l0ffn"]["w2"],
                "l0f_b1T": m["l0ffn"]["b1T"], "l0f_b2T": m["l0ffn"]["b2T"]})
    a = m["l0attn"]
    out.update({"l0a_wq": a["wq"], "l0a_wk": a["wk"], "l0a_wv": a["wv"],
                "l0a_wo": a["wo"], "l0a_wqT": a["wqT"], "l0a_wkT": a["wkT"],
                "l0a_rexp": a["rexp"], "l0a_bqT": a["bqT"], "l0a_bkT": a["bkT"],
                "l0a_bvT": a["bvT"], "l0a_boT": a["boT"]})
    c = m["l1attn"]
    out.update({"l1a_wq": c["wq"], "l1a_wk": c["wk"], "l1a_wv": c["wv"],
                "l1a_wo": c["wo"]})
    out["s_wqkv"] = np.stack([m["asub"][i]["wqkv"] for i in range(6)])
    out["s_wo"] = np.stack([m["asub"][i]["wo"] for i in range(6)])
    out["s_w1"] = np.stack([m["fsub"][i]["w1"] for i in range(6)])
    out["s_w2"] = np.stack([m["fsub"][i]["w2"] for i in range(6)])
    out["s_qsT"] = np.concatenate([m["asub"][i]["qsT"] for i in range(6)], 1)
    out["s_qspT"] = np.concatenate([m["asub"][i]["qspT"] for i in range(6)], 1)
    out["s_ksT"] = np.concatenate([m["asub"][i]["ksT"] for i in range(6)], 1)
    out["s_kspT"] = np.concatenate([m["asub"][i]["kspT"] for i in range(6)], 1)
    out["s_b1T"] = np.concatenate([m["fsub"][i]["b1T"] for i in range(6)], 1)
    out["s_b2T"] = np.concatenate([m["fsub"][i]["b2T"] for i in range(6)], 1)
    out["s_lngT"] = np.concatenate([m["fsub"][i]["lngT"] for i in range(6)], 1)
    out["s_lnbT"] = np.concatenate([m["fsub"][i]["lnbT"] for i in range(6)], 1)
    out.update(consts)
    return out


def kernel_fused(**inputs):
    x = np.asarray(inputs["L0_attn_x"], np.float32)
    wm = host_pack_fused(inputs)
    nc, innames = _get_prog("fused", build_fused)
    ims = []
    for c in range(NCORES):
        im = {k: v for k, v in wm.items() if k in innames}
        im["x"] = np.ascontiguousarray(x[2 * c:2 * c + 2].transpose(0, 2, 1))
        ims.append(im)
    res = run_bass_kernel_spmd(nc, ims, core_ids=list(range(NCORES)))
    out = np.empty((16, S, D), np.float32)
    for c in range(NCORES):
        out[2 * c:2 * c + 2] = res.results[c]["y"].transpose(0, 2, 1)
    return out


_PROG_CACHE = {}


def _get_prog(name, builder):
    if name not in _PROG_CACHE:
        nc = builder()
        innames = set()
        for alloc in nc.m.functions[0].allocations:
            if isinstance(alloc, mybir.MemoryLocationSet) and alloc.kind == "ExternalInput":
                innames.add(alloc.memorylocations[0].name)
        _PROG_CACHE[name] = (nc, innames)
    return _PROG_CACHE[name]


def kernel(**inputs):
    return kernel_fused(**inputs)


def kernel_unfused(**inputs):
    x = np.asarray(inputs["L0_attn_x"], np.float32)
    m = host_pack_all(inputs)
    nseq = 2
    cores = list(range(NCORES))
    xs = [np.ascontiguousarray(x[2 * c:2 * c + 2].transpose(0, 2, 1))
          for c in range(NCORES)]

    def launch(name, builder, wm, xkey="x", extra=None):
        nc, innames = _get_prog(name, builder)
        ims = []
        for c in range(NCORES):
            im = {k: v for k, v in wm.items() if k in innames}
            im[xkey] = xs[c]
            if extra is not None:
                im["yin"] = extra[c]
            ims.append(im)
        res = run_bass_kernel_spmd(nc, ims, core_ids=cores)
        return [res.results[c]["y"] for c in range(NCORES)]

    def set_x(out):
        for c in range(NCORES):
            xs[c] = out[c]

    set_x(launch("l0ffn", build_l0ffn, m["l0ffn"]))
    set_x(launch("l0attn", build_l0attn, m["l0attn"]))
    set_x(launch("l0ffn", build_l0ffn, m["l0ffn"]))
    for call in range(2):
        inner = launch("lnout", build_ln_out, m["lnout"])
        saved = xs
        xs = inner
        for i in range(6):
            set_x(launch("asub", build_l1ffn_asub, m["asub"][i]))
            set_x(launch("fsub", build_l1ffn_fsub, m["fsub"][i]))
        inner = xs
        xs = saved
        set_x(launch("axpy", build_axpy, m["axpy"], extra=inner))
        if call == 0:
            set_x(launch("l1attn", build_l1attn, m["l1attn"]))
    out = np.empty((16, S, D), np.float32)
    for c in range(NCORES):
        out[2 * c:2 * c + 2] = xs[c].transpose(0, 2, 1)
    return out



# version 11 (final)
# speedup vs baseline: 1.0425x; 1.0425x over previous
import math
import numpy as np
from contextlib import ExitStack

import concourse.bass as bass
import concourse.mybir as mybir
from concourse.bass_utils import run_bass_kernel_spmd

F32 = mybir.dt.float32
BF16 = mybir.dt.bfloat16
AF = mybir.ActivationFunctionType
ALU = mybir.AluOpType
AX = mybir.AxisListType

NCORES = 8
S = 2048
D = 1024
NCH = D // 128
TT = 512
NTT = S // TT
EPS = 1e-5
C_FULL = 40.0
OH_SCALE = 64.0


class Prog:
    ENGINES = ("pe", "dve", "act", "pool", "sp")

    def __init__(self, nc, es):
        self.nc = nc
        self.es = es
        self.eng = {"pe": "tensor", "dve": "vector", "act": "scalar",
                    "pool": "gpsimd", "sp": "sync"}
        self.sem = {}
        self.cnt = {}
        self.step = {}
        self.stream_q = {}
        self.frozen = set()
        self.max_waited = {}
        self.eclock = {e: {} for e in self.ENGINES}
        self.reg = {}
        self.queue = {e: [] for e in self.ENGINES}
        self.envp = None
        self.cur_phase = "init"
        self.pe_phases = []
        for e in ("pe", "dve", "act", "pool"):
            self._mksem("s_" + e, 1)

    def _mksem(self, name, step):
        self.sem[name] = self.es.enter_context(self.nc.semaphore(name))
        self.cnt[name] = 0
        self.step[name] = step

    def freeze_stream(self, name):
        self.frozen.add(name)

    def stream(self, name, qeng):
        assert qeng in ("sp", "pool", "act")
        self._mksem(name, 16)
        self.stream_q[name] = qeng
        return name

    def _region(self, key):
        r = self.reg.get(key)
        if r is None:
            r = {"w": {}, "r": {}}
            if "@" in key:
                parent = self.reg.get(key.split("@", 1)[0])
                if parent is not None:
                    r["w"] = dict(parent["w"])
            self.reg[key] = r
        return r

    def op(self, e, fn, reads=(), writes=(), stream=None):
        need = {}
        wr = [self._region(k) for k in writes]
        for k in reads:
            for s, c in self._region(k)["w"].items():
                if need.get(s, 0) < c:
                    need[s] = c
        for r in wr:
            for m in (r["w"], r["r"]):
                for s, c in m.items():
                    if need.get(s, 0) < c:
                        need[s] = c
        ec = self.eclock[e]
        waits = []
        for s in sorted(need):
            c = need[s]
            if s in self.stream_q:
                c = self.cnt[s]
            if ec.get(s, 0) < c:
                waits.append((s, c))
                ec[s] = c
                if s in self.stream_q and self.max_waited.get(s, 0) < c:
                    self.max_waited[s] = c
        if stream is not None:
            w = self.max_waited.get(stream, 0)
            if ec.get(stream, 0) < w:
                waits.append((stream, w))
                ec[stream] = w
        for k in reads:
            for s, c in self._region(k)["w"].items():
                if ec.get(s, 0) < c:
                    ec[s] = c
        if stream is None:
            sname = "s_" + e
        else:
            sname = stream
            assert self.stream_q[stream] == e, (stream, e)
            assert stream not in self.frozen, f"issue on frozen stream {stream}"
        self.cnt[sname] += self.step[sname]
        ident = self.cnt[sname]
        env = self.envp.env() if self.envp is not None else None
        self.queue[e].append((waits, fn, sname, env))
        if e == "pe":
            self.pe_phases.append(self.cur_phase)
        wclock = dict(ec)
        wclock[sname] = max(wclock.get(sname, 0), ident)
        for r in wr:
            r["w"] = wclock
            r["r"] = {}
        for k in reads:
            rr = self._region(k)["r"]
            if rr.get(sname, 0) < ident:
                rr[sname] = ident
        return (sname, ident)

    def wait_all(self, e, items):
        ec = self.eclock[e]
        waits = []
        for s, c in items:
            if ec.get(s, 0) < c:
                waits.append((s, c))
                ec[s] = c
        if waits:
            self.queue[e].append((waits, None, None, None))

    def wait_region(self, e, keys):
        items = {}
        for k in keys:
            for s, c in self._region(k)["w"].items():
                if s in self.stream_q:
                    c = self.cnt[s]
                if items.get(s, 0) < c:
                    items[s] = c
        self.wait_all(e, sorted(items.items()))

    def finalize(self):
        block = self.es.enter_context(self.nc.Block())
        for e in self.ENGINES:
            q = self.queue[e]
            if not q:
                continue
            bass_eng = getattr(self.nc, self.eng[e])

            def run(engine, q=q, prog=self, e=e):
                if e == "pe":
                    cnt = [0]
                    class _Px:
                        def __getattr__(px, k):
                            a = getattr(engine, k)
                            if k in ("matmul", "transpose"):
                                def w(*ar, **kw):
                                    cnt[0] += 1
                                    return a(*ar, **kw)
                                return w
                            return a
                    px = _Px()
                    prog.pe_mm_cum = []
                    for waits, fn, sname, env in q:
                        for s, c in waits:
                            engine.wait_ge(prog.sem[s], c)
                        if fn is not None:
                            if env is not None:
                                prog.envp.restore(env)
                            instr = fn(px)
                            instr.then_inc(prog.sem[sname], prog.step[sname])
                        prog.pe_mm_cum.append(cnt[0])
                    return
                for waits, fn, sname, env in q:
                    for s, c in waits:
                        engine.wait_ge(prog.sem[s], c)
                    if fn is not None:
                        if env is not None:
                            prog.envp.restore(env)
                        instr = fn(engine)
                        instr.then_inc(prog.sem[sname], prog.step[sname])

            getattr(block, self.eng[e])(run)


class Ctx:
    def __init__(self, nc, es, prog):
        self.nc = nc
        self.es = es
        self.p = prog
        self.t = {}
        self.xs_off = 0
        self._scopes = []
        self._dead = []
        self._snaps = []
        self._dirty = True
        self._cur_env = -1
        prog.envp = self

    def env(self):
        if self._dirty:
            self._snaps.append(dict(self.t))
            self._dirty = False
        return len(self._snaps) - 1

    def restore(self, env):
        if env != self._cur_env:
            self.t.clear()
            self.t.update(self._snaps[env])
            self._cur_env = env

    def _addr(self, name):
        for f in self.nc.m.functions:
            for al in f.allocations:
                if isinstance(al, mybir.MemoryLocationSet):
                    ml = al.memorylocations[0]
                    if ml.name == name:
                        return ml.addr, ml.addr + ml.size()
        raise KeyError(name)

    _uid = 0

    def _alloc(self, kind, name, shape, dtype):
        es = self._scopes[-1][0] if self._scopes else self.es
        mk = self.nc.sbuf_tensor if kind == "sb" else self.nc.psum_tensor
        real = name
        if self._scopes:
            Ctx._uid += 1
            real = f"{name}_s{Ctx._uid}"
        h = es.enter_context(mk(real, shape, dtype))
        self.t[name] = h
        self._dirty = True
        lo, hi = self._addr(real)
        space = "sb" if kind == "sb" else "ps"
        if self._scopes:
            self._scopes[-1][1].append((name, space, lo, hi))
        clk = {}
        keep = []
        for (sp2, lo2, hi2, c2) in self._dead:
            if sp2 == space and lo < hi2 and lo2 < hi:
                for s, c in c2.items():
                    if clk.get(s, 0) < c:
                        clk[s] = c
            keep.append((sp2, lo2, hi2, c2))
        self._dead = keep
        if clk:
            r = self.p._region(name)
            for s, c in clk.items():
                if r["w"].get(s, 0) < c:
                    r["w"][s] = c
        return h

    def sb(self, name, shape, dtype):
        return self._alloc("sb", name, shape, dtype)

    def ps(self, name, shape, dtype):
        return self._alloc("ps", name, shape, dtype)

    def wload(self, wk, src_ap, cols=None):
        t = self.t
        ncols = cols if cols is not None else src_ap.shape[-1]
        rkey = "hbm:" + src_ap.tensor.name
        if src_ap.dtype == BF16:
            self.p.op("sp", lambda g, wk=wk, src_ap=src_ap, ncols=ncols:
                      g.dma_start(out=t[wk][:, 0:ncols], in_=src_ap),
                      reads=(rkey,), writes=(wk,), stream="d_" + wk)
        else:
            self.p.op("pool", lambda g, wk=wk, src_ap=src_ap, ncols=ncols:
                      g.dma_start(out=t[wk][:, 0:ncols], in_=src_ap),
                      reads=(rkey,), writes=(wk,), stream="dp_" + wk)

    def scope(self):
        from contextlib import contextmanager

        @contextmanager
        def _sc():
            es = ExitStack()
            self._scopes.append((es, []))
            try:
                yield
            finally:
                es2, names = self._scopes.pop()
                for (name, space, lo, hi) in names:
                    clk = {}
                    for key in [name] + [k for k in self.p.reg
                                         if k.startswith(name + "@")]:
                        r = self.p._region(key)
                        for m in (r["w"], r["r"]):
                            for s, c in m.items():
                                if clk.get(s, 0) < c:
                                    clk[s] = c
                        self.p.reg.pop(key, None)
                    self._dead.append((space, lo, hi, clk))
                    self.t.pop(name, None)
                self._dirty = True
                es2.close()
        return _sc()


def pack_w(w):
    Kd, Nd = w.shape
    nk, no = Kd // 128, Nd // 128
    return np.ascontiguousarray(
        w.reshape(nk, 128, no, 128).transpose(2, 1, 0, 3).reshape(no, 128, nk * 128))


def host_constants():
    c = {}
    c["CONST_ONES"] = np.ones((128, 128), np.float32)
    c["CONST_IDENT"] = np.eye(128, dtype=np.float32)
    c["CONST_IDENT2"] = np.vstack([np.eye(64, dtype=np.float32)] * 2)
    kk, qq = np.meshgrid(np.arange(128), np.arange(128), indexing="ij")
    c["CONST_TRIU"] = (qq >= kk).astype(np.float32)
    c["CONST_TRIL"] = (qq <= kk).astype(np.float32)
    DH, WIN = 64, 256
    half = np.arange(0, DH, 2, dtype=np.float32)
    inv_freq = 1.0 / 10000.0 ** (half / DH)
    tpos = np.arange(S, dtype=np.float32)
    freqs = tpos[:, None] * inv_freq[None, :]
    freqs = np.concatenate([freqs, freqs], -1)
    base_scale = (half + 0.4 * DH) / (1.4 * DH)
    power = (tpos - S // 2) / (WIN // 2)
    scale = base_scale[None, :] ** power[:, None]
    scale = np.concatenate([scale, scale], -1)
    cosf, sinf = np.cos(freqs), np.sin(freqs)
    st2 = lambda a: np.ascontiguousarray(np.vstack([a.T, a.T]).astype(np.float32))
    c["CONST_TBQC"] = st2(cosf * scale)
    c["CONST_TBQS"] = st2(sinf * scale)
    c["CONST_TBKC"] = st2(cosf / scale)
    c["CONST_TBKS"] = st2(sinf / scale)
    Rh = np.zeros((128, 128), np.float32)
    for blk in range(2):
        for m in range(64):
            if m < 32:
                Rh[blk * 64 + m, blk * 64 + m + 32] = -1.0
            else:
                Rh[blk * 64 + m, blk * 64 + m - 32] = 1.0
    c["CONST_RHT"] = np.ascontiguousarray(Rh.T)
    e2 = np.zeros((128, 2), np.float32)
    e2[0:64, 0] = 1.0
    e2[64:128, 1] = 1.0
    c["CONST_EHD2"] = e2
    c["CONST_E2T"] = np.ascontiguousarray(e2.T)
    return c


def make_std(ctx, n_w_tiles=8, w_cols=1024, psum=True, xs_cols=S):
    sb, ps = ctx.sb, ctx.ps
    for i in range(NCH):
        sb(f"XS{i}", [128, xs_cols], F32)
        sb(f"XN{i}", [128, S], BF16)
    sb("ONESb", [128, 128], BF16)
    sb("U0", [128, TT], BF16)
    sb("U1", [128, TT], BF16)
    sb("ONESf", [128, 128], F32)
    sb("IDENTb", [128, 128], BF16)
    sb("IDENT2b", [128, 64], BF16)

    for i in range(4):
        sb(f"SQ{i}", [128, TT], BF16)
    sb("CONSTS", [128, 8], F32)
    sb("AROW", [1, TT], F32)
    sb("BROW0", [1, TT], F32)
    sb("BROW1", [1, TT], F32)
    sb("MU", [1, TT], F32)
    sb("VAR", [1, TT], F32)
    for i in range(n_w_tiles):
        sb(f"W{i}", [128, w_cols], BF16)
    if psum:
        for i in range(8):
            ps(f"P{i}", [128, 512], F32)


CONST_COLS = {"eps": 0, "negC": 1, "zero": 2, "negBIGC": 4, "negCfull": 5}


def load_constants(ctx, streams):
    p = ctx.p
    nc = ctx.nc
    cc = ctx.t["CONSTS"]
    p.op("dve", lambda v: v.memset(cc[:, 0:1], EPS), writes=("CONSTS",))
    p.op("dve", lambda v: v.memset(cc[:, 1:2], -C_FULL), writes=("CONSTS",))
    p.op("dve", lambda v: v.memset(cc[:, 2:3], 0.0), writes=("CONSTS",))
    p.op("dve", lambda v: v.memset(cc[:, 4:5], -(OH_SCALE * OH_SCALE + C_FULL)), writes=("CONSTS",))
    p.op("dve", lambda v: v.memset(cc[:, 5:6], -C_FULL), writes=("CONSTS",))
    p.op("dve", lambda v: v.memset(cc[:, 6:7], 1e-24), writes=("CONSTS",))
    ones = nc.dram_tensor("CONST_ONES", [128, 128], F32, kind="ExternalInput").ap()
    ident = nc.dram_tensor("CONST_IDENT", [128, 128], F32, kind="ExternalInput").ap()
    t = ctx.t
    p.op("pool", lambda g: g.dma_start(out=t["ONESb"][:], in_=ones),
         reads=("c:ONES",), writes=("ONESb",), stream=streams["wg"])
    p.op("sp", lambda g: g.dma_start(out=t["ONESf"][:], in_=ones),
         reads=("c:ONES",), writes=("ONESf",), stream=streams["setup"])
    p.op("pool", lambda g: g.dma_start(out=t["IDENTb"][:], in_=ident),
         reads=("c:IDENT",), writes=("IDENTb",), stream=streams["wg"])
    ident2 = nc.dram_tensor("CONST_IDENT2", [128, 64], F32, kind="ExternalInput").ap()
    p.op("pool", lambda g: g.dma_start(out=t["IDENT2b"][:], in_=ident2),
         reads=("c:IDENT2",), writes=("IDENT2b",), stream=streams["wg"])


def emit_load_x(ctx, x_hbm, seq, streams, xs_prefix="XS", xs_off=None):
    p, t = ctx.p, ctx.t
    if xs_off is None:
        xs_off = ctx.xs_off
    for c in range(NCH):
        dst = t[f"{xs_prefix}{c}"]
        src = x_hbm[seq, c * 128:(c + 1) * 128, :]
        p.op("sp", lambda g, dst=dst, src=src, xs_off=xs_off: g.dma_start(
            out=dst[:, xs_off:xs_off + S], in_=src),
            reads=(f"hbm:x{seq}",), writes=(f"{xs_prefix}{c}",), stream=streams["xs"])


def emit_store_x(ctx, x_hbm, seq, streams, xs_prefix="XS", xs_off=None):
    p, t = ctx.p, ctx.t
    if xs_off is None:
        xs_off = ctx.xs_off
    for c in range(NCH):
        src = t[f"{xs_prefix}{c}"]
        dst = x_hbm[seq, c * 128:(c + 1) * 128, :]
        p.op("sp", lambda g, dst=dst, src=src, xs_off=xs_off: g.dma_start(
            out=dst, in_=src[:, xs_off:xs_off + S]),
            reads=(f"{xs_prefix}{c}",), writes=(f"hbm:x{seq}",), stream=streams["xs"])


def emit_cast_x(ctx, xs="XS", xn="XN"):
    p, t = ctx.p, ctx.t
    off = ctx.xs_off
    for c in range(NCH):
        src, dst = t[f"{xs}{c}"], t[f"{xn}{c}"]
        eng = "pool" if c % 2 == 0 else "dve"
        p.op(eng, lambda v, src=src, dst=dst, off=off: v.tensor_copy(
            dst[:, 0:S], src[:, off:off + S]),
            reads=(f"{xs}{c}",),
            writes=tuple(f"{xn}{c}@{i}" for i in range(NTT)))


def emit_ln(ctx, xn="XN", g_col=None, b_col=None, out=None, out_f32=None,
           srow=False, tile_hook=None):
    p, t = ctx.p, ctx.t
    banks = getattr(ctx, "ln_banks", ("P6", "P7"))
    out = out or xn
    def _banks(tt):
        if len(banks) >= 4:
            return banks[2 * (tt % 2)], banks[2 * (tt % 2) + 1]
        return banks[0], banks[1]

    def stage1(tt):
        LB0, LB1 = _banks(tt)
        sl = slice(tt * TT, (tt + 1) * TT)
        def mm_s1(eng, sl=sl, LB0=LB0):
            last = None
            for c in range(NCH):
                last = eng.matmul(t[LB0][0:1, 0:TT], t["ONESb"][:, 0:1],
                                  t[f"{xn}{c}"][:, sl],
                                  start=(c == 0), stop=(c == NCH - 1))
            return last
        p.op("pe", mm_s1,
             reads=tuple(f"{xn}{c}@{tt}" for c in range(NCH)) + ("ONESb",),
             writes=(LB0,))
        for half2 in range(2):
            for cc in range(4):
                c = 4 * half2 + cc
                p.op("dve", lambda v, c=c, cc=cc, sl=sl: v.tensor_tensor(
                    t[f"SQ{cc}"][:, 0:TT], t[f"{xn}{c}"][:, sl], t[f"{xn}{c}"][:, sl],
                    op=ALU.mult),
                    reads=(f"{xn}{c}@{tt}",), writes=(f"SQ{cc}",))
            def mm_s2(eng, half2=half2, LB1=LB1):
                last = None
                for cc in range(4):
                    last = eng.matmul(t[LB1][0:1, 0:TT], t["ONESb"][:, 0:1],
                                      t[f"SQ{cc}"][:, 0:TT],
                                      start=(half2 == 0 and cc == 0),
                                      stop=(half2 == 1 and cc == 3))
                return last
            p.op("pe", mm_s2, reads=tuple(f"SQ{cc}" for cc in range(4)) + ("ONESb",),
                 writes=(LB1,))

    def stage2(tt):
        LB0, LB1 = _banks(tt)
        sl = slice(tt * TT, (tt + 1) * TT)
        p.op("dve", lambda v, LB0=LB0: v.tensor_scalar(
            t["MU"][:], t[LB0][0:1, 0:TT], 1.0 / D, None, op0=ALU.mult),
            reads=(LB0,), writes=("MU",))
        p.op("dve", lambda v: v.tensor_tensor(
            t["AROW"][:], t["MU"][:], t["MU"][:], op=ALU.mult),
            reads=("MU",), writes=("AROW",))
        p.op("dve", lambda v, LB1=LB1: v.scalar_tensor_tensor(
            t["VAR"][:], t[LB1][0:1, 0:TT], 1.0 / D, t["AROW"][:],
            op0=ALU.mult, op1=ALU.subtract),
            reads=(LB1, "AROW"), writes=("VAR",))
        p.op("act", lambda s_: s_.activation(
            t["VAR"][:], t["VAR"][:], AF.Ln, bias=t["CONSTS"][0:1, 0:1]),
            reads=("VAR", "CONSTS"), writes=("VAR",))
        p.op("act", lambda s_: s_.activation(
            t["AROW"][:], t["VAR"][:], AF.Exp, scale=-0.5),
            reads=("VAR",), writes=("AROW",))
        if srow:
            p.op("act", lambda s_: s_.activation(
                t["SROW"][:], t["VAR"][:], AF.Exp, scale=0.5),
                reads=("VAR",), writes=("SROW",))
        p.op("dve", lambda v: v.scalar_tensor_tensor(
            t["BROW0"][:, 0:TT], t["MU"][:], -1.0, t["AROW"][:],
            op0=ALU.mult, op1=ALU.mult),
            reads=("MU", "AROW"), writes=("BROW0",))
        p.op("pe", lambda eng, LB0=LB0: eng.matmul(
            t[LB0][:, 0:TT], t["ONESf"][0:1, 0:128], t["AROW"][:],
            start=True, stop=True), reads=("AROW", "ONESf"), writes=(LB0,))
        p.op("pe", lambda eng, LB1=LB1: eng.matmul(
            t[LB1][:, 0:TT], t["ONESf"][0:1, 0:128], t["BROW0"][:, 0:TT],
            start=True, stop=True), reads=("BROW0", "ONESf"), writes=(LB1,))

    def stage2b(tt):
        LB0, LB1 = _banks(tt)
        sl = slice(tt * TT, (tt + 1) * TT)
        for c in range(NCH):
            u = t["U0"] if c % 2 == 0 else t["U1"]
            uk = "U0" if c % 2 == 0 else "U1"
            p.op("dve", lambda v, c=c, u=u, sl=sl, LB0=LB0: v.tensor_tensor(
                u[:, 0:TT], t[f"{xn}{c}"][:, sl], t[LB0][:, 0:TT], op=ALU.mult),
                reads=(f"{xn}{c}@{tt}", LB0), writes=(uk,))
            if out_f32 is not None:
                osl = slice(ctx.xs_off + sl.start, ctx.xs_off + sl.stop)
                p.op("dve", lambda v, c=c, u=u, osl=osl, LB1=LB1: v.tensor_tensor(
                    t[f"{out_f32}{c}"][:, osl], u[:, 0:TT], t[LB1][:, 0:TT], op=ALU.add),
                    reads=(uk, LB1), writes=(f"{out_f32}{c}",))
            if g_col is None:
                p.op("dve", lambda v, c=c, u=u, sl=sl, LB1=LB1: v.tensor_tensor(
                    t[f"{out}{c}"][:, sl], u[:, 0:TT], t[LB1][:, 0:TT], op=ALU.add),
                    reads=(uk, LB1), writes=(f"{out}{c}@{tt}",))
            else:
                p.op("dve", lambda v, c=c, u=u, sl=sl, LB1=LB1: v.tensor_tensor(
                    u[:, 0:TT], u[:, 0:TT], t[LB1][:, 0:TT], op=ALU.add),
                    reads=(uk, LB1), writes=(uk,))
                gc, bc, preg = g_col(c), b_col(c), getattr(ctx, "ln_param_regions", ())
                p.op("dve", lambda v, c=c, u=u, gc=gc, bc=bc, sl=sl: v.tensor_scalar(
                    t[f"{out}{c}"][:, sl], u[:, 0:TT], gc, bc, op0=ALU.mult, op1=ALU.add),
                    reads=(uk,) + tuple(preg), writes=(f"{out}{c}@{tt}",))
        if tile_hook is not None:
            tile_hook(tt)

    if len(banks) >= 4:
        for k2 in range(NTT + 2):
            if k2 >= 2:
                stage2b(k2 - 2)
            if 1 <= k2 <= NTT:
                stage2(k2 - 1)
            if k2 < NTT:
                stage1(k2)
    else:
        for tt in range(NTT):
            stage1(tt)
            stage2(tt)
            stage2b(tt)


def emit_ffn_l0(ctx, w1_hbm, b1_col, w2_hbm, b2h_col, streams, scale=0.5,
                bias_regions=(), tag=""):
    p, t = ctx.p, ctx.t
    NH1 = 4096 // 128
    HW = S // 2
    for half in range(2):
        hsl = slice(half * HW, (half + 1) * HW)
        for c4 in range(NH1):
            wk = f"W{c4 % 4}"
            wt = t[wk]
            ctx.wload(wk, w1_hbm[c4], 1024)
            for tt in range(2):
                sl = slice(half * HW + tt * TT, half * HW + (tt + 1) * TT)
                pb = f"P{(2 * c4 + tt) % 4}"
                def mm(eng, wt=wt, sl=sl, pb=pb):
                    last = None
                    for k in range(NCH):
                        last = eng.matmul(t[pb][:, 0:TT], wt[:, k * 128:(k + 1) * 128],
                                          t[f"XN{k}"][:, sl],
                                          start=(k == 0), stop=(k == NCH - 1))
                    return last
                p.op("pe", mm,
                     reads=tuple(f"XN{k}@{sl.start // TT}" for k in range(NCH)) + (wk,),
                     writes=(pb,))
                bc = b1_col(c4)
                p.op("act", lambda s_, c4=c4, tt=tt, pb=pb, bc=bc: s_.activation(
                    t[f"H{c4}"][:, tt * TT:(tt + 1) * TT], t[pb][:, 0:TT],
                    AF.Gelu, bias=bc),
                    reads=(pb,) + bias_regions, writes=(f"H{c4}",))
        for c in range(NCH):
            for hw2 in range(2):
                ctx.wload(f"W{4 + hw2}", w2_hbm[c][:, hw2 * 2048:(hw2 + 1) * 2048], 2048)
            for tt in range(2):
                sl = slice(half * HW + tt * TT, half * HW + (tt + 1) * TT)
                pb = f"P{(2 * c + tt) % 2 + 4}" if False else f"P{2 + tt % 2}"
                def mm(eng, tt=tt, pb=pb):
                    last = None
                    for k in range(NH1):
                        wt2 = t[f"W{4 + k // 16}"]
                        last = eng.matmul(t[pb][:, 0:TT],
                                          wt2[:, (k % 16) * 128:(k % 16 + 1) * 128],
                                          t[f"H{k}"][:, tt * TT:(tt + 1) * TT],
                                          start=(k == 0), stop=(k == NH1 - 1))
                    return last
                p.op("pe", mm, reads=tuple(f"H{k}" for k in range(NH1)) + ("W4", "W5"),
                     writes=(pb,))
                bc = b2h_col(c)
                u = t["U0"] if tt % 2 == 0 else t["U1"]
                uk = "U0" if tt % 2 == 0 else "U1"
                p.op("act", lambda s_, pb=pb, u=u, bc=bc: s_.activation(
                    u[:, 0:TT], t[pb][:, 0:TT], AF.Identity, bias=bc, scale=scale),
                    reads=(pb,) + bias_regions, writes=(uk,))
                xsl = slice(ctx.xs_off + sl.start, ctx.xs_off + sl.stop)
                p.op("pool", lambda g, c=c, u=u, xsl=xsl: g.tensor_tensor(
                    t[f"XS{c}"][:, xsl], t[f"XS{c}"][:, xsl], u[:, 0:TT], op=ALU.add),
                    reads=(f"XS{c}", uk), writes=(f"XS{c}",))




def emit_qr_onehot(ctx, tile, streams):
    p, t = ctx.p, ctx.t
    xo = ctx.xs_off
    for sub in range(4):
        ts0 = tile * TT + sub * 128
        for side, pcol in (("Q", 0), ("K", 128)):
            def mm(eng, ts0=ts0, side=side, pcol=pcol, sub=sub):
                last = None
                for c in range(NCH):
                    last = eng.matmul(
                        t["PS1"][:, pcol:pcol + 128],
                        t[f"XS{c}"][:, xo + ts0:xo + ts0 + 128],
                        t[f"WR{side}"][:, c * 128:(c + 1) * 128],
                        start=(c == 0), stop=False)
                eng.matmul(t["PS1"][:, pcol:pcol + 128],
                           t["MU"][0:1, sub * 128:sub * 128 + 128],
                           t[f"NCS{side}"][0:1, 0:128], start=False, stop=False)
                last = eng.matmul(t["PS1"][:, pcol:pcol + 128],
                                  t["SROW"][0:1, sub * 128:sub * 128 + 128],
                                  t[f"B{side}R"][0:1, 0:128], start=False, stop=True)
                return last
            p.op("pe", mm,
                 reads=tuple(f"XS{c}" for c in range(NCH)) +
                       (f"WR{side}", "MU", "SROW", f"NCS{side}", f"B{side}R"),
                 writes=("PS1",))
            p.op("dve", lambda v, pcol=pcol: v.tensor_reduce(
                t["M16"][:, 0:16],
                bass.AP(t["PS1"], pcol, [[512, 128], [8, 16], [1, 8]]),
                axis=AX.X, op=ALU.max),
                reads=("PS1",), writes=("M16",))
            p.op("dve", lambda v, pcol=pcol: v.tensor_tensor(
                t["OHSB"][:, 0:128],
                bass.AP(t["PS1"], pcol, [[512, 128], [8, 16], [1, 8]]),
                bass.AP(t["M16"], 0, [[16, 128], [1, 16], [0, 8]]),
                op=ALU.is_ge),
                reads=("PS1", "M16"), writes=("OHSB",))
            p.op("dve", lambda v: v.tensor_scalar(
                t["OHSB"][:, 0:128], t["OHSB"][:, 0:128], OH_SCALE, None, op0=ALU.mult),
                reads=("OHSB",), writes=("OHSB",))
            p.op("sp", lambda g, ts0=ts0, side=side: g.dma_start(
                out=t[f"TRANS{side}"][:, ts0:ts0 + 128], in_=t["OHSB"][:, 0:128],
                transpose=True),
                reads=("OHSB",), writes=(f"TRANS{side}",), stream=streams["tr"])


def emit_full_attn(ctx, streams, *, bucket, causal, wq_hbm, wk_hbm, wv_hbm,
                   wo_hbm, bq8_col=None, bk_col=None, bv_col=None, bo_col=None,
                   param_regions=(), tag=""):
    p, t = ctx.p, ctx.t
    KAUG = 72 if bucket else 64
    eshift = 4 if bucket else 5
    nsp = 0
    for ch in range(NCH):
        for wi, (w_hbm, dst, key) in enumerate(
                ((wq_hbm, "QP", "q"), (wk_hbm, "KP", "k"), (wv_hbm, "VP", "v"))):
            wk_ = f"W{2 * wi + ch % 2}"
            wt = t[wk_]
            ctx.wload(wk_, w_hbm[ch], 1024)
            for tt in range(NTT):
                sl = slice(tt * TT, (tt + 1) * TT)
                pb = f"PS{(wi * NTT + tt) % 3}"
                def mm(eng, wt=wt, sl=sl, pb=pb):
                    last = None
                    for k in range(NCH):
                        last = eng.matmul(t[pb][:, 0:TT], wt[:, k * 128:(k + 1) * 128],
                                          t[f"XN{k}"][:, sl],
                                          start=(k == 0), stop=(k == NCH - 1))
                    return last
                p.op("pe", mm,
                     reads=tuple(f"XN{k}@{sl.start // TT}" for k in range(NCH)) + (wk_,),
                     writes=(pb,))
                if key == "q":
                    bc = bq8_col(ch) if bq8_col is not None else None
                    if bc is not None:
                        p.op("act", lambda s_, pb=pb, sl=sl, bc=bc: s_.activation(
                            t["QP"][:, sl], t[pb][:, 0:TT], AF.Identity,
                            bias=bc, scale=0.125),
                            reads=(pb,) + param_regions, writes=("QP",))
                    else:
                        p.op("act", lambda s_, pb=pb, sl=sl: s_.activation(
                            t["QP"][:, sl], t[pb][:, 0:TT], AF.Identity,
                            bias=0.0, scale=0.125),
                            reads=(pb,), writes=("QP",))
                else:
                    dst_ = "KP" if key == "k" else "VP"
                    bc = bk_col(ch) if (key == "k" and bk_col) else (
                        bv_col(ch) if (key == "v" and bv_col) else None)
                    if bc is not None:
                        p.op("act", lambda s_, pb=pb, sl=sl, bc=bc, dst_=dst_:
                             s_.activation(t[dst_][:, sl], t[pb][:, 0:TT],
                                           AF.Identity, bias=bc),
                             reads=(pb,) + param_regions, writes=(dst_,))
                    else:
                        p.op("act", lambda s_, pb=pb, sl=sl, dst_=dst_: s_.copy(
                            t[dst_][:, sl], t[pb][:, 0:TT]),
                            reads=(pb,), writes=(dst_,))
        for hh in range(2):
            h = 2 * ch + hh
            b = 64 * hh
            vt = "VTM0"
            for j0 in range(4):
                def trs(eng, b=b, j0=j0):
                    last = None
                    for jj in range(4):
                        j = 4 * j0 + jj
                        last = eng.transpose(t["PTB"][:, jj * 64:(jj + 1) * 64],
                                             t["VP"][b:b + 64, j * 128:(j + 1) * 128],
                                             t["IDENT2b"][b:b + 64, 0:64])
                    return last
                p.op("pe", trs, reads=("VP", "IDENT2b"), writes=("PTB",))
                p.op("dve", lambda v, vt=vt, j0=j0: v.tensor_copy(
                    bass.AP(t[vt], 4 * j0 * 65, [[1040, 128], [65, 4], [1, 64]]),
                    bass.AP(t["PTB"], 0, [[1024, 128], [64, 4], [1, 64]])),
                    reads=("PTB",), writes=(vt,))
            if bucket:
                p.op("sp", lambda g, b=b: g.dma_start(
                    out=t["AUGQ"][0:64, :], in_=t["QP"][b:b + 64, :]),
                    reads=("QP",), writes=("AUGQ",), stream=streams["tr"])
                p.op("sp", lambda g, b=b: g.dma_start(
                    out=t["AUGK"][0:64, :], in_=t["KP"][b:b + 64, :]),
                    reads=("KP",), writes=("AUGK",), stream=streams["tr"])
                p.op("sp", lambda g, h=h: g.dma_start(
                    out=t["AUGQ"][64:72, :], in_=t["TRANSQ"][8 * h:8 * h + 8, :]),
                    reads=("TRANSQ",), writes=("AUGQ",), stream=streams["tr"])
                p.op("sp", lambda g, h=h: g.dma_start(
                    out=t["AUGK"][64:72, :], in_=t["TRANSK"][8 * h:8 * h + 8, :]),
                    reads=("TRANSK",), writes=("AUGK",), stream=streams["tr"])
            iters = [(j, qb) for j in range(16) for qb in range(4)
                     if not (causal and j > 4 * qb + 3)]

            def emit_s(idx):
                j, qb = iters[idx]
                qsl = slice(qb * TT, (qb + 1) * TT)
                pb = f"PS{idx % 3}"
                if bucket:
                    p.op("pe", lambda eng, j=j, qsl=qsl, pb=pb: eng.matmul(
                        t[pb][:, 0:TT], t["AUGK"][0:KAUG, j * 128:(j + 1) * 128],
                        t["AUGQ"][0:KAUG, qsl], start=True, stop=True),
                        reads=("AUGK", "AUGQ"), writes=(pb,))
                else:
                    p.op("pe", lambda eng, j=j, qsl=qsl, pb=pb, b=b: eng.matmul(
                        t[pb][:, 0:TT], t["KP"][b:b + 64, j * 128:(j + 1) * 128],
                        t["QP"][b:b + 64, qsl], start=True, stop=True),
                        reads=("KP", "QP"), writes=(pb,))

            def emit_rest(idx, nsp):
                j, qb = iters[idx]
                pb = f"PS{idx % 3}"
                sp = f"SQ{nsp % 3}"
                p.op("act", lambda s_, pb=pb, sp=sp: s_.activation(
                    t[sp][:, 0:TT], t[pb][:, 0:TT], AF.Exp,
                    bias=t["CONSTS"][:, eshift:eshift + 1]),
                    reads=(pb, "CONSTS"), writes=(sp,))
                if causal and j >= 4 * qb:
                    c0 = j * 128 - qb * TT
                    if c0 > 0:
                        p.op("pool", lambda g, sp=sp, c0=c0: g.memset(
                            t[sp][:, 0:c0], 0.0), reads=(), writes=(sp,))
                    p.op("pool", lambda g, sp=sp, c0=c0: g.tensor_tensor(
                        t[sp][:, c0:c0 + 128], t[sp][:, c0:c0 + 128],
                        t["TRIUb"][:, 0:128], op=ALU.mult),
                        reads=("TRIUb", sp), writes=(sp,))
                jlast = (4 * qb + 3) if causal else 15
                p.op("pe", lambda eng, vt=vt, j=j, qb=qb, sp=sp,
                     jlast=jlast: eng.matmul(
                         t["PZ"][:, qb * TT:(qb + 1) * TT],
                         t[vt][:, j * 65:j * 65 + 65], t[sp][:, 0:TT],
                         start=(j == 0), stop=(j == jlast)),
                     reads=(vt, sp), writes=("PZ",))

            for idx in range(len(iters) + 2):
                if idx < len(iters):
                    emit_s(idx)
                if idx >= 2:
                    emit_rest(idx - 2, nsp)
                    nsp += 1
            for qb in range(4):
                qsl = slice(qb * TT, (qb + 1) * TT)
                br = f"BROW{qb % 2}"
                pbq = f"PS{qb % 2}"
                p.op("act", lambda s_, qsl=qsl, br=br: s_.activation(
                    t[br][:, 0:TT], t["PZ"][64:65, qsl], AF.Ln),
                    reads=("PZ",), writes=(br,))
                p.op("act", lambda s_, br=br: s_.activation(
                    t[br][:, 0:TT], t[br][:, 0:TT], AF.Exp, scale=-1.0),
                    reads=(br,), writes=(br,))
                p.op("pe", lambda eng, br=br, pbq=pbq: eng.matmul(
                    t[pbq][0:64, 0:TT], t["ONESf"][0:1, 0:64],
                    t[br][0:1, 0:TT], start=True, stop=True),
                    reads=(br, "ONESf"), writes=(pbq,))
                sp = f"SQ{nsp % 3}"
                nsp += 1
                p.op("dve", lambda v, sp=sp, pbq=pbq: v.tensor_copy(
                    t[sp][0:64, 0:TT], t[pbq][0:64, 0:TT]),
                    reads=(pbq,), writes=(sp,))
                p.op("dve", lambda v, ch=ch, b=b, qsl=qsl, sp=sp:
                     v.tensor_tensor(
                         t[f"ZB{ch}"][b:b + 64, qsl], t["PZ"][0:64, qsl],
                         t[sp][0:64, 0:TT], op=ALU.mult),
                     reads=("PZ", sp), writes=(f"ZB{ch}",))
    for c in range(NCH):
        wk_ = f"W{c % 2}"
        wt = t[wk_]
        ctx.wload(wk_, wo_hbm[c], 1024)
        for tt in range(NTT):
            sl = slice(tt * TT, (tt + 1) * TT)
            pb = f"PS{tt % 2}"
            def mm(eng, wt=wt, sl=sl, pb=pb):
                last = None
                for k in range(NCH):
                    last = eng.matmul(t[pb][:, 0:TT], wt[:, k * 128:(k + 1) * 128],
                                      t[f"ZB{k}"][:, sl],
                                      start=(k == 0), stop=(k == NCH - 1))
                return last
            p.op("pe", mm, reads=tuple(f"ZB{k}" for k in range(NCH)) + (wk_,),
                 writes=(pb,))
            xsl = slice(ctx.xs_off + sl.start, ctx.xs_off + sl.stop)
            if bo_col is not None:
                bc = bo_col(c)
                p.op("dve", lambda v, pb=pb, bc=bc, c=c, xsl=xsl:
                     v.scalar_tensor_tensor(
                         t[f"XS{c}"][:, xsl], t[pb][:, 0:TT], bc,
                         t[f"XS{c}"][:, xsl], op0=ALU.add, op1=ALU.add),
                     reads=(pb, f"XS{c}") + param_regions, writes=(f"XS{c}",))
            else:
                p.op("dve", lambda v, pb=pb, c=c, xsl=xsl: v.tensor_tensor(
                    t[f"XS{c}"][:, xsl], t[f"XS{c}"][:, xsl], t[pb][:, 0:TT],
                    op=ALU.add),
                    reads=(f"XS{c}", pb), writes=(f"XS{c}",))


def build_l0ffn(nseq=2):
    nc = bass.Bass("TRN2", target_bir_lowering=False)
    x_hbm = nc.dram_tensor("x", [nseq, D, S], F32, kind="ExternalInput").ap()
    y_hbm = nc.dram_tensor("y", [nseq, D, S], F32, kind="ExternalOutput").ap()
    w1 = nc.dram_tensor("w1", [32, 128, D], F32, kind="ExternalInput").ap()
    b1T = nc.dram_tensor("b1T", [128, 4096 // 128], F32, kind="ExternalInput").ap()
    w2 = nc.dram_tensor("w2", [NCH, 128, 4096], F32, kind="ExternalInput").ap()
    b2T = nc.dram_tensor("b2T", [128, NCH], F32, kind="ExternalInput").ap()

    es = ExitStack()
    with es:
        prog = Prog(nc, es)
        ctx = Ctx(nc, es, prog)
        streams = {"xs": prog.stream("d_xs", "sp"),
                   "setup": prog.stream("d_setup", "sp"),
                   "wg": prog.stream("d_wg", "pool")}
        for i in range(6):
            prog.stream(f"d_W{i}", "sp")
            prog.stream(f"dp_W{i}", "pool")
        make_std(ctx, n_w_tiles=0, w_cols=0)
        for i in range(4):
            ctx.sb(f"W{i}", [128, 1024], BF16)
        for i in range(4, 6):
            ctx.sb(f"W{i}", [128, 2048], BF16)
        for i in range(32):
            ctx.sb(f"H{i}", [128, 1024], BF16)
        bt = ctx.sb("B1T", [128, 4096 // 128], F32)
        prog.op("sp", lambda g: g.dma_start(out=bt[:], in_=b1T),
                reads=("hbm:b1",), writes=("B1T",), stream=streams["setup"])
        b2t = ctx.sb("B2T", [128, NCH], F32)
        prog.op("sp", lambda g: g.dma_start(out=b2t[:], in_=b2T),
                reads=("hbm:b2",), writes=("B2T",), stream=streams["setup"])
        b2h = ctx.sb("B2H", [128, NCH], F32)
        load_constants(ctx, streams)
        prog.freeze_stream("d_setup")
        prog.op("dve", lambda v: v.tensor_scalar(
            b2h[:], b2t[:], 0.5, None, op0=ALU.mult),
            reads=("B2T",), writes=("B2H",))

        for seq in range(nseq):
            emit_load_x(ctx, x_hbm, seq, streams)
            emit_cast_x(ctx)
            emit_ln(ctx)
            emit_ffn_l0(ctx, w1, lambda c4: bt[:, c4:c4 + 1], w2,
                        lambda c: b2h[:, c:c + 1], streams,
                        bias_regions=("B1T", "B2H"))
            emit_store_x(ctx, y_hbm, seq, streams)
        prog.wait_region("sp", tuple(f"hbm:x{s_}" for s_ in range(nseq)))
        prog.finalize()
    return nc




def build_l0attn(nseq=2):
    nc = bass.Bass("TRN2", target_bir_lowering=False)
    x_hbm = nc.dram_tensor("x", [nseq, D, S], F32, kind="ExternalInput").ap()
    y_hbm = nc.dram_tensor("y", [nseq, D, S], F32, kind="ExternalOutput").ap()
    wq = nc.dram_tensor("wq", [NCH, 128, D], F32, kind="ExternalInput").ap()
    wk = nc.dram_tensor("wk", [NCH, 128, D], F32, kind="ExternalInput").ap()
    wv = nc.dram_tensor("wv", [NCH, 128, D], F32, kind="ExternalInput").ap()
    wo = nc.dram_tensor("wo", [NCH, 128, D], F32, kind="ExternalInput").ap()
    wqT = nc.dram_tensor("wqT", [NCH, 128, D], F32, kind="ExternalInput").ap()
    wkT = nc.dram_tensor("wkT", [NCH, 128, D], F32, kind="ExternalInput").ap()
    rexp = nc.dram_tensor("rexp", [128, 1024], F32, kind="ExternalInput").ap()
    bqT = nc.dram_tensor("bqT", [128, NCH], F32, kind="ExternalInput").ap()
    bkT = nc.dram_tensor("bkT", [128, NCH], F32, kind="ExternalInput").ap()
    bvT = nc.dram_tensor("bvT", [128, NCH], F32, kind="ExternalInput").ap()
    boT = nc.dram_tensor("boT", [128, NCH], F32, kind="ExternalInput").ap()

    es = ExitStack()
    with es:
        prog = Prog(nc, es)
        ctx = Ctx(nc, es, prog)
        streams = {"xs": prog.stream("d_xs", "sp"),
                   "setup": prog.stream("d_setup", "sp"),
                   "tr": prog.stream("d_tr", "sp"),
                   "wg": prog.stream("d_wg", "pool")}
        for i in range(6):
            prog.stream(f"d_W{i}", "sp")
            prog.stream(f"dp_W{i}", "pool")
        make_std(ctx, n_w_tiles=6, w_cols=1024, psum=False)
        sb, ps = ctx.sb, ctx.ps
        sb("SROW", [1, TT], F32)
        sb("WRQ", [128, 1024], F32)
        sb("WRK", [128, 1024], F32)
        sb("NCSQ", [1, 128], F32)
        sb("NCSK", [1, 128], F32)
        sb("BQR", [1, 128], F32)
        sb("BKR", [1, 128], F32)
        sb("M16", [128, 16], F32)
        sb("OHSB", [128, 128], BF16)
        sb("TRANSQ", [128, S], BF16)
        sb("TRANSK", [128, S], BF16)
        sb("QP", [128, S], BF16)
        sb("KP", [128, S], BF16)
        sb("VP", [128, S], BF16)
        sb("AUGQ", [72, S], BF16)
        sb("AUGK", [72, S], BF16)
        sb("VTM0", [128, 16 * 65], BF16)
        sb("WTMP", [128, 1024], F32)
        for ch in range(NCH):
            sb(f"ZB{ch}", [128, S], BF16)
        for nm in ("BQ8", "BKC", "BVC", "BOC"):
            sb(nm, [128, NCH], F32)
        sb("REXP", [128, 1024], F32)
        ps("PS0", [128, 512], F32)
        ps("PS1", [128, 512], F32)
        ps("PS2", [128, 512], F32)
        ps("PZ", [65, 2048], F32)
        ps("PTB", [128, 1024], BF16)

        t = ctx.t
        prog.op("sp", lambda g: g.dma_start(out=t["REXP"][:], in_=rexp),
                reads=("hbm:rexp",), writes=("REXP",), stream=streams["setup"])
        for nm, src in (("BQ8", bqT), ("BKC", bkT), ("BVC", bvT), ("BOC", boT)):
            prog.op("sp", lambda g, nm=nm, src=src: g.dma_start(out=t[nm][:], in_=src),
                    reads=(f"hbm:{nm}",), writes=(nm,), stream=streams["setup"])
        load_constants(ctx, streams)
        prog.freeze_stream("d_setup")
        prog.op("dve", lambda v: v.tensor_scalar(
            t["BQ8"][:], t["BQ8"][:], 0.125, None, op0=ALU.mult),
            reads=("BQ8",), writes=("BQ8",))
        ctx.ln_banks = ("PS0", "PS1")
        emit_wr_setup_inline(ctx, wqT, wkT, streams)

        for seq in range(nseq):
            emit_load_x(ctx, x_hbm, seq, streams)
            emit_cast_x(ctx)
            prog.op("dve", lambda v: v.memset(
                bass.AP(ctx.t["VTM0"], 64, [[1040, 128], [65, 16], [1, 1]]), 1.0),
                writes=("VTM0",))
            emit_ln(ctx, srow=True,
                    tile_hook=lambda tt: emit_qr_onehot(ctx, tt, streams))
            emit_full_attn(ctx, streams, bucket=True, causal=False,
                           wq_hbm=wq, wk_hbm=wk, wv_hbm=wv, wo_hbm=wo,
                           bq8_col=lambda c: t["BQ8"][:, c:c + 1],
                           bk_col=lambda c: t["BKC"][:, c:c + 1],
                           bv_col=lambda c: t["BVC"][:, c:c + 1],
                           bo_col=lambda c: t["BOC"][:, c:c + 1],
                           param_regions=("BQ8", "BKC", "BVC", "BOC"), tag="l0a")
            emit_store_x(ctx, y_hbm, seq, streams)
        prog.wait_region("sp", tuple(f"hbm:x{s_}" for s_ in range(nseq)))
        prog.finalize()
    return nc


def emit_wr_setup_inline(ctx, wqT, wkT, streams):
    p, t = ctx.p, ctx.t
    for side, wT, bTk in (("Q", wqT, "BQ8"), ("K", wkT, "BKC")):
        wr = t[f"WR{side}"]
        for m in range(NCH):
            p.op("sp", lambda g, src=wT[m]: g.dma_start(
                out=t["WTMP"][:, 0:1024], in_=src),
                reads=(f"hbm:w{side}T",), writes=("WTMP",), stream=streams["tr"])
            def mm(eng):
                last = None
                for k in range(NCH):
                    last = eng.matmul(t["PS0"][:, 0:128],
                                      t["WTMP"][:, k * 128:(k + 1) * 128],
                                      t["REXP"][:, k * 128:(k + 1) * 128],
                                      start=(k == 0), stop=(k == NCH - 1))
                return last
            p.op("pe", mm, reads=("WTMP", "REXP"), writes=("PS0",))
            p.op("dve", lambda v, m=m, wr=wr, side=side: v.tensor_copy(
                wr[:, m * 128:(m + 1) * 128], t["PS0"][:, 0:128]),
                reads=("PS0",), writes=(f"WR{side}",))
        def mmc(eng, wr=wr):
            last = None
            for k in range(NCH):
                last = eng.matmul(t["PS1"][0:1, 0:128], t["ONESf"][:, 0:1],
                                  wr[:, k * 128:(k + 1) * 128],
                                  start=(k == 0), stop=(k == NCH - 1))
            return last
        p.op("pe", mmc, reads=(f"WR{side}", "ONESf"), writes=("PS1",))
        p.op("dve", lambda v, side=side: v.tensor_scalar(
            t[f"NCS{side}"][:, 0:128], t["PS1"][0:1, 0:128], -1.0, None,
            op0=ALU.mult),
            reads=("PS1",), writes=(f"NCS{side}",))
        bf = 8.0 if side == "Q" else 1.0
        def mmb(eng, bTk=bTk, bf=bf):
            last = None
            for k in range(NCH):
                last = eng.matmul(t["PS1"][0:1, 128:256], t[bTk][:, k:k + 1],
                                  t["REXP"][:, k * 128:(k + 1) * 128],
                                  start=(k == 0), stop=(k == NCH - 1))
            return last
        p.op("pe", mmb, reads=(bTk, "REXP"), writes=("PS1",))
        p.op("dve", lambda v, side=side, bf=bf: v.tensor_scalar(
            t[f"B{side}R"][:, 0:128], t["PS1"][0:1, 128:256], bf, None,
            op0=ALU.mult),
            reads=("PS1",), writes=(f"B{side}R",))


def build_l1attn(nseq=2):
    nc = bass.Bass("TRN2", target_bir_lowering=False)
    x_hbm = nc.dram_tensor("x", [nseq, D, S], F32, kind="ExternalInput").ap()
    y_hbm = nc.dram_tensor("y", [nseq, D, S], F32, kind="ExternalOutput").ap()
    wq = nc.dram_tensor("wq", [NCH, 128, D], F32, kind="ExternalInput").ap()
    wk = nc.dram_tensor("wk", [NCH, 128, D], F32, kind="ExternalInput").ap()
    wv = nc.dram_tensor("wv", [NCH, 128, D], F32, kind="ExternalInput").ap()
    wo = nc.dram_tensor("wo", [NCH, 128, D], F32, kind="ExternalInput").ap()
    triu = nc.dram_tensor("CONST_TRIU", [128, 128], F32, kind="ExternalInput").ap()

    es = ExitStack()
    with es:
        prog = Prog(nc, es)
        ctx = Ctx(nc, es, prog)
        streams = {"xs": prog.stream("d_xs", "sp"),
                   "setup": prog.stream("d_setup", "sp"),
                   "tr": prog.stream("d_tr", "sp"),
                   "wg": prog.stream("d_wg", "pool")}
        for i in range(6):
            prog.stream(f"d_W{i}", "sp")
            prog.stream(f"dp_W{i}", "pool")
        make_std(ctx, n_w_tiles=6, w_cols=1024, psum=False)
        sb, ps = ctx.sb, ctx.ps
        sb("QP", [128, S], BF16)
        sb("KP", [128, S], BF16)
        sb("VP", [128, S], BF16)
        sb("AUGQ", [72, S], BF16)
        sb("AUGK", [72, S], BF16)
        sb("VTM0", [128, 16 * 65], BF16)
        sb("TRIUb", [128, 128], BF16)
        for ch in range(NCH):
            sb(f"ZB{ch}", [128, S], BF16)
        ps("PS0", [128, 512], F32)
        ps("PS1", [128, 512], F32)
        ps("PS2", [128, 512], F32)
        ps("PZ", [65, 2048], F32)
        ps("PTB", [128, 1024], BF16)
        ctx.ln_banks = ("PS0", "PS1")

        t = ctx.t
        prog.op("pool", lambda g: g.dma_start(out=t["TRIUb"][:], in_=triu),
                reads=("c:TRIU",), writes=("TRIUb",), stream=streams["wg"])
        load_constants(ctx, streams)
        prog.freeze_stream("d_setup")

        for seq in range(nseq):
            emit_load_x(ctx, x_hbm, seq, streams)
            emit_cast_x(ctx)
            prog.op("dve", lambda v: v.memset(
                bass.AP(ctx.t["VTM0"], 64, [[1040, 128], [65, 16], [1, 1]]), 1.0),
                writes=("VTM0",))
            emit_ln(ctx)
            emit_full_attn(ctx, streams, bucket=False, causal=True,
                           wq_hbm=wq, wk_hbm=wk, wv_hbm=wv, wo_hbm=wo, tag="l1a")
            emit_store_x(ctx, y_hbm, seq, streams)
        prog.wait_region("sp", tuple(f"hbm:x{s_}" for s_ in range(nseq)))
        prog.finalize()
    return nc


def emit_cast_shift(ctx, off=1):
    p, t = ctx.p, ctx.t
    for c in range(NCH):
        so = off if c < 4 else off - 1
        eng = "pool" if c % 2 == 0 else "dve"
        p.op(eng, lambda v, c=c, so=so: v.tensor_copy(
            t[f"XN{c}"][:, 0:S], t[f"XS{c}"][:, so:so + S]),
            reads=(f"XS{c}",),
            writes=tuple(f"XN{c}@{i}" for i in range(NTT)))


def emit_ffn_gated(ctx, w1_hbm, b1_col, w2_hbm, b2_col, streams, lng_col,
                   lnb_col, param_regions=(), tag=""):
    p, t = ctx.p, ctx.t
    NA = 22
    HW = S // 2
    for half in range(2):
        for c4 in range(NA):
            for part, oc in (("g", 22 + c4), ("a", c4)):
                wk_ = f"W{(2 * c4 + (0 if part == "g" else 1)) % 4}"
                wt = t[wk_]
                ctx.wload(wk_, w1_hbm[oc], 1024)
                for tt in range(2):
                    sl = slice(half * HW + tt * TT, half * HW + (tt + 1) * TT)
                    pb = f"P{tt % 2}" if part == "g" else f"P{2 + tt % 2}"
                    def mm(eng, wt=wt, sl=sl, pb=pb):
                        last = None
                        for k in range(NCH):
                            last = eng.matmul(t[pb][:, 0:TT],
                                              wt[:, k * 128:(k + 1) * 128],
                                              t[f"XN{k}"][:, sl],
                                              start=(k == 0), stop=(k == NCH - 1))
                        return last
                    p.op("pe", mm,
                         reads=tuple(f"XN{k}@{sl.start // TT}" for k in range(NCH)) + (wk_,),
                         writes=(pb,))
                    bc = b1_col(oc)
                    if part == "g":
                        sq = f"SQ{tt % 2}"
                        p.op("act", lambda s_, pb=pb, bc=bc, sq=sq: s_.activation(
                            t[sq][:, 0:TT], t[pb][:, 0:TT], AF.Gelu, bias=bc),
                            reads=(pb,) + param_regions, writes=(sq,))
                    else:
                        sq = f"SQ{tt % 2}"
                        p.op("dve", lambda v, pb=pb, bc=bc, sq=sq, c4=c4, tt=tt:
                             v.scalar_tensor_tensor(
                                 t[f"H{c4}"][:, tt * TT:(tt + 1) * TT],
                                 t[pb][:, 0:TT], bc, t[sq][:, 0:TT],
                                 op0=ALU.add, op1=ALU.mult),
                             reads=(pb, sq) + param_regions, writes=(f"H{c4}",))
        for c in range(NCH):
            wk_ = f"W{4 + c % 2}"
            wt = t[wk_]
            ctx.wload(wk_, w2_hbm[c], 2816)
            for tt in range(2):
                sl = slice(half * HW + tt * TT, half * HW + (tt + 1) * TT)
                xsl = slice(ctx.xs_off + half * HW + tt * TT,
                            ctx.xs_off + half * HW + (tt + 1) * TT)
                pb = f"P{2 + tt % 2}"
                def mm(eng, wt=wt, tt=tt, pb=pb):
                    last = None
                    for k in range(NA):
                        last = eng.matmul(t[pb][:, 0:TT], wt[:, k * 128:(k + 1) * 128],
                                          t[f"H{k}"][:, tt * TT:(tt + 1) * TT],
                                          start=(k == 0), stop=(k == NA - 1))
                    return last
                p.op("pe", mm, reads=tuple(f"H{k}" for k in range(NA)) + (wk_,),
                     writes=(pb,))
                bc = b2_col(c)
                p.op("dve", lambda v, pb=pb, bc=bc, c=c, xsl=xsl:
                     v.scalar_tensor_tensor(
                         t[f"XS{c}"][:, xsl], t[pb][:, 0:TT], bc,
                         t[f"XS{c}"][:, xsl], op0=ALU.add, op1=ALU.add),
                     reads=(pb, f"XS{c}") + param_regions, writes=(f"XS{c}",))


def build_l1ffn_fsub(nseq=2):
    nc = bass.Bass("TRN2", target_bir_lowering=False)
    x_hbm = nc.dram_tensor("x", [nseq, D, S], F32, kind="ExternalInput").ap()
    y_hbm = nc.dram_tensor("y", [nseq, D, S], F32, kind="ExternalOutput").ap()
    w1 = nc.dram_tensor("w1", [44, 128, D], F32, kind="ExternalInput").ap()
    w2 = nc.dram_tensor("w2", [NCH, 128, 2816], F32, kind="ExternalInput").ap()
    b1T = nc.dram_tensor("b1T", [128, 44], F32, kind="ExternalInput").ap()
    b2T = nc.dram_tensor("b2T", [128, NCH], F32, kind="ExternalInput").ap()
    lngT = nc.dram_tensor("lngT", [128, NCH], F32, kind="ExternalInput").ap()
    lnbT = nc.dram_tensor("lnbT", [128, NCH], F32, kind="ExternalInput").ap()

    es = ExitStack()
    with es:
        prog = Prog(nc, es)
        ctx = Ctx(nc, es, prog)
        streams = {"xs": prog.stream("d_xs", "sp"),
                   "setup": prog.stream("d_setup", "sp"),
                   "wg": prog.stream("d_wg", "pool")}
        for i in range(6):
            prog.stream(f"d_W{i}", "sp")
            prog.stream(f"dp_W{i}", "pool")
        make_std(ctx, n_w_tiles=0, w_cols=0, psum=True, xs_cols=S + 1)
        ctx.xs_off = 1
        for i in range(4):
            ctx.sb(f"W{i}", [128, 1024], BF16)
        for i in range(4, 6):
            ctx.sb(f"W{i}", [128, 2816], BF16)
        for i in range(22):
            ctx.sb(f"H{i}", [128, 1024], BF16)
        t = ctx.t
        for nm, src_ in (("B1C", b1T), ("B2C", b2T), ("LNG", lngT), ("LNB", lnbT)):
            dd = ctx.sb(nm, [128, src_.shape[-1]], F32)
            prog.op("sp", lambda g, dd=dd, src_=src_: g.dma_start(out=dd[:], in_=src_),
                    reads=(f"hbm:{nm}",), writes=(nm,), stream=streams["setup"])
        load_constants(ctx, streams)
        prog.freeze_stream("d_setup")

        for seq in range(nseq):
            emit_load_x(ctx, x_hbm, seq, streams)
            for c in range(NCH):
                prog.op("dve", lambda v, c=c: v.memset(t[f"XS{c}"][:, 0:1], 0.0),
                        writes=(f"XS{c}",))
            emit_cast_shift(ctx)
            ctx.ln_param_regions = ("LNG", "LNB")
            emit_ln(ctx, g_col=lambda c: t["LNG"][:, c:c + 1],
                    b_col=lambda c: t["LNB"][:, c:c + 1])
            emit_ffn_gated(ctx, w1, lambda oc: t["B1C"][:, oc:oc + 1], w2,
                           lambda c: t["B2C"][:, c:c + 1], streams,
                           None, None, param_regions=("B1C", "B2C"))
            emit_store_x(ctx, y_hbm, seq, streams)
        prog.wait_region("sp", tuple(f"hbm:x{s_}" for s_ in range(nseq)))
        prog.finalize()
    return nc


def emit_tables(ctx, streams, tb_aps, qs_col, qsp_col, ks_col, ksp_col,
                param_regions=()):
    p, t = ctx.p, ctx.t
    for dst, base, col, sc in (("TQ1", "TBQC", qs_col, 8.0),
                               ("TQ2", "TBQS", qsp_col, 8.0),
                               ("TK1", "TBKC", ks_col, 1.0),
                               ("TK2", "TBKS", ksp_col, 1.0)):
        ap = tb_aps[base]
        if ap.dtype == BF16:
            p.op("sp", lambda g, dst=dst, ap=ap: g.dma_start(
                out=t[dst][:], in_=ap),
                reads=(f"c:{base}",), writes=(dst,), stream=streams["tr"])
        else:
            p.op("pool", lambda g, dst=dst, ap=ap: g.dma_start(
                out=t[dst][:], in_=ap),
                reads=(f"c:{base}",), writes=(dst,), stream=streams["wg"])
        p.op("dve", lambda v, dst=dst, col=col, sc=sc: v.tensor_scalar(
            t[dst][:], t[dst][:], col, sc, op0=ALU.mult, op1=ALU.mult),
            reads=(dst,) + param_regions, writes=(dst,))


def emit_attn_sublayer(ctx, streams, wqkv_hbm, wo_hbm, qs_col, qsp_col, ks_col,
                       ksp_col, param_regions=(), tag=""):
    p, t = ctx.p, ctx.t
    W = 256
    for ch in range(NCH):
        for wi, dst in ((0, "QP"), (1, "KP"), (2, "VP")):
            wk_ = f"W{2 * wi + ch % 2}"
            ctx.wload(wk_, wqkv_hbm[8 * wi + ch], 1024)

        def proj(tt, wi, dst):
            sl = slice(tt * TT, (tt + 1) * TT)
            wk_ = f"W{2 * wi + ch % 2}"
            wt = t[wk_]
            pb = f"PS{(wi + 3 * tt) % 2}"
            def mm(eng, wt=wt, sl=sl, pb=pb):
                last = None
                for k in range(NCH):
                    last = eng.matmul(t[pb][:, 0:TT], wt[:, k * 128:(k + 1) * 128],
                                      t[f"XN{k}"][:, sl],
                                      start=(k == 0), stop=(k == NCH - 1))
                return last
            p.op("pe", mm,
                 reads=tuple(f"XN{k}@{tt}" for k in range(NCH)) + (wk_,),
                 writes=(pb,))
            if dst == "VP":
                p.op("act", lambda s_, pb=pb, sl=sl, dst=dst: s_.copy(
                    t[dst][:, sl], t[pb][:, 0:TT]),
                    reads=(pb,), writes=(f"{dst}@{tt}",))
            else:
                p.op("dve", lambda v, pb=pb, sl=sl, dst=dst: v.tensor_copy(
                    t[dst][:, sl], t[pb][:, 0:TT]),
                    reads=(pb,), writes=(f"{dst}@{tt}",))

        rit = 0

        def rotary(tt, side):
            nonlocal rit
            src_, t1, t2 = ("QP", "TQ1", "TQ2") if side == "q" else ("KP", "TK1", "TK2")
            sl = slice(tt * TT, (tt + 1) * TT)
            par = rit % 2
            rit += 1
            pA = "PS2"
            pR = "PS3"
            s0, s1, s2, s3 = (f"SQ{4 * par}", f"SQ{4 * par + 1}",
                              f"SQ{4 * par + 2}", f"SQ{4 * par + 3}")
            rn = f"RN2{par}"
            p.op("pool", lambda g, src_=src_, sl=sl, s0=s0, tt=tt: g.tensor_tensor(
                t[s0][:, 0:TT], t[src_][:, sl], t[src_][:, sl], op=ALU.mult),
                reads=(f"{src_}@{tt}",), writes=(s0,))
            p.op("pe", lambda eng, pA=pA, s0=s0: eng.matmul(
                t[pA][0:2, 0:TT], t["EHD2b"][:, 0:2], t[s0][:, 0:TT],
                start=True, stop=True),
                reads=(s0, "EHD2b"), writes=(pA,))
            p.op("act", lambda s_, pA=pA, rn=rn: s_.activation(
                t[rn][:, 0:TT], t[pA][0:2, 0:TT], AF.Ln,
                bias=t["CONSTS"][0:2, 6:7]),
                reads=(pA, "CONSTS"), writes=(rn,))
            p.op("act", lambda s_, rn=rn: s_.activation(
                t[rn][:, 0:TT], t[rn][:, 0:TT], AF.Exp, scale=-0.5),
                reads=(rn,), writes=(rn,))
            p.op("pe", lambda eng, pA=pA, rn=rn: eng.matmul(
                t[pA][:, 0:TT], t["E2Tf"][:, 0:128], t[rn][:, 0:TT],
                start=True, stop=True),
                reads=(rn, "E2Tf"), writes=(pA,))
            p.op("pe", lambda eng, pR=pR, src_=src_, sl=sl, tt=tt: eng.matmul(
                t[pR][:, 0:TT], t["RHTb"][:, 0:128], t[src_][:, sl],
                start=True, stop=True),
                reads=(f"{src_}@{tt}", "RHTb"), writes=(pR,))
            p.op("pool", lambda g, t1=t1, sl=sl, src_=src_, s2=s2, tt=tt:
                 g.tensor_tensor(
                     t[s2][:, 0:TT], t[src_][:, sl], t[t1][:, sl], op=ALU.mult),
                 reads=(f"{src_}@{tt}", t1), writes=(s2,))
            p.op("dve", lambda v, t2=t2, sl=sl, pR=pR, s3=s3: v.tensor_tensor(
                t[s3][:, 0:TT], t[pR][:, 0:TT], t[t2][:, sl], op=ALU.mult),
                reads=(pR, t2), writes=(s3,))
            p.op("pool", lambda g, s1=s1, s2=s2, s3=s3: g.tensor_tensor(
                t[s1][:, 0:TT], t[s2][:, 0:TT], t[s3][:, 0:TT], op=ALU.add),
                reads=(s2, s3), writes=(s1,))
            p.op("dve", lambda v, src_=src_, sl=sl, pA=pA, s1=s1, tt=tt:
                 v.tensor_tensor(
                     t[src_][:, sl], t[s1][:, 0:TT], t[pA][:, 0:TT], op=ALU.mult),
                 reads=(s1, pA), writes=(f"{src_}@{tt}",))

        for tt in range(NTT + 1):
            if tt < NTT:
                proj(tt, 0, "QP")
                proj(tt, 1, "KP")
                proj(tt, 2, "VP")
            if tt >= 1:
                rotary(tt - 1, "q")
                rotary(tt - 1, "k")
        for hh in range(2):
            b = 64 * hh
            vt = f"VTM{hh}"
            for j0 in range(4):
                pco = 256 * (j0 % 2) + 512 * hh
                def trs(eng, b=b, j0=j0, pco=pco):
                    last = None
                    for jj in range(4):
                        j = 4 * j0 + jj
                        last = eng.transpose(t["PTB"][:, pco + jj * 64:pco + (jj + 1) * 64],
                                             t["VP"][b:b + 64, j * 128:(j + 1) * 128],
                                             t["IDENT2b"][b:b + 64, 0:64])
                    return last
                p.op("pe", trs, reads=(f"VP@{j0}", "IDENT2b"), writes=("PTB",))
                p.op("dve", lambda v, vt=vt, j0=j0, pco=pco: v.tensor_copy(
                    bass.AP(t[vt], 4 * j0 * 65, [[1040, 128], [65, 4], [1, 64]]),
                    bass.AP(t["PTB"], pco, [[1024, 128], [64, 4], [1, 64]])),
                    reads=("PTB",), writes=(vt,))
        for grp in range(4):
            iters = [(hh, qt) for qt in range(4 * grp, 4 * grp + 4)
                     for hh in range(2)]

            def emit_s_l(k):
                hh, qt = iters[k]
                b = 64 * hh
                c_lo = max(0, qt - 2)
                bw = (qt - c_lo + 1) * 128
                pb = f"PS{k % 3}"
                kt0, kt1 = c_lo // 4, qt // 4
                p.op("pe", lambda eng, b=b, qt=qt, c_lo=c_lo, bw=bw, pb=pb:
                     eng.matmul(t[pb][:, 0:bw],
                                t["QP"][b:b + 64, qt * 128:(qt + 1) * 128],
                                t["KP"][b:b + 64, c_lo * 128:c_lo * 128 + bw],
                                start=True, stop=True),
                     reads=(f"QP@{qt // 4}", f"KP@{kt0}", f"KP@{kt1}"),
                     writes=(pb,))

            def emit_m_l(k):
                hh, qt = iters[k]
                c_lo = max(0, qt - 2)
                nb = qt - c_lo + 1
                bw = nb * 128
                pb = f"PS{k % 3}"
                nm = f"NM{k % 2}"
                p.op("dve", lambda v, pb=pb, bw=bw, nm=nm: v.tensor_reduce(
                    t[nm][:, 0:1], t[pb][:, 0:bw], axis=AX.X, op=ALU.max,
                    negate=True),
                    reads=(pb,), writes=(nm,))
                sq = f"SQ{k % 6}"
                p.op("act", lambda s_, pb=pb, sq=sq, bw=bw, nm=nm: s_.activation(
                    t[sq][:, 0:bw], t[pb][:, 0:bw], AF.Exp, bias=t[nm][:, 0:1]),
                    reads=(pb, nm), writes=(sq,))
                if qt >= 2:
                    p.op("pool", lambda g, sq=sq: g.tensor_tensor(
                        t[sq][:, 0:128], t[sq][:, 0:128], t["TRIUb"][:, 0:128],
                        op=ALU.mult),
                        reads=("TRIUb", sq), writes=(sq,))
                p.op("pool", lambda g, sq=sq, bw=bw: g.tensor_tensor(
                    t[sq][:, bw - 128:bw], t[sq][:, bw - 128:bw],
                    t["TRILb"][:, 0:128], op=ALU.mult),
                    reads=("TRILb", sq), writes=(sq,))

            def emit_t_l(k):
                hh, qt = iters[k]
                c_lo = max(0, qt - 2)
                nb = qt - c_lo + 1
                sq = f"SQ{k % 6}"
                pcol = 512 * (k % 2)
                def trs2(eng, sq=sq, nb=nb, pcol=pcol):
                    last = None
                    for cc in range(nb):
                        last = eng.transpose(
                            t["PTB"][:, pcol + cc * 128:pcol + (cc + 1) * 128],
                            t[sq][:, cc * 128:(cc + 1) * 128], t["IDENTb"][:])
                    return last
                p.op("pe", trs2, reads=(sq, "IDENTb"), writes=("PTB",))
                ptt = f"PTT{k % 2}"
                p.op("dve", lambda v, ptt=ptt, nb=nb, pcol=pcol: v.tensor_copy(
                    t[ptt][:, 0:nb * 128], t["PTB"][:, pcol:pcol + nb * 128]),
                    reads=("PTB",), writes=(ptt,))

            def emit_pv_l(k):
                hh, qt = iters[k]
                vt = f"VTM{hh}"
                c_lo = max(0, qt - 2)
                nb = qt - c_lo + 1
                pz = f"PZ{hh}"
                zsl = slice((qt % 4) * 128, (qt % 4) * 128 + 128)
                ptt = f"PTT{k % 2}"
                def pv(eng, vt=vt, c_lo=c_lo, nb=nb, pz=pz, zsl=zsl, ptt=ptt):
                    last = None
                    for cc in range(nb):
                        last = eng.matmul(
                            t[pz][:, zsl],
                            t[vt][:, (c_lo + cc) * 65:(c_lo + cc) * 65 + 65],
                            t[ptt][:, cc * 128:(cc + 1) * 128],
                            start=(cc == 0), stop=(cc == nb - 1))
                    return last
                p.op("pe", pv, reads=(vt, ptt), writes=(pz,))

            for k_ in range(len(iters) + 6):
                if k_ < len(iters):
                    emit_s_l(k_)
                if 2 <= k_ and k_ - 2 < len(iters):
                    emit_m_l(k_ - 2)
                if k_ >= 6:
                    emit_pv_l(k_ - 6)
                if 4 <= k_ and k_ - 4 < len(iters):
                    emit_t_l(k_ - 4)
            for hh in range(2):
                b = 64 * hh
                pz = f"PZ{hh}"
                qsl = slice(grp * TT, (grp + 1) * TT)
                br = f"BROW{hh % 2}"
                p.op("act", lambda s_, pz=pz, br=br: s_.activation(
                    t[br][:, 0:TT], t[pz][64:65, 0:TT], AF.Ln),
                    reads=(pz,), writes=(br,))
                p.op("act", lambda s_, br=br: s_.activation(
                    t[br][:, 0:TT], t[br][:, 0:TT], AF.Exp, scale=-1.0),
                    reads=(br,), writes=(br,))
                pbq = f"PS{hh % 2}"
                p.op("pe", lambda eng, br=br, pbq=pbq: eng.matmul(
                    t[pbq][0:64, 0:TT], t["ONESf"][0:1, 0:64],
                    t[br][0:1, 0:TT], start=True, stop=True),
                    reads=(br, "ONESf"), writes=(pbq,))
                sq = f"SQ{6 + hh}"
                p.op("dve", lambda v, sq=sq, pbq=pbq: v.tensor_copy(
                    t[sq][0:64, 0:TT], t[pbq][0:64, 0:TT]),
                    reads=(pbq,), writes=(sq,))
                p.op("dve", lambda v, ch=ch, b=b, qsl=qsl, sq=sq, pz=pz:
                     v.tensor_tensor(
                         t[f"ZB{ch}"][b:b + 64, qsl], t[pz][0:64, 0:TT],
                         t[sq][0:64, 0:TT], op=ALU.mult),
                     reads=(pz, sq), writes=(f"ZB{ch}",))
    for c in range(NCH):
        wk_ = f"W{c % 2}"
        wt = t[wk_]
        ctx.wload(wk_, wo_hbm[c], 1024)
        for tt in range(NTT):
            sl = slice(tt * TT, (tt + 1) * TT)
            xsl = slice(ctx.xs_off + tt * TT, ctx.xs_off + (tt + 1) * TT)
            pb = f"PS{tt % 2}"
            def mm(eng, wt=wt, sl=sl, pb=pb):
                last = None
                for k in range(NCH):
                    last = eng.matmul(t[pb][:, 0:TT], wt[:, k * 128:(k + 1) * 128],
                                      t[f"ZB{k}"][:, sl],
                                      start=(k == 0), stop=(k == NCH - 1))
                return last
            p.op("pe", mm, reads=tuple(f"ZB{k}" for k in range(NCH)) + (wk_,),
                 writes=(pb,))
            p.op("dve", lambda v, c=c, pb=pb, xsl=xsl: v.tensor_tensor(
                t[f"XS{c}"][:, xsl], t[f"XS{c}"][:, xsl], t[pb][:, 0:TT], op=ALU.add),
                reads=(f"XS{c}", pb), writes=(f"XS{c}",))


def build_l1ffn_asub(nseq=2):
    nc = bass.Bass("TRN2", target_bir_lowering=False)
    x_hbm = nc.dram_tensor("x", [nseq, D, S], F32, kind="ExternalInput").ap()
    y_hbm = nc.dram_tensor("y", [nseq, D, S], F32, kind="ExternalOutput").ap()
    wqkv = nc.dram_tensor("wqkv", [24, 128, D], F32, kind="ExternalInput").ap()
    wo = nc.dram_tensor("wo", [NCH, 128, D], F32, kind="ExternalInput").ap()
    qsT = nc.dram_tensor("qsT", [128, 1], F32, kind="ExternalInput").ap()
    qspT = nc.dram_tensor("qspT", [128, 1], F32, kind="ExternalInput").ap()
    ksT = nc.dram_tensor("ksT", [128, 1], F32, kind="ExternalInput").ap()
    kspT = nc.dram_tensor("kspT", [128, 1], F32, kind="ExternalInput").ap()

    es = ExitStack()
    with es:
        prog = Prog(nc, es)
        ctx = Ctx(nc, es, prog)
        streams = {"xs": prog.stream("d_xs", "sp"),
                   "setup": prog.stream("d_setup", "sp"),
                   "tr": prog.stream("d_tr", "sp"),
                   "wg": prog.stream("d_wg", "pool")}
        for i in range(6):
            prog.stream(f"d_W{i}", "sp")
            prog.stream(f"dp_W{i}", "pool")
        make_std(ctx, n_w_tiles=6, w_cols=1024, psum=False, xs_cols=S + 1)
        ctx.xs_off = 1
        sb, ps = ctx.sb, ctx.ps
        alloc_asub(ctx)
        t = ctx.t
        emit_asub_setup(ctx, streams, qsT, qspT, ksT, kspT)
        load_constants(ctx, streams)
        prog.freeze_stream("d_setup")

        for seq in range(nseq):
            emit_load_x(ctx, x_hbm, seq, streams)
            for c in range(NCH):
                prog.op("dve", lambda v, c=c: v.memset(t[f"XS{c}"][:, 0:1], 0.0),
                        writes=(f"XS{c}",))
            for vtn in ("VTM0", "VTM1"):
                prog.op("dve", lambda v, vtn=vtn: v.memset(
                    bass.AP(t[vtn], 64, [[1040, 128], [65, 16], [1, 1]]), 1.0),
                    writes=(vtn,))
            emit_tables(ctx, streams, ctx.tb_aps,
                        t["QSC"][:, 0:1], t["QSPC"][:, 0:1],
                        t["KSC"][:, 0:1], t["KSPC"][:, 0:1],
                        param_regions=("QSC", "QSPC", "KSC", "KSPC"))
            emit_cast_shift(ctx)
            emit_attn_sublayer(ctx, streams, wqkv, wo,
                               t["QSC"][:, 0:1], t["QSPC"][:, 0:1],
                               t["KSC"][:, 0:1], t["KSPC"][:, 0:1],
                               param_regions=("QSC", "QSPC", "KSC", "KSPC"))
            emit_store_x(ctx, y_hbm, seq, streams)
        prog.wait_region("sp", tuple(f"hbm:x{s_}" for s_ in range(nseq)))
        prog.finalize()
    return nc


def alloc_asub(ctx):
    sb, ps = ctx.sb, ctx.ps
    sb("QP", [128, S], BF16)
    sb("KP", [128, S], BF16)
    sb("VP", [128, S], BF16)
    sb("VTM0", [128, 16 * 65], BF16)
    sb("VTM1", [128, 16 * 65], BF16)
    sb("TRIUb", [128, 128], BF16)
    sb("TRILb", [128, 128], BF16)
    sb("RHTb", [128, 128], BF16)
    sb("EHD2b", [128, 2], BF16)
    sb("E2Tf", [2, 128], F32)
    sb("RN20", [2, TT], F32)
    sb("RN21", [2, TT], F32)
    sb("NM0", [128, 1], F32)
    sb("NM1", [128, 1], F32)
    for i in range(4, 8):
        sb(f"SQ{i}", [128, TT], BF16)
    for i in range(2):
        sb(f"PTT{i}", [128, 384], BF16)
    for nm in ("TQ1", "TQ2", "TK1", "TK2"):
        sb(nm, [128, S], BF16)
    for ch in range(NCH):
        sb(f"ZB{ch}", [128, S], BF16)
    for nm in ("QSC", "QSPC", "KSC", "KSPC"):
        sb(nm, [128, 6], F32)
    ps("PS0", [128, 512], F32)
    ps("PS1", [128, 512], F32)
    ps("PS2", [128, 512], F32)
    ps("PS3", [128, 512], F32)
    ps("PZ0", [65, 512], F32)
    ps("PZ1", [65, 512], F32)
    ps("PTB", [128, 1024], BF16)


def emit_asub_setup(ctx, streams, qsT, qspT, ksT, kspT):
    p, t = ctx.p, ctx.t
    nc = ctx.nc
    for nm, src in (("QSC", qsT), ("QSPC", qspT), ("KSC", ksT), ("KSPC", kspT)):
        p.op("sp", lambda g, nm=nm, src=src: g.dma_start(
            out=t[nm][:, 0:src.shape[-1]], in_=src),
            reads=(f"hbm:{nm}",), writes=(nm,), stream=streams["setup"])
    ctx.tb_aps = {}
    for nm in ("TRIL", "RHT", "EHD2"):
        shape = {"TRIL": [128, 128], "RHT": [128, 128], "EHD2": [128, 2]}[nm]
        ap = nc.dram_tensor("CONST_" + nm, shape, F32, kind="ExternalInput").ap()
        dstn = nm + "b" if nm in ("TRIL", "RHT") else "EHD2b"
        p.op("pool", lambda g, ap=ap, dstn=dstn: g.dma_start(out=t[dstn][:], in_=ap),
             reads=(f"c:{nm}",), writes=(dstn,), stream=streams["wg"])
    for nm in ("TBQC", "TBQS", "TBKC", "TBKS"):
        ctx.tb_aps[nm] = nc.dram_tensor("CONST_" + nm, [128, S], F32,
                                        kind="ExternalInput").ap()
    ap = nc.dram_tensor("CONST_E2T", [2, 128], F32, kind="ExternalInput").ap()
    p.op("sp", lambda g, ap=ap: g.dma_start(out=t["E2Tf"][:], in_=ap),
         reads=("c:E2T",), writes=("E2Tf",), stream=streams["setup"])
    ap2 = nc.dram_tensor("CONST_TRIU", [128, 128], F32, kind="ExternalInput").ap()
    p.op("pool", lambda g: g.dma_start(out=t["TRIUb"][:], in_=ap2),
         reads=("c:TRIU",), writes=("TRIUb",), stream=streams["wg"])


def build_ln_out(nseq=2):
    nc = bass.Bass("TRN2", target_bir_lowering=False)
    x_hbm = nc.dram_tensor("x", [nseq, D, S], F32, kind="ExternalInput").ap()
    y_hbm = nc.dram_tensor("y", [nseq, D, S], F32, kind="ExternalOutput").ap()
    es = ExitStack()
    with es:
        prog = Prog(nc, es)
        ctx = Ctx(nc, es, prog)
        streams = {"xs": prog.stream("d_xs", "sp"),
                   "setup": prog.stream("d_setup", "sp"),
                   "wg": prog.stream("d_wg", "pool")}
        make_std(ctx, n_w_tiles=0, w_cols=0, psum=True)
        load_constants(ctx, streams)
        prog.freeze_stream("d_setup")
        for seq in range(nseq):
            emit_load_x(ctx, x_hbm, seq, streams)
            emit_cast_x(ctx)
            emit_ln(ctx, out_f32="XS")
            emit_store_x(ctx, y_hbm, seq, streams)
        prog.wait_region("sp", tuple(f"hbm:x{s_}" for s_ in range(nseq)))
        prog.finalize()
    return nc


def build_axpy(nseq=2, alpha=0.5):
    nc = bass.Bass("TRN2", target_bir_lowering=False)
    x_hbm = nc.dram_tensor("x", [nseq, D, S], F32, kind="ExternalInput").ap()
    yin_hbm = nc.dram_tensor("yin", [nseq, D, S], F32, kind="ExternalInput").ap()
    y_hbm = nc.dram_tensor("y", [nseq, D, S], F32, kind="ExternalOutput").ap()
    es = ExitStack()
    with es:
        prog = Prog(nc, es)
        ctx = Ctx(nc, es, prog)
        streams = {"xs": prog.stream("d_xs", "sp"), "y": prog.stream("d_y", "sp")}
        t = ctx.t
        for i in range(NCH):
            ctx.sb(f"XS{i}", [128, S], F32)
        for i in range(2):
            ctx.sb(f"YT{i}", [128, S], F32)
        for seq in range(nseq):
            emit_load_x(ctx, x_hbm, seq, streams)
            for c in range(NCH):
                yt = f"YT{c % 2}"
                prog.op("sp", lambda g, c=c, yt=yt, seq=seq: g.dma_start(
                    out=t[yt][:], in_=yin_hbm[seq, c * 128:(c + 1) * 128, :]),
                    reads=(f"hbm:yin{seq}",), writes=(yt,), stream=streams["y"])
                for tt in range(NTT):
                    sl = slice(tt * TT, (tt + 1) * TT)
                    prog.op("dve", lambda v, c=c, yt=yt, sl=sl: v.scalar_tensor_tensor(
                        t[f"XS{c}"][:, sl], t[yt][:, sl], alpha, t[f"XS{c}"][:, sl],
                        op0=ALU.mult, op1=ALU.add),
                        reads=(yt, f"XS{c}"), writes=(f"XS{c}",))
            emit_store_x(ctx, y_hbm, seq, streams)
        prog.wait_region("sp", tuple(f"hbm:x{s_}" for s_ in range(nseq)))
        prog.finalize()
    return nc




_LAST_PROG = None


def build_fused(nseq=2):
    global _LAST_PROG
    nc = bass.Bass("TRN2", target_bir_lowering=False)
    gi = lambda n, shp: nc.dram_tensor(n, shp, F32, kind="ExternalInput").ap()
    x_hbm = gi("x", [nseq, D, S])
    y_hbm = nc.dram_tensor("y", [nseq, D, S], F32, kind="ExternalOutput").ap()
    spill = nc.dram_tensor("spill", [D, S], F32).ap()
    fw1_ = gi("l0f_w1", [32, 128, D]); fw2_ = gi("l0f_w2", [NCH, 128, 4096])
    fb1T = gi("l0f_b1T", [128, 32]); fb2T = gi("l0f_b2T", [128, NCH])
    awq = gi("l0a_wq", [NCH, 128, D]); awk = gi("l0a_wk", [NCH, 128, D])
    awv = gi("l0a_wv", [NCH, 128, D]); awo = gi("l0a_wo", [NCH, 128, D])
    awqT = gi("l0a_wqT", [NCH, 128, D]); awkT = gi("l0a_wkT", [NCH, 128, D])
    arexp = gi("l0a_rexp", [128, 1024])
    abqT = gi("l0a_bqT", [128, NCH]); abkT = gi("l0a_bkT", [128, NCH])
    abvT = gi("l0a_bvT", [128, NCH]); aboT = gi("l0a_boT", [128, NCH])
    cwq = gi("l1a_wq", [NCH, 128, D]); cwk = gi("l1a_wk", [NCH, 128, D])
    cwv = gi("l1a_wv", [NCH, 128, D]); cwo = gi("l1a_wo", [NCH, 128, D])
    swqkv = gi("s_wqkv", [6, 24, 128, D]); swo = gi("s_wo", [6, NCH, 128, D])
    sw1 = gi("s_w1", [6, 44, 128, D]); sw2 = gi("s_w2", [6, NCH, 128, 2816])
    sqsT = gi("s_qsT", [128, 6]); sqspT = gi("s_qspT", [128, 6])
    sksT = gi("s_ksT", [128, 6]); skspT = gi("s_kspT", [128, 6])
    sb1T = gi("s_b1T", [128, 6 * 44]); sb2T = gi("s_b2T", [128, 6 * NCH])
    slngT = gi("s_lngT", [128, 6 * NCH]); slnbT = gi("s_lnbT", [128, 6 * NCH])

    es = ExitStack()
    with es:
        prog = Prog(nc, es)
        ctx = Ctx(nc, es, prog)
        streams = {"xs": prog.stream("d_xs", "sp"),
                   "setup": prog.stream("d_setup", "sp"),
                   "tr": prog.stream("d_tr", "sp"),
                   "wg": prog.stream("d_wg", "pool"),
                   "sw": prog.stream("d_sw", "sp")}
        for i in range(6):
            prog.stream(f"d_W{i}", "sp")
            prog.stream(f"dp_W{i}", "pool")
        make_std(ctx, n_w_tiles=4, w_cols=1024, psum=False, xs_cols=S + 1)
        ctx.xs_off = 1
        t = ctx.t
        sb, ps = ctx.sb, ctx.ps

        sb("TRIUb", [128, 128], BF16)
        sb("TRILb", [128, 128], BF16)
        sb("RHTb", [128, 128], BF16)
        sb("EHD2b", [128, 2], BF16)
        sb("E2Tf", [2, 128], F32)
        sb("B1T", [128, 32], F32)
        sb("B2T", [128, NCH], F32)
        sb("B2H", [128, NCH], F32)
        for nm in ("QSC", "QSPC", "KSC", "KSPC"):
            sb(nm, [128, 6], F32)
        sb("B1C", [128, 6 * 44], F32)
        sb("B2C", [128, 6 * NCH], F32)
        sb("LNG", [128, 6 * NCH], F32)
        sb("LNB", [128, 6 * NCH], F32)
        for nm, src_ in (("B1T", fb1T), ("B2T", fb2T), ("QSC", sqsT),
                         ("QSPC", sqspT), ("KSC", sksT), ("KSPC", skspT),
                         ("B1C", sb1T), ("B2C", sb2T), ("LNG", slngT),
                         ("LNB", slnbT), ("E2Tf", None)):
            if nm == "E2Tf":
                src_ = nc.dram_tensor("CONST_E2T", [2, 128], F32,
                                      kind="ExternalInput").ap()
            prog.op("sp", lambda g, nm=nm, src_=src_: g.dma_start(
                out=t[nm][:], in_=src_),
                reads=(f"hbm:{nm}",), writes=(nm,), stream=streams["setup"])
        load_constants(ctx, streams)
        for nm in ("TRIL", "RHT", "EHD2", "TRIU"):
            shp = {"TRIL": [128, 128], "RHT": [128, 128], "EHD2": [128, 2],
                   "TRIU": [128, 128]}[nm]
            ap = nc.dram_tensor("CONST_" + nm, shp, F32, kind="ExternalInput").ap()
            dstn = nm + "b"
            prog.op("pool", lambda g, ap=ap, dstn=dstn: g.dma_start(
                out=t[dstn][:], in_=ap),
                reads=(f"c:{nm}",), writes=(dstn,), stream=streams["wg"])
        prog.freeze_stream("d_setup")
        tb_f32 = {nm: nc.dram_tensor("CONST_" + nm, [128, S], F32,
                                     kind="ExternalInput").ap()
                  for nm in ("TBQC", "TBQS", "TBKC", "TBKS")}
        def bfdecl(name, shape):
            return nc.dram_tensor(name + "_bf", shape, BF16).ap()
        fw1b = bfdecl("l0f_w1", [32, 128, D])
        fw2b = bfdecl("l0f_w2", [NCH, 128, 4096])
        awqb, awkb, awvb, awob = (bfdecl(n, [NCH, 128, D]) for n in
                                  ("l0a_wq", "l0a_wk", "l0a_wv", "l0a_wo"))
        cwqb, cwkb, cwvb, cwob = (bfdecl(n, [NCH, 128, D]) for n in
                                  ("l1a_wq", "l1a_wk", "l1a_wv", "l1a_wo"))
        swqkvb = bfdecl("s_wqkv", [6, 24, 128, D])
        swob = bfdecl("s_wo", [6, NCH, 128, D])
        sw1b = bfdecl("s_w1", [6, 44, 128, D])
        sw2b = bfdecl("s_w2", [6, NCH, 128, 2816])
        tb_aps = {nm: bfdecl("TB" + nm, [128, S]) for nm in
                  ("TBQC", "TBQS", "TBKC", "TBKS")}
        groups = [
            ("cv_l0f", [(fw1_[i], fw1b[i]) for i in range(32)] +
                       [(fw2_[i], fw2b[i]) for i in range(NCH)]),
            ("cv_l0a", [(a[i], b_[i]) for a, b_ in
                        ((awq, awqb), (awk, awkb), (awv, awvb), (awo, awob))
                        for i in range(NCH)]),
            ("cv_tb", [(tb_f32[nm], tb_aps[nm]) for nm in tb_f32]),
        ]
        for si in range(6):
            groups.append((f"cv_s{si}",
                           [(swqkv[si][i], swqkvb[si][i]) for i in range(24)] +
                           [(sw1[si][i], sw1b[si][i]) for i in range(44)] +
                           [(sw2[si][i], sw2b[si][i]) for i in range(NCH)] +
                           [(swo[si][i], swob[si][i]) for i in range(NCH)]))
        groups.append(("cv_l1a", [(a[i], b_[i]) for a, b_ in
                                  ((cwq, cwqb), (cwk, cwkb), (cwv, cwvb),
                                   (cwo, cwob)) for i in range(NCH)]))
        with ctx.scope():
            for i in range(4):
                sb(f"CV{i}", [128, 4096], BF16)
            cvi = 0
            for gname, pairs in groups:
                sin = prog.stream("din_" + gname, "pool")
                sout = prog.stream("dout_" + gname, "sp")
                for srcap, dstap in pairs:
                    cv = f"CV{cvi % 4}"
                    cvi += 1
                    ncols = srcap.shape[-1]
                    p_ = prog
                    p_.op("pool", lambda g, cv=cv, srcap=srcap, ncols=ncols:
                          g.dma_start(out=t[cv][:, 0:ncols], in_=srcap),
                          reads=("hbm:" + srcap.tensor.name,), writes=(cv,),
                          stream=sin)
                    p_.op("sp", lambda g, cv=cv, dstap=dstap, ncols=ncols:
                          g.dma_start(out=dstap, in_=t[cv][:, 0:ncols]),
                          reads=(cv,), writes=("hbm:" + dstap.tensor.name,),
                          stream=sout)
        prog.op("dve", lambda v: v.tensor_scalar(
            t["B2H"][:], t["B2T"][:], 0.5, None, op0=ALU.mult),
            reads=("B2T",), writes=("B2H",))

        def l0ffn_scope():
            with ctx.scope():
                for i in range(32):
                    sb(f"H{i}", [128, 1024], BF16)
                sb("W4", [128, 2048], BF16)
                sb("W5", [128, 2048], BF16)
                for i in range(4):
                    ps(f"P{i}", [128, 512], F32)
                ps("P4", [128, 512], F32)
                ps("P5", [128, 512], F32)
                ps("P6", [128, 512], F32)
                ps("P7", [128, 512], F32)
                ctx.ln_banks = ("P6", "P7", "P4", "P5")
                emit_cast_x(ctx)
                emit_ln(ctx)
                emit_ffn_l0(ctx, fw1b, lambda c4: t["B1T"][:, c4:c4 + 1], fw2b,
                            lambda c: t["B2H"][:, c:c + 1], streams,
                            bias_regions=("B1T", "B2H"))

        def attn_scope(*, bucket, tag):
            with ctx.scope():
                sb("QP", [128, S], BF16)
                sb("KP", [128, S], BF16)
                sb("VP", [128, S], BF16)
                sb("VTM0", [128, 16 * 65], BF16)
                sb("W4", [128, 1024], BF16)
                sb("W5", [128, 1024], BF16)
                for ch in range(NCH):
                    sb(f"ZB{ch}", [128, S], BF16)
                ps("PS0", [128, 512], F32)
                ps("PS1", [128, 512], F32)
                ps("PS2", [128, 512], F32)
                ps("PZ", [65, 2048], F32)
                ps("PTB", [128, 1024], BF16)
                ctx.ln_banks = ("PS0", "PS1")
                if bucket:
                    sb("SROW", [1, TT], F32)
                    sb("WRQ", [128, 1024], F32)
                    sb("WRK", [128, 1024], F32)
                    sb("NCSQ", [1, 128], F32)
                    sb("NCSK", [1, 128], F32)
                    sb("BQR", [1, 128], F32)
                    sb("BKR", [1, 128], F32)
                    for nm, src_ in (("BQ8", abqT), ("BKC", abkT),
                                     ("BVC", abvT), ("BOC", aboT)):
                        dd = sb(nm, [128, NCH], F32)
                        prog.op("sp", lambda g, dd=dd, src_=src_: g.dma_start(
                            out=dd[:], in_=src_),
                            reads=(f"hbm:{nm}2",), writes=(nm,), stream=streams["tr"])
                    prog.op("dve", lambda v: v.tensor_scalar(
                        t["BQ8"][:], t["BQ8"][:], 0.125, None, op0=ALU.mult),
                        reads=("BQ8",), writes=("BQ8",))
                    with ctx.scope():
                        sb("WTMP", [128, 1024], F32)
                        sb("REXP", [128, 1024], F32)
                        prog.op("sp", lambda g: g.dma_start(
                            out=t["REXP"][:], in_=arexp),
                            reads=("hbm:rexp",), writes=("REXP",), stream=streams["tr"])
                        emit_wr_setup_inline(ctx, awqT, awkT, streams)
                    sb("AUGQ", [72, S], BF16)
                    sb("AUGK", [72, S], BF16)
                    sb("TRANSQ", [128, S], BF16)
                    sb("TRANSK", [128, S], BF16)
                    sb("OHSB", [128, 128], BF16)
                    sb("M16", [128, 16], F32)
                prog.op("dve", lambda v: v.memset(
                    bass.AP(t["VTM0"], 64, [[1040, 128], [65, 16], [1, 1]]), 1.0),
                    writes=("VTM0",))
                emit_cast_x(ctx)
                if bucket:
                    emit_ln(ctx, srow=True,
                            tile_hook=lambda tt: emit_qr_onehot(ctx, tt, streams))
                    emit_full_attn(ctx, streams, bucket=True, causal=False,
                                   wq_hbm=awqb, wk_hbm=awkb, wv_hbm=awvb,
                                   wo_hbm=awob,
                                   bq8_col=lambda c: t["BQ8"][:, c:c + 1],
                                   bk_col=lambda c: t["BKC"][:, c:c + 1],
                                   bv_col=lambda c: t["BVC"][:, c:c + 1],
                                   bo_col=lambda c: t["BOC"][:, c:c + 1],
                                   param_regions=("BQ8", "BKC", "BVC", "BOC"),
                                   tag=tag)
                else:
                    emit_ln(ctx)
                    emit_full_attn(ctx, streams, bucket=False, causal=True,
                                   wq_hbm=cwqb, wk_hbm=cwkb, wv_hbm=cwvb,
                                   wo_hbm=cwob, tag=tag)

        for seq in range(nseq):
            emit_load_x(ctx, x_hbm, seq, streams)
            for c in range(NCH):
                prog.op("dve", lambda v, c=c: v.memset(t[f"XS{c}"][:, 0:1], 0.0),
                        writes=(f"XS{c}",))
            prog.cur_phase = f"l0ffnA{seq}"
            l0ffn_scope()
            prog.cur_phase = f"l0attn{seq}"
            attn_scope(bucket=True, tag=f"l0a{seq}")
            prog.cur_phase = f"l0ffnB{seq}"
            l0ffn_scope()
            for call in range(2):
                for c in range(NCH):
                    prog.op("sp", lambda g, c=c: g.dma_start(
                        out=spill[c * 128:(c + 1) * 128, :],
                        in_=t[f"XS{c}"][:, 1:1 + S]),
                        reads=(f"XS{c}",), writes=("hbm:spill",), stream=streams["sw"])
                prog.cur_phase = f"lnout{seq}_{call}"
                with ctx.scope():
                    ps("P6", [128, 512], F32)
                    ps("P7", [128, 512], F32)
                    ctx.ln_banks = ("P6", "P7")
                    emit_cast_x(ctx)
                    emit_ln(ctx, out_f32="XS")
                with ctx.scope():
                    for nm in ("TQ1", "TQ2", "TK1", "TK2"):
                        sb(nm, [128, S], BF16)
                    for i in range(6):
                        with ctx.scope():
                            sb("QP", [128, S], BF16)
                            sb("KP", [128, S], BF16)
                            sb("VP", [128, S], BF16)
                            sb("VTM0", [128, 16 * 65], BF16)
                            sb("VTM1", [128, 16 * 65], BF16)
                            sb("W4", [128, 1024], BF16)
                            sb("W5", [128, 1024], BF16)
                            sb("RN20", [2, TT], F32)
                            sb("RN21", [2, TT], F32)
                            sb("NM0", [128, 1], F32)
                            sb("NM1", [128, 1], F32)
                            for k_ in range(4, 8):
                                sb(f"SQ{k_}", [128, TT], BF16)
                            for k_ in range(2):
                                sb(f"PTT{k_}", [128, 384], BF16)
                            for ch in range(NCH):
                                sb(f"ZB{ch}", [128, S], BF16)
                            for k_ in range(4):
                                ps(f"PS{k_}", [128, 512], F32)
                            ps("PZ0", [65, 512], F32)
                            ps("PZ1", [65, 512], F32)
                            ps("PTB", [128, 1024], BF16)
                            prog.cur_phase = f"asub{seq}_{call}_{i}"
                            for vtn in ("VTM0", "VTM1"):
                                prog.op("dve", lambda v, vtn=vtn: v.memset(
                                    bass.AP(t[vtn], 64,
                                            [[1040, 128], [65, 16], [1, 1]]), 1.0),
                                    writes=(vtn,))
                            emit_tables(ctx, streams, tb_aps,
                                        t["QSC"][:, i:i + 1], t["QSPC"][:, i:i + 1],
                                        t["KSC"][:, i:i + 1], t["KSPC"][:, i:i + 1],
                                        param_regions=("QSC", "QSPC", "KSC", "KSPC"))
                            emit_cast_shift(ctx)
                            emit_attn_sublayer(
                                ctx, streams, swqkvb[i], swob[i],
                                t["QSC"][:, i:i + 1], t["QSPC"][:, i:i + 1],
                                t["KSC"][:, i:i + 1], t["KSPC"][:, i:i + 1],
                                param_regions=("QSC", "QSPC", "KSC", "KSPC"),
                                tag=f"s{seq}{call}{i}")
                        with ctx.scope():
                            for k_ in range(22):
                                sb(f"H{k_}", [128, 1024], BF16)
                            sb("W4", [128, 2816], BF16)
                            sb("W5", [128, 2816], BF16)
                            for k_ in range(4):
                                ps(f"P{k_}", [128, 512], F32)
                            ps("P4", [128, 512], F32)
                            ps("P5", [128, 512], F32)
                            ps("P6", [128, 512], F32)
                            ps("P7", [128, 512], F32)
                            ctx.ln_banks = ("P6", "P7", "P4", "P5")
                            prog.cur_phase = f"fsub{seq}_{call}_{i}"
                            emit_cast_shift(ctx)
                            ctx.ln_param_regions = ("LNG", "LNB")
                            emit_ln(ctx,
                                    g_col=lambda c, i=i: t["LNG"][:, 8 * i + c:8 * i + c + 1],
                                    b_col=lambda c, i=i: t["LNB"][:, 8 * i + c:8 * i + c + 1])
                            emit_ffn_gated(
                                ctx, sw1b[i],
                                lambda oc, i=i: t["B1C"][:, 44 * i + oc:44 * i + oc + 1],
                                sw2b[i],
                                lambda c, i=i: t["B2C"][:, 8 * i + c:8 * i + c + 1],
                                streams, None, None,
                                param_regions=("B1C", "B2C"), tag=f"f{seq}{call}{i}")
                ctx_sc = ctx.scope()
                ctx_sc.__enter__()
                sb("XOT0", [128, TT], F32)
                sb("XOT1", [128, TT], F32)
                for c in range(NCH):
                    for tt in range(NTT):
                        sl = slice(tt * TT, (tt + 1) * TT)
                        xsl = slice(1 + tt * TT, 1 + (tt + 1) * TT)
                        xo = "XOT0" if tt % 2 == 0 else "XOT1"
                        prog.op("sp", lambda g, c=c, sl=sl, xo=xo: g.dma_start(
                            out=t[xo][:, 0:TT], in_=spill[c * 128:(c + 1) * 128, sl]),
                            reads=("hbm:spill",), writes=(xo,), stream=streams["sw"])
                        prog.op("dve", lambda v, c=c, xsl=xsl, xo=xo:
                                v.scalar_tensor_tensor(
                                    t[f"XS{c}"][:, xsl], t[f"XS{c}"][:, xsl], 0.5,
                                    t[xo][:, 0:TT], op0=ALU.mult, op1=ALU.add),
                                reads=(f"XS{c}", xo), writes=(f"XS{c}",))
                ctx_sc.__exit__(None, None, None)
                if call == 0:
                    prog.cur_phase = f"l1attn{seq}"
                    attn_scope(bucket=False, tag=f"l1a{seq}")
            emit_store_x(ctx, y_hbm, seq, streams)
        prog.wait_region("sp", tuple(f"hbm:x{s_}" for s_ in range(nseq)))
        prog.finalize()
    _LAST_PROG = prog
    return nc


def host_pack_all(inputs):
    gi = lambda k: np.ascontiguousarray(np.asarray(inputs[k], np.float32))
    consts = host_constants()
    bT = lambda b, n: np.ascontiguousarray(np.asarray(b, np.float32).reshape(n, 128).T)

    m = {}
    m["l0ffn"] = dict(w1=pack_w(gi("L0_ffn_w1")), w2=pack_w(gi("L0_ffn_w2")),
                      b1T=bT(inputs["L0_ffn_b1"], 32), b2T=bT(inputs["L0_ffn_b2"], 8),
                      **consts)
    Wq, Wk = gi("L0_attn_Wq"), gi("L0_attn_Wk")
    R = gi("L0_attn_R")
    R_exp = np.zeros((1024, 128), np.float32)
    for h in range(16):
        R_exp[64 * h:64 * h + 64, 8 * h:8 * h + 8] = R.T
    rexp_host = np.ascontiguousarray(
        R_exp.reshape(8, 128, 128).transpose(1, 0, 2).reshape(128, 1024))
    m["l0attn"] = dict(
        wq=pack_w(Wq), wk=pack_w(Wk), wv=pack_w(gi("L0_attn_Wv")),
        wo=pack_w(gi("L0_attn_Wo")),
        wqT=pack_w(np.ascontiguousarray(Wq.T)),
        wkT=pack_w(np.ascontiguousarray(Wk.T)), rexp=rexp_host,
        bqT=bT(inputs["L0_attn_bq"], 8), bkT=bT(inputs["L0_attn_bk"], 8),
        bvT=bT(inputs["L0_attn_bv"], 8), boT=bT(inputs["L0_attn_bo"], 8),
        **consts)
    tohm = lambda W: np.ascontiguousarray(
        np.transpose(np.asarray(W, np.float32), (2, 0, 1)).reshape(1024, 1024))
    m["l1attn"] = dict(
        wq=pack_w(tohm(gi("L1_attn_W_Q"))), wk=pack_w(tohm(gi("L1_attn_W_K"))),
        wv=pack_w(tohm(gi("L1_attn_W_V"))),
        wo=pack_w(np.ascontiguousarray(gi("L1_attn_W_O").T)), **consts)
    Wqkv = gi("L1_ffn_Wqkv")
    Wol = gi("L1_ffn_Wo")
    qsc = gi("L1_ffn_q_scale")
    ksc = gi("L1_ffn_k_scale")
    W1 = gi("L1_ffn_W1")
    b1 = gi("L1_ffn_b1")
    W2 = gi("L1_ffn_W2")
    b2 = gi("L1_ffn_b2")
    lng = gi("L1_ffn_ln_g")
    lnb = gi("L1_ffn_ln_b")
    DI = 2730
    m["asub"] = []
    m["fsub"] = []
    st = lambda v: np.ascontiguousarray(np.concatenate([v, v])[:, None])
    for i in range(6):
        qs, ks = qsc[i], ksc[i]
        m["asub"].append(dict(
            wqkv=pack_w(Wqkv[i]), wo=pack_w(Wol[i]),
            qsT=st(qs), qspT=st(np.concatenate([qs[32:], qs[:32]])),
            ksT=st(ks), kspT=st(np.concatenate([ks[32:], ks[:32]])), **consts))
        W1p = np.zeros((1024, 5632), np.float32)
        W1p[:, 0:DI] = W1[i][:, 0:DI]
        W1p[:, 2816:2816 + DI] = W1[i][:, DI:]
        b1p = np.zeros(5632, np.float32)
        b1p[0:DI] = b1[i][:DI]
        b1p[2816:2816 + DI] = b1[i][DI:]
        W2p = np.zeros((2816, 1024), np.float32)
        W2p[0:DI] = W2[i]
        m["fsub"].append(dict(
            w1=pack_w(W1p), w2=pack_w(W2p), b1T=bT(b1p, 44), b2T=bT(b2[i], 8),
            lngT=bT(lng[i], 8), lnbT=bT(lnb[i], 8), **consts))
    m["lnout"] = dict(**consts)
    m["axpy"] = {}
    return m




def host_pack_fused(inputs):
    m = host_pack_all(inputs)
    consts = host_constants()
    out = {}
    out.update({"l0f_w1": m["l0ffn"]["w1"], "l0f_w2": m["l0ffn"]["w2"],
                "l0f_b1T": m["l0ffn"]["b1T"], "l0f_b2T": m["l0ffn"]["b2T"]})
    a = m["l0attn"]
    out.update({"l0a_wq": a["wq"], "l0a_wk": a["wk"], "l0a_wv": a["wv"],
                "l0a_wo": a["wo"], "l0a_wqT": a["wqT"], "l0a_wkT": a["wkT"],
                "l0a_rexp": a["rexp"], "l0a_bqT": a["bqT"], "l0a_bkT": a["bkT"],
                "l0a_bvT": a["bvT"], "l0a_boT": a["boT"]})
    c = m["l1attn"]
    out.update({"l1a_wq": c["wq"], "l1a_wk": c["wk"], "l1a_wv": c["wv"],
                "l1a_wo": c["wo"]})
    out["s_wqkv"] = np.stack([m["asub"][i]["wqkv"] for i in range(6)])
    out["s_wo"] = np.stack([m["asub"][i]["wo"] for i in range(6)])
    out["s_w1"] = np.stack([m["fsub"][i]["w1"] for i in range(6)])
    out["s_w2"] = np.stack([m["fsub"][i]["w2"] for i in range(6)])
    out["s_qsT"] = np.concatenate([m["asub"][i]["qsT"] for i in range(6)], 1)
    out["s_qspT"] = np.concatenate([m["asub"][i]["qspT"] for i in range(6)], 1)
    out["s_ksT"] = np.concatenate([m["asub"][i]["ksT"] for i in range(6)], 1)
    out["s_kspT"] = np.concatenate([m["asub"][i]["kspT"] for i in range(6)], 1)
    out["s_b1T"] = np.concatenate([m["fsub"][i]["b1T"] for i in range(6)], 1)
    out["s_b2T"] = np.concatenate([m["fsub"][i]["b2T"] for i in range(6)], 1)
    out["s_lngT"] = np.concatenate([m["fsub"][i]["lngT"] for i in range(6)], 1)
    out["s_lnbT"] = np.concatenate([m["fsub"][i]["lnbT"] for i in range(6)], 1)
    out.update(consts)
    return out


def kernel_fused(**inputs):
    x = np.asarray(inputs["L0_attn_x"], np.float32)
    wm = host_pack_fused(inputs)
    nc, innames = _get_prog("fused", build_fused)
    ims = []
    for c in range(NCORES):
        im = {k: v for k, v in wm.items() if k in innames}
        im["x"] = np.ascontiguousarray(x[2 * c:2 * c + 2].transpose(0, 2, 1))
        ims.append(im)
    res = run_bass_kernel_spmd(nc, ims, core_ids=list(range(NCORES)))
    out = np.empty((16, S, D), np.float32)
    for c in range(NCORES):
        out[2 * c:2 * c + 2] = res.results[c]["y"].transpose(0, 2, 1)
    return out


_PROG_CACHE = {}


def _get_prog(name, builder):
    if name not in _PROG_CACHE:
        nc = builder()
        innames = set()
        for alloc in nc.m.functions[0].allocations:
            if isinstance(alloc, mybir.MemoryLocationSet) and alloc.kind == "ExternalInput":
                innames.add(alloc.memorylocations[0].name)
        _PROG_CACHE[name] = (nc, innames)
    return _PROG_CACHE[name]


def kernel(**inputs):
    return kernel_fused(**inputs)


def kernel_unfused(**inputs):
    x = np.asarray(inputs["L0_attn_x"], np.float32)
    m = host_pack_all(inputs)
    nseq = 2
    cores = list(range(NCORES))
    xs = [np.ascontiguousarray(x[2 * c:2 * c + 2].transpose(0, 2, 1))
          for c in range(NCORES)]

    def launch(name, builder, wm, xkey="x", extra=None):
        nc, innames = _get_prog(name, builder)
        ims = []
        for c in range(NCORES):
            im = {k: v for k, v in wm.items() if k in innames}
            im[xkey] = xs[c]
            if extra is not None:
                im["yin"] = extra[c]
            ims.append(im)
        res = run_bass_kernel_spmd(nc, ims, core_ids=cores)
        return [res.results[c]["y"] for c in range(NCORES)]

    def set_x(out):
        for c in range(NCORES):
            xs[c] = out[c]

    set_x(launch("l0ffn", build_l0ffn, m["l0ffn"]))
    set_x(launch("l0attn", build_l0attn, m["l0attn"]))
    set_x(launch("l0ffn", build_l0ffn, m["l0ffn"]))
    for call in range(2):
        inner = launch("lnout", build_ln_out, m["lnout"])
        saved = xs
        xs = inner
        for i in range(6):
            set_x(launch("asub", build_l1ffn_asub, m["asub"][i]))
            set_x(launch("fsub", build_l1ffn_fsub, m["fsub"][i]))
        inner = xs
        xs = saved
        set_x(launch("axpy", build_axpy, m["axpy"], extra=inner))
        if call == 0:
            set_x(launch("l1attn", build_l1attn, m["l1attn"]))
    out = np.empty((16, S, D), np.float32)
    for c in range(NCORES):
        out[2 * c:2 * c + 2] = xs[c].transpose(0, 2, 1)
    return out



# version 12
# speedup vs baseline: 1.0665x; 1.0231x over previous
import math
import numpy as np
from contextlib import ExitStack

import concourse.bass as bass
import concourse.mybir as mybir
from concourse.bass_utils import run_bass_kernel_spmd

F32 = mybir.dt.float32
BF16 = mybir.dt.bfloat16
AF = mybir.ActivationFunctionType
ALU = mybir.AluOpType
AX = mybir.AxisListType

NCORES = 8
S = 2048
D = 1024
NCH = D // 128
TT = 512
NTT = S // TT
EPS = 1e-5
C_FULL = 40.0
OH_SCALE = 64.0


class Prog:
    ENGINES = ("pe", "dve", "act", "pool", "sp")

    def __init__(self, nc, es):
        self.nc = nc
        self.es = es
        self.eng = {"pe": "tensor", "dve": "vector", "act": "scalar",
                    "pool": "gpsimd", "sp": "sync"}
        self.sem = {}
        self.cnt = {}
        self.step = {}
        self.stream_q = {}
        self.frozen = set()
        self.max_waited = {}
        self.eclock = {e: {} for e in self.ENGINES}
        self.reg = {}
        self.queue = {e: [] for e in self.ENGINES}
        self.envp = None
        self.cur_phase = "init"
        self.pe_phases = []
        for e in ("pe", "dve", "act", "pool"):
            self._mksem("s_" + e, 1)

    def _mksem(self, name, step):
        self.sem[name] = self.es.enter_context(self.nc.semaphore(name))
        self.cnt[name] = 0
        self.step[name] = step

    def freeze_stream(self, name):
        self.frozen.add(name)

    def stream(self, name, qeng):
        assert qeng in ("sp", "pool", "act")
        self._mksem(name, 16)
        self.stream_q[name] = qeng
        return name

    def _region(self, key):
        r = self.reg.get(key)
        if r is None:
            r = {"w": {}, "r": {}}
            if "@" in key:
                parent = self.reg.get(key.split("@", 1)[0])
                if parent is not None:
                    r["w"] = dict(parent["w"])
            self.reg[key] = r
        return r

    def op(self, e, fn, reads=(), writes=(), stream=None):
        need = {}
        wr = [self._region(k) for k in writes]
        for k in reads:
            for s, c in self._region(k)["w"].items():
                if need.get(s, 0) < c:
                    need[s] = c
        for r in wr:
            for m in (r["w"], r["r"]):
                for s, c in m.items():
                    if need.get(s, 0) < c:
                        need[s] = c
        ec = self.eclock[e]
        waits = []
        for s in sorted(need):
            c = need[s]
            if s in self.stream_q:
                c = self.cnt[s]
            if ec.get(s, 0) < c:
                waits.append((s, c))
                ec[s] = c
                if s in self.stream_q and self.max_waited.get(s, 0) < c:
                    self.max_waited[s] = c
        if stream is not None:
            w = self.max_waited.get(stream, 0)
            if ec.get(stream, 0) < w:
                waits.append((stream, w))
                ec[stream] = w
        for k in reads:
            for s, c in self._region(k)["w"].items():
                if ec.get(s, 0) < c:
                    ec[s] = c
        if stream is None:
            sname = "s_" + e
        else:
            sname = stream
            assert self.stream_q[stream] == e, (stream, e)
            assert stream not in self.frozen, f"issue on frozen stream {stream}"
        self.cnt[sname] += self.step[sname]
        ident = self.cnt[sname]
        env = self.envp.env() if self.envp is not None else None
        self.queue[e].append((waits, fn, sname, env))
        if e == "pe":
            self.pe_phases.append(self.cur_phase)
        wclock = dict(ec)
        wclock[sname] = max(wclock.get(sname, 0), ident)
        for r in wr:
            r["w"] = wclock
            r["r"] = {}
        for k in reads:
            rr = self._region(k)["r"]
            if rr.get(sname, 0) < ident:
                rr[sname] = ident
        return (sname, ident)

    def wait_all(self, e, items):
        ec = self.eclock[e]
        waits = []
        for s, c in items:
            if ec.get(s, 0) < c:
                waits.append((s, c))
                ec[s] = c
        if waits:
            self.queue[e].append((waits, None, None, None))

    def wait_region(self, e, keys):
        items = {}
        for k in keys:
            for s, c in self._region(k)["w"].items():
                if s in self.stream_q:
                    c = self.cnt[s]
                if items.get(s, 0) < c:
                    items[s] = c
        self.wait_all(e, sorted(items.items()))

    def finalize(self):
        block = self.es.enter_context(self.nc.Block())
        for e in self.ENGINES:
            q = self.queue[e]
            if not q:
                continue
            bass_eng = getattr(self.nc, self.eng[e])

            def run(engine, q=q, prog=self, e=e):
                if e == "pe":
                    cnt = [0]
                    class _Px:
                        def __getattr__(px, k):
                            a = getattr(engine, k)
                            if k in ("matmul", "transpose"):
                                def w(*ar, **kw):
                                    cnt[0] += 1
                                    return a(*ar, **kw)
                                return w
                            return a
                    px = _Px()
                    prog.pe_mm_cum = []
                    for waits, fn, sname, env in q:
                        for s, c in waits:
                            engine.wait_ge(prog.sem[s], c)
                        if fn is not None:
                            if env is not None:
                                prog.envp.restore(env)
                            instr = fn(px)
                            instr.then_inc(prog.sem[sname], prog.step[sname])
                        prog.pe_mm_cum.append(cnt[0])
                    return
                for waits, fn, sname, env in q:
                    for s, c in waits:
                        engine.wait_ge(prog.sem[s], c)
                    if fn is not None:
                        if env is not None:
                            prog.envp.restore(env)
                        instr = fn(engine)
                        instr.then_inc(prog.sem[sname], prog.step[sname])

            getattr(block, self.eng[e])(run)


class Ctx:
    def __init__(self, nc, es, prog):
        self.nc = nc
        self.es = es
        self.p = prog
        self.t = {}
        self.xs_off = 0
        self._scopes = []
        self._dead = []
        self._snaps = []
        self._dirty = True
        self._cur_env = -1
        prog.envp = self

    def env(self):
        if self._dirty:
            self._snaps.append(dict(self.t))
            self._dirty = False
        return len(self._snaps) - 1

    def restore(self, env):
        if env != self._cur_env:
            self.t.clear()
            self.t.update(self._snaps[env])
            self._cur_env = env

    def _addr(self, name):
        for f in self.nc.m.functions:
            for al in f.allocations:
                if isinstance(al, mybir.MemoryLocationSet):
                    ml = al.memorylocations[0]
                    if ml.name == name:
                        return ml.addr, ml.addr + ml.size()
        raise KeyError(name)

    _uid = 0

    def _alloc(self, kind, name, shape, dtype):
        es = self._scopes[-1][0] if self._scopes else self.es
        mk = self.nc.sbuf_tensor if kind == "sb" else self.nc.psum_tensor
        real = name
        if self._scopes:
            Ctx._uid += 1
            real = f"{name}_s{Ctx._uid}"
        h = es.enter_context(mk(real, shape, dtype))
        self.t[name] = h
        self._dirty = True
        lo, hi = self._addr(real)
        space = "sb" if kind == "sb" else "ps"
        if self._scopes:
            self._scopes[-1][1].append((name, space, lo, hi))
        clk = {}
        keep = []
        for (sp2, lo2, hi2, c2) in self._dead:
            if sp2 == space and lo < hi2 and lo2 < hi:
                for s, c in c2.items():
                    if clk.get(s, 0) < c:
                        clk[s] = c
            keep.append((sp2, lo2, hi2, c2))
        self._dead = keep
        if clk:
            r = self.p._region(name)
            for s, c in clk.items():
                if r["w"].get(s, 0) < c:
                    r["w"][s] = c
        return h

    def sb(self, name, shape, dtype):
        return self._alloc("sb", name, shape, dtype)

    def ps(self, name, shape, dtype):
        return self._alloc("ps", name, shape, dtype)

    def wload(self, wk, src_ap, cols=None):
        t = self.t
        ncols = cols if cols is not None else src_ap.shape[-1]
        rkey = "hbm:" + src_ap.tensor.name
        if src_ap.dtype == BF16:
            self.p.op("sp", lambda g, wk=wk, src_ap=src_ap, ncols=ncols:
                      g.dma_start(out=t[wk][:, 0:ncols], in_=src_ap),
                      reads=(rkey,), writes=(wk,), stream="d_" + wk)
        else:
            self.p.op("pool", lambda g, wk=wk, src_ap=src_ap, ncols=ncols:
                      g.dma_start(out=t[wk][:, 0:ncols], in_=src_ap),
                      reads=(rkey,), writes=(wk,), stream="dp_" + wk)

    def scope(self):
        from contextlib import contextmanager

        @contextmanager
        def _sc():
            es = ExitStack()
            self._scopes.append((es, []))
            try:
                yield
            finally:
                es2, names = self._scopes.pop()
                for (name, space, lo, hi) in names:
                    clk = {}
                    for key in [name] + [k for k in self.p.reg
                                         if k.startswith(name + "@")]:
                        r = self.p._region(key)
                        for m in (r["w"], r["r"]):
                            for s, c in m.items():
                                if clk.get(s, 0) < c:
                                    clk[s] = c
                        self.p.reg.pop(key, None)
                    self._dead.append((space, lo, hi, clk))
                    self.t.pop(name, None)
                self._dirty = True
                es2.close()
        return _sc()


def pack_w(w):
    Kd, Nd = w.shape
    nk, no = Kd // 128, Nd // 128
    return np.ascontiguousarray(
        w.reshape(nk, 128, no, 128).transpose(2, 1, 0, 3).reshape(no, 128, nk * 128))


def host_constants():
    c = {}
    c["CONST_ONES"] = np.ones((128, 128), np.float32)
    c["CONST_IDENT"] = np.eye(128, dtype=np.float32)
    c["CONST_IDENT2"] = np.vstack([np.eye(64, dtype=np.float32)] * 2)
    kk, qq = np.meshgrid(np.arange(128), np.arange(128), indexing="ij")
    c["CONST_TRIU"] = (qq >= kk).astype(np.float32)
    c["CONST_TRIL"] = (qq <= kk).astype(np.float32)
    DH, WIN = 64, 256
    half = np.arange(0, DH, 2, dtype=np.float32)
    inv_freq = 1.0 / 10000.0 ** (half / DH)
    tpos = np.arange(S, dtype=np.float32)
    freqs = tpos[:, None] * inv_freq[None, :]
    freqs = np.concatenate([freqs, freqs], -1)
    base_scale = (half + 0.4 * DH) / (1.4 * DH)
    power = (tpos - S // 2) / (WIN // 2)
    scale = base_scale[None, :] ** power[:, None]
    scale = np.concatenate([scale, scale], -1)
    cosf, sinf = np.cos(freqs), np.sin(freqs)
    st2 = lambda a: np.ascontiguousarray(np.vstack([a.T, a.T]).astype(np.float32))
    c["CONST_TBQC"] = st2(cosf * scale)
    c["CONST_TBQS"] = st2(sinf * scale)
    c["CONST_TBKC"] = st2(cosf / scale)
    c["CONST_TBKS"] = st2(sinf / scale)
    Rh = np.zeros((128, 128), np.float32)
    for blk in range(2):
        for m in range(64):
            if m < 32:
                Rh[blk * 64 + m, blk * 64 + m + 32] = -1.0
            else:
                Rh[blk * 64 + m, blk * 64 + m - 32] = 1.0
    c["CONST_RHT"] = np.ascontiguousarray(Rh.T)
    e2 = np.zeros((128, 2), np.float32)
    e2[0:64, 0] = 1.0
    e2[64:128, 1] = 1.0
    c["CONST_EHD2"] = e2
    c["CONST_E2T"] = np.ascontiguousarray(e2.T)
    return c


def make_std(ctx, n_w_tiles=8, w_cols=1024, psum=True, xs_cols=S):
    sb, ps = ctx.sb, ctx.ps
    for i in range(NCH):
        sb(f"XS{i}", [128, xs_cols], F32)
        sb(f"XN{i}", [128, S], BF16)
    sb("ONESb", [128, 128], BF16)
    sb("U0", [128, TT], BF16)
    sb("U1", [128, TT], BF16)
    sb("ONESf", [128, 128], F32)
    sb("IDENTb", [128, 128], BF16)
    sb("IDENT2b", [128, 64], BF16)

    for i in range(4):
        sb(f"SQ{i}", [128, TT], BF16)
    sb("CONSTS", [128, 8], F32)
    sb("AROW", [1, TT], F32)
    sb("BROW0", [1, TT], F32)
    sb("BROW1", [1, TT], F32)
    sb("MU", [1, TT], F32)
    sb("VAR", [1, TT], F32)
    for i in range(n_w_tiles):
        sb(f"W{i}", [128, w_cols], BF16)
    if psum:
        for i in range(8):
            ps(f"P{i}", [128, 512], F32)


CONST_COLS = {"eps": 0, "negC": 1, "zero": 2, "negBIGC": 4, "negCfull": 5}


def load_constants(ctx, streams):
    p = ctx.p
    nc = ctx.nc
    cc = ctx.t["CONSTS"]
    p.op("dve", lambda v: v.memset(cc[:, 0:1], EPS), writes=("CONSTS",))
    p.op("dve", lambda v: v.memset(cc[:, 1:2], -C_FULL), writes=("CONSTS",))
    p.op("dve", lambda v: v.memset(cc[:, 2:3], 0.0), writes=("CONSTS",))
    p.op("dve", lambda v: v.memset(cc[:, 4:5], -(OH_SCALE * OH_SCALE + C_FULL)), writes=("CONSTS",))
    p.op("dve", lambda v: v.memset(cc[:, 5:6], -C_FULL), writes=("CONSTS",))
    p.op("dve", lambda v: v.memset(cc[:, 6:7], 1e-24), writes=("CONSTS",))
    ones = nc.dram_tensor("CONST_ONES", [128, 128], F32, kind="ExternalInput").ap()
    ident = nc.dram_tensor("CONST_IDENT", [128, 128], F32, kind="ExternalInput").ap()
    t = ctx.t
    p.op("pool", lambda g: g.dma_start(out=t["ONESb"][:], in_=ones),
         reads=("c:ONES",), writes=("ONESb",), stream=streams["wg"])
    p.op("sp", lambda g: g.dma_start(out=t["ONESf"][:], in_=ones),
         reads=("c:ONES",), writes=("ONESf",), stream=streams["setup"])
    p.op("pool", lambda g: g.dma_start(out=t["IDENTb"][:], in_=ident),
         reads=("c:IDENT",), writes=("IDENTb",), stream=streams["wg"])
    ident2 = nc.dram_tensor("CONST_IDENT2", [128, 64], F32, kind="ExternalInput").ap()
    p.op("pool", lambda g: g.dma_start(out=t["IDENT2b"][:], in_=ident2),
         reads=("c:IDENT2",), writes=("IDENT2b",), stream=streams["wg"])


def emit_load_x(ctx, x_hbm, seq, streams, xs_prefix="XS", xs_off=None):
    p, t = ctx.p, ctx.t
    if xs_off is None:
        xs_off = ctx.xs_off
    for c in range(NCH):
        dst = t[f"{xs_prefix}{c}"]
        src = x_hbm[seq, c * 128:(c + 1) * 128, :]
        p.op("sp", lambda g, dst=dst, src=src, xs_off=xs_off: g.dma_start(
            out=dst[:, xs_off:xs_off + S], in_=src),
            reads=(f"hbm:x{seq}",), writes=(f"{xs_prefix}{c}",), stream=streams["xs"])


def emit_store_x(ctx, x_hbm, seq, streams, xs_prefix="XS", xs_off=None):
    p, t = ctx.p, ctx.t
    if xs_off is None:
        xs_off = ctx.xs_off
    for c in range(NCH):
        src = t[f"{xs_prefix}{c}"]
        dst = x_hbm[seq, c * 128:(c + 1) * 128, :]
        p.op("sp", lambda g, dst=dst, src=src, xs_off=xs_off: g.dma_start(
            out=dst, in_=src[:, xs_off:xs_off + S]),
            reads=(f"{xs_prefix}{c}",), writes=(f"hbm:x{seq}",), stream=streams["xs"])


def emit_cast_x(ctx, xs="XS", xn="XN"):
    p, t = ctx.p, ctx.t
    off = ctx.xs_off
    for c in range(NCH):
        src, dst = t[f"{xs}{c}"], t[f"{xn}{c}"]
        eng = "pool" if c % 2 == 0 else "dve"
        p.op(eng, lambda v, src=src, dst=dst, off=off: v.tensor_copy(
            dst[:, 0:S], src[:, off:off + S]),
            reads=(f"{xs}{c}",),
            writes=tuple(f"{xn}{c}@{i}" for i in range(NTT)))


def emit_ln(ctx, xn="XN", g_col=None, b_col=None, out=None, out_f32=None,
           srow=False, tile_hook=None):
    p, t = ctx.p, ctx.t
    banks = getattr(ctx, "ln_banks", ("P6", "P7"))
    out = out or xn
    def _banks(tt):
        if len(banks) >= 4:
            return banks[2 * (tt % 2)], banks[2 * (tt % 2) + 1]
        return banks[0], banks[1]

    def stage1(tt):
        LB0, LB1 = _banks(tt)
        sl = slice(tt * TT, (tt + 1) * TT)
        def mm_s1(eng, sl=sl, LB0=LB0):
            last = None
            for c in range(NCH):
                last = eng.matmul(t[LB0][0:1, 0:TT], t["ONESb"][:, 0:1],
                                  t[f"{xn}{c}"][:, sl],
                                  start=(c == 0), stop=(c == NCH - 1))
            return last
        p.op("pe", mm_s1,
             reads=tuple(f"{xn}{c}@{tt}" for c in range(NCH)) + ("ONESb",),
             writes=(LB0,))
        for half2 in range(2):
            for cc in range(4):
                c = 4 * half2 + cc
                p.op("dve", lambda v, c=c, cc=cc, sl=sl: v.tensor_tensor(
                    t[f"SQ{cc}"][:, 0:TT], t[f"{xn}{c}"][:, sl], t[f"{xn}{c}"][:, sl],
                    op=ALU.mult),
                    reads=(f"{xn}{c}@{tt}",), writes=(f"SQ{cc}",))
            def mm_s2(eng, half2=half2, LB1=LB1):
                last = None
                for cc in range(4):
                    last = eng.matmul(t[LB1][0:1, 0:TT], t["ONESb"][:, 0:1],
                                      t[f"SQ{cc}"][:, 0:TT],
                                      start=(half2 == 0 and cc == 0),
                                      stop=(half2 == 1 and cc == 3))
                return last
            p.op("pe", mm_s2, reads=tuple(f"SQ{cc}" for cc in range(4)) + ("ONESb",),
                 writes=(LB1,))

    def stage2(tt):
        LB0, LB1 = _banks(tt)
        sl = slice(tt * TT, (tt + 1) * TT)
        p.op("dve", lambda v, LB0=LB0: v.tensor_scalar(
            t["MU"][:], t[LB0][0:1, 0:TT], 1.0 / D, None, op0=ALU.mult),
            reads=(LB0,), writes=("MU",))
        p.op("dve", lambda v: v.tensor_tensor(
            t["AROW"][:], t["MU"][:], t["MU"][:], op=ALU.mult),
            reads=("MU",), writes=("AROW",))
        p.op("dve", lambda v, LB1=LB1: v.scalar_tensor_tensor(
            t["VAR"][:], t[LB1][0:1, 0:TT], 1.0 / D, t["AROW"][:],
            op0=ALU.mult, op1=ALU.subtract),
            reads=(LB1, "AROW"), writes=("VAR",))
        p.op("act", lambda s_: s_.activation(
            t["VAR"][:], t["VAR"][:], AF.Ln, bias=t["CONSTS"][0:1, 0:1]),
            reads=("VAR", "CONSTS"), writes=("VAR",))
        p.op("act", lambda s_: s_.activation(
            t["AROW"][:], t["VAR"][:], AF.Exp, scale=-0.5),
            reads=("VAR",), writes=("AROW",))
        if srow:
            p.op("act", lambda s_: s_.activation(
                t["SROW"][:], t["VAR"][:], AF.Exp, scale=0.5),
                reads=("VAR",), writes=("SROW",))
        p.op("dve", lambda v: v.scalar_tensor_tensor(
            t["BROW0"][:, 0:TT], t["MU"][:], -1.0, t["AROW"][:],
            op0=ALU.mult, op1=ALU.mult),
            reads=("MU", "AROW"), writes=("BROW0",))
        p.op("pe", lambda eng, LB0=LB0: eng.matmul(
            t[LB0][:, 0:TT], t["ONESf"][0:1, 0:128], t["AROW"][:],
            start=True, stop=True), reads=("AROW", "ONESf"), writes=(LB0,))
        p.op("pe", lambda eng, LB1=LB1: eng.matmul(
            t[LB1][:, 0:TT], t["ONESf"][0:1, 0:128], t["BROW0"][:, 0:TT],
            start=True, stop=True), reads=("BROW0", "ONESf"), writes=(LB1,))

    def stage2b(tt):
        LB0, LB1 = _banks(tt)
        sl = slice(tt * TT, (tt + 1) * TT)
        for c in range(NCH):
            u = t["U0"] if c % 2 == 0 else t["U1"]
            uk = "U0" if c % 2 == 0 else "U1"
            p.op("dve", lambda v, c=c, u=u, sl=sl, LB0=LB0: v.tensor_tensor(
                u[:, 0:TT], t[f"{xn}{c}"][:, sl], t[LB0][:, 0:TT], op=ALU.mult),
                reads=(f"{xn}{c}@{tt}", LB0), writes=(uk,))
            if out_f32 is not None:
                osl = slice(ctx.xs_off + sl.start, ctx.xs_off + sl.stop)
                p.op("dve", lambda v, c=c, u=u, osl=osl, LB1=LB1: v.tensor_tensor(
                    t[f"{out_f32}{c}"][:, osl], u[:, 0:TT], t[LB1][:, 0:TT], op=ALU.add),
                    reads=(uk, LB1), writes=(f"{out_f32}{c}",))
            if g_col is None:
                p.op("dve", lambda v, c=c, u=u, sl=sl, LB1=LB1: v.tensor_tensor(
                    t[f"{out}{c}"][:, sl], u[:, 0:TT], t[LB1][:, 0:TT], op=ALU.add),
                    reads=(uk, LB1), writes=(f"{out}{c}@{tt}",))
            else:
                p.op("dve", lambda v, c=c, u=u, sl=sl, LB1=LB1: v.tensor_tensor(
                    u[:, 0:TT], u[:, 0:TT], t[LB1][:, 0:TT], op=ALU.add),
                    reads=(uk, LB1), writes=(uk,))
                gc, bc, preg = g_col(c), b_col(c), getattr(ctx, "ln_param_regions", ())
                p.op("dve", lambda v, c=c, u=u, gc=gc, bc=bc, sl=sl: v.tensor_scalar(
                    t[f"{out}{c}"][:, sl], u[:, 0:TT], gc, bc, op0=ALU.mult, op1=ALU.add),
                    reads=(uk,) + tuple(preg), writes=(f"{out}{c}@{tt}",))
        if tile_hook is not None:
            tile_hook(tt)

    if len(banks) >= 4:
        for k2 in range(NTT + 2):
            if k2 >= 2:
                stage2b(k2 - 2)
            if 1 <= k2 <= NTT:
                stage2(k2 - 1)
            if k2 < NTT:
                stage1(k2)
    else:
        for tt in range(NTT):
            stage1(tt)
            stage2(tt)
            stage2b(tt)


def emit_ffn_l0(ctx, w1_hbm, b1_col, w2_hbm, b2h_col, streams, scale=0.5,
                bias_regions=(), tag=""):
    p, t = ctx.p, ctx.t
    NH1 = 4096 // 128
    HW = S // 2
    for half in range(2):
        hsl = slice(half * HW, (half + 1) * HW)
        for c4 in range(NH1):
            wk = f"W{c4 % 4}"
            wt = t[wk]
            ctx.wload(wk, w1_hbm[c4], 1024)
            for tt in range(2):
                sl = slice(half * HW + tt * TT, half * HW + (tt + 1) * TT)
                pb = f"P{(2 * c4 + tt) % 4}"
                def mm(eng, wt=wt, sl=sl, pb=pb):
                    last = None
                    for k in range(NCH):
                        last = eng.matmul(t[pb][:, 0:TT], wt[:, k * 128:(k + 1) * 128],
                                          t[f"XN{k}"][:, sl],
                                          start=(k == 0), stop=(k == NCH - 1))
                    return last
                p.op("pe", mm,
                     reads=tuple(f"XN{k}@{sl.start // TT}" for k in range(NCH)) + (wk,),
                     writes=(pb,))
                bc = b1_col(c4)
                p.op("act", lambda s_, c4=c4, tt=tt, pb=pb, bc=bc: s_.activation(
                    t[f"H{c4}"][:, tt * TT:(tt + 1) * TT], t[pb][:, 0:TT],
                    AF.Gelu, bias=bc),
                    reads=(pb,) + bias_regions, writes=(f"H{c4}",))
        for c in range(NCH):
            for hw2 in range(2):
                ctx.wload(f"W{4 + hw2}", w2_hbm[c][:, hw2 * 2048:(hw2 + 1) * 2048], 2048)
            for tt in range(2):
                sl = slice(half * HW + tt * TT, half * HW + (tt + 1) * TT)
                pb = f"P{(2 * c + tt) % 2 + 4}" if False else f"P{2 + tt % 2}"
                def mm(eng, tt=tt, pb=pb):
                    last = None
                    for k in range(NH1):
                        wt2 = t[f"W{4 + k // 16}"]
                        last = eng.matmul(t[pb][:, 0:TT],
                                          wt2[:, (k % 16) * 128:(k % 16 + 1) * 128],
                                          t[f"H{k}"][:, tt * TT:(tt + 1) * TT],
                                          start=(k == 0), stop=(k == NH1 - 1))
                    return last
                p.op("pe", mm, reads=tuple(f"H{k}" for k in range(NH1)) + ("W4", "W5"),
                     writes=(pb,))
                bc = b2h_col(c)
                u = t["U0"] if tt % 2 == 0 else t["U1"]
                uk = "U0" if tt % 2 == 0 else "U1"
                p.op("act", lambda s_, pb=pb, u=u, bc=bc: s_.activation(
                    u[:, 0:TT], t[pb][:, 0:TT], AF.Identity, bias=bc, scale=scale),
                    reads=(pb,) + bias_regions, writes=(uk,))
                xsl = slice(ctx.xs_off + sl.start, ctx.xs_off + sl.stop)
                p.op("pool", lambda g, c=c, u=u, xsl=xsl: g.tensor_tensor(
                    t[f"XS{c}"][:, xsl], t[f"XS{c}"][:, xsl], u[:, 0:TT], op=ALU.add),
                    reads=(f"XS{c}", uk), writes=(f"XS{c}",))




def emit_qr_onehot(ctx, tile, streams):
    p, t = ctx.p, ctx.t
    xo = ctx.xs_off
    for sub in range(4):
        ts0 = tile * TT + sub * 128
        for side, pcol in (("Q", 0), ("K", 128)):
            def mm(eng, ts0=ts0, side=side, pcol=pcol, sub=sub):
                last = None
                for c in range(NCH):
                    last = eng.matmul(
                        t["PS1"][:, pcol:pcol + 128],
                        t[f"XS{c}"][:, xo + ts0:xo + ts0 + 128],
                        t[f"WR{side}"][:, c * 128:(c + 1) * 128],
                        start=(c == 0), stop=False)
                eng.matmul(t["PS1"][:, pcol:pcol + 128],
                           t["MU"][0:1, sub * 128:sub * 128 + 128],
                           t[f"NCS{side}"][0:1, 0:128], start=False, stop=False)
                last = eng.matmul(t["PS1"][:, pcol:pcol + 128],
                                  t["SROW"][0:1, sub * 128:sub * 128 + 128],
                                  t[f"B{side}R"][0:1, 0:128], start=False, stop=True)
                return last
            p.op("pe", mm,
                 reads=tuple(f"XS{c}" for c in range(NCH)) +
                       (f"WR{side}", "MU", "SROW", f"NCS{side}", f"B{side}R"),
                 writes=("PS1",))
            p.op("dve", lambda v, pcol=pcol: v.tensor_reduce(
                t["M16"][:, 0:16],
                bass.AP(t["PS1"], pcol, [[512, 128], [8, 16], [1, 8]]),
                axis=AX.X, op=ALU.max),
                reads=("PS1",), writes=("M16",))
            p.op("dve", lambda v, pcol=pcol: v.tensor_tensor(
                t["OHSB"][:, 0:128],
                bass.AP(t["PS1"], pcol, [[512, 128], [8, 16], [1, 8]]),
                bass.AP(t["M16"], 0, [[16, 128], [1, 16], [0, 8]]),
                op=ALU.is_ge),
                reads=("PS1", "M16"), writes=("OHSB",))
            p.op("dve", lambda v: v.tensor_scalar(
                t["OHSB"][:, 0:128], t["OHSB"][:, 0:128], OH_SCALE, None, op0=ALU.mult),
                reads=("OHSB",), writes=("OHSB",))
            p.op("sp", lambda g, ts0=ts0, side=side: g.dma_start(
                out=t[f"TRANS{side}"][:, ts0:ts0 + 128], in_=t["OHSB"][:, 0:128],
                transpose=True),
                reads=("OHSB",), writes=(f"TRANS{side}",), stream=streams["tr"])


def emit_full_attn(ctx, streams, *, bucket, causal, wq_hbm, wk_hbm, wv_hbm,
                   wo_hbm, bq8_col=None, bk_col=None, bv_col=None, bo_col=None,
                   param_regions=(), tag=""):
    p, t = ctx.p, ctx.t
    KAUG = 72 if bucket else 64
    eshift = 4 if bucket else 5
    nsp = 0
    for ch in range(NCH):
        for wi, (w_hbm, dst, key) in enumerate(
                ((wq_hbm, "QP", "q"), (wk_hbm, "KP", "k"), (wv_hbm, "VP", "v"))):
            wk_ = f"W{2 * wi + ch % 2}"
            wt = t[wk_]
            ctx.wload(wk_, w_hbm[ch], 1024)
            for tt in range(NTT):
                sl = slice(tt * TT, (tt + 1) * TT)
                pb = f"PS{(wi * NTT + tt) % 3}"
                def mm(eng, wt=wt, sl=sl, pb=pb):
                    last = None
                    for k in range(NCH):
                        last = eng.matmul(t[pb][:, 0:TT], wt[:, k * 128:(k + 1) * 128],
                                          t[f"XN{k}"][:, sl],
                                          start=(k == 0), stop=(k == NCH - 1))
                    return last
                p.op("pe", mm,
                     reads=tuple(f"XN{k}@{sl.start // TT}" for k in range(NCH)) + (wk_,),
                     writes=(pb,))
                if key == "q":
                    bc = bq8_col(ch) if bq8_col is not None else None
                    if bc is not None:
                        p.op("act", lambda s_, pb=pb, sl=sl, bc=bc: s_.activation(
                            t["QP"][:, sl], t[pb][:, 0:TT], AF.Identity,
                            bias=bc, scale=0.125),
                            reads=(pb,) + param_regions, writes=("QP",))
                    else:
                        p.op("act", lambda s_, pb=pb, sl=sl: s_.activation(
                            t["QP"][:, sl], t[pb][:, 0:TT], AF.Identity,
                            bias=0.0, scale=0.125),
                            reads=(pb,), writes=("QP",))
                else:
                    dst_ = "KP" if key == "k" else "VP"
                    bc = bk_col(ch) if (key == "k" and bk_col) else (
                        bv_col(ch) if (key == "v" and bv_col) else None)
                    if bc is not None:
                        p.op("act", lambda s_, pb=pb, sl=sl, bc=bc, dst_=dst_:
                             s_.activation(t[dst_][:, sl], t[pb][:, 0:TT],
                                           AF.Identity, bias=bc),
                             reads=(pb,) + param_regions, writes=(dst_,))
                    else:
                        p.op("act", lambda s_, pb=pb, sl=sl, dst_=dst_: s_.copy(
                            t[dst_][:, sl], t[pb][:, 0:TT]),
                            reads=(pb,), writes=(dst_,))
        for hh in range(2):
            h = 2 * ch + hh
            b = 64 * hh
            vt = "VTM0"
            for j0 in range(4):
                def trs(eng, b=b, j0=j0):
                    last = None
                    for jj in range(4):
                        j = 4 * j0 + jj
                        last = eng.transpose(t["PTB"][:, jj * 64:(jj + 1) * 64],
                                             t["VP"][b:b + 64, j * 128:(j + 1) * 128],
                                             t["IDENT2b"][b:b + 64, 0:64])
                    return last
                p.op("pe", trs, reads=("VP", "IDENT2b"), writes=("PTB",))
                p.op("dve", lambda v, vt=vt, j0=j0: v.tensor_copy(
                    bass.AP(t[vt], 4 * j0 * 65, [[1040, 128], [65, 4], [1, 64]]),
                    bass.AP(t["PTB"], 0, [[1024, 128], [64, 4], [1, 64]])),
                    reads=("PTB",), writes=(vt,))
            if bucket:
                p.op("sp", lambda g, b=b: g.dma_start(
                    out=t["AUGQ"][0:64, :], in_=t["QP"][b:b + 64, :]),
                    reads=("QP",), writes=("AUGQ",), stream=streams["tr"])
                p.op("sp", lambda g, b=b: g.dma_start(
                    out=t["AUGK"][0:64, :], in_=t["KP"][b:b + 64, :]),
                    reads=("KP",), writes=("AUGK",), stream=streams["tr"])
                p.op("sp", lambda g, h=h: g.dma_start(
                    out=t["AUGQ"][64:72, :], in_=t["TRANSQ"][8 * h:8 * h + 8, :]),
                    reads=("TRANSQ",), writes=("AUGQ",), stream=streams["tr"])
                p.op("sp", lambda g, h=h: g.dma_start(
                    out=t["AUGK"][64:72, :], in_=t["TRANSK"][8 * h:8 * h + 8, :]),
                    reads=("TRANSK",), writes=("AUGK",), stream=streams["tr"])
            iters = [(j, qb) for j in range(16) for qb in range(4)
                     if not (causal and j > 4 * qb + 3)]

            def emit_s(idx):
                j, qb = iters[idx]
                qsl = slice(qb * TT, (qb + 1) * TT)
                pb = f"PS{idx % 3}"
                if bucket:
                    p.op("pe", lambda eng, j=j, qsl=qsl, pb=pb: eng.matmul(
                        t[pb][:, 0:TT], t["AUGK"][0:KAUG, j * 128:(j + 1) * 128],
                        t["AUGQ"][0:KAUG, qsl], start=True, stop=True),
                        reads=("AUGK", "AUGQ"), writes=(pb,))
                else:
                    p.op("pe", lambda eng, j=j, qsl=qsl, pb=pb, b=b: eng.matmul(
                        t[pb][:, 0:TT], t["KP"][b:b + 64, j * 128:(j + 1) * 128],
                        t["QP"][b:b + 64, qsl], start=True, stop=True),
                        reads=("KP", "QP"), writes=(pb,))

            def emit_rest(idx, nsp):
                j, qb = iters[idx]
                pb = f"PS{idx % 3}"
                sp = f"SQ{nsp % 3}"
                p.op("act", lambda s_, pb=pb, sp=sp: s_.activation(
                    t[sp][:, 0:TT], t[pb][:, 0:TT], AF.Exp,
                    bias=t["CONSTS"][:, eshift:eshift + 1]),
                    reads=(pb, "CONSTS"), writes=(sp,))
                if causal and j >= 4 * qb:
                    c0 = j * 128 - qb * TT
                    if c0 > 0:
                        p.op("pool", lambda g, sp=sp, c0=c0: g.memset(
                            t[sp][:, 0:c0], 0.0), reads=(), writes=(sp,))
                    p.op("pool", lambda g, sp=sp, c0=c0: g.tensor_tensor(
                        t[sp][:, c0:c0 + 128], t[sp][:, c0:c0 + 128],
                        t["TRIUb"][:, 0:128], op=ALU.mult),
                        reads=("TRIUb", sp), writes=(sp,))
                jlast = (4 * qb + 3) if causal else 15
                p.op("pe", lambda eng, vt=vt, j=j, qb=qb, sp=sp,
                     jlast=jlast: eng.matmul(
                         t["PZ"][:, qb * TT:(qb + 1) * TT],
                         t[vt][:, j * 65:j * 65 + 65], t[sp][:, 0:TT],
                         start=(j == 0), stop=(j == jlast)),
                     reads=(vt, sp), writes=("PZ",))

            for idx in range(len(iters) + 2):
                if idx < len(iters):
                    emit_s(idx)
                if idx >= 2:
                    emit_rest(idx - 2, nsp)
                    nsp += 1
            for qb in range(4):
                qsl = slice(qb * TT, (qb + 1) * TT)
                br = f"BROW{qb % 2}"
                pbq = f"PS{qb % 2}"
                p.op("act", lambda s_, qsl=qsl, br=br: s_.activation(
                    t[br][:, 0:TT], t["PZ"][64:65, qsl], AF.Ln),
                    reads=("PZ",), writes=(br,))
                p.op("act", lambda s_, br=br: s_.activation(
                    t[br][:, 0:TT], t[br][:, 0:TT], AF.Exp, scale=-1.0),
                    reads=(br,), writes=(br,))
                p.op("pe", lambda eng, br=br, pbq=pbq: eng.matmul(
                    t[pbq][0:64, 0:TT], t["ONESf"][0:1, 0:64],
                    t[br][0:1, 0:TT], start=True, stop=True),
                    reads=(br, "ONESf"), writes=(pbq,))
                sp = f"SQ{nsp % 3}"
                nsp += 1
                p.op("dve", lambda v, sp=sp, pbq=pbq: v.tensor_copy(
                    t[sp][0:64, 0:TT], t[pbq][0:64, 0:TT]),
                    reads=(pbq,), writes=(sp,))
                p.op("dve", lambda v, ch=ch, b=b, qsl=qsl, sp=sp:
                     v.tensor_tensor(
                         t[f"ZB{ch}"][b:b + 64, qsl], t["PZ"][0:64, qsl],
                         t[sp][0:64, 0:TT], op=ALU.mult),
                     reads=("PZ", sp), writes=(f"ZB{ch}",))
    for c in range(NCH):
        wk_ = f"W{c % 2}"
        wt = t[wk_]
        ctx.wload(wk_, wo_hbm[c], 1024)
        for tt in range(NTT):
            sl = slice(tt * TT, (tt + 1) * TT)
            pb = f"PS{tt % 2}"
            def mm(eng, wt=wt, sl=sl, pb=pb):
                last = None
                for k in range(NCH):
                    last = eng.matmul(t[pb][:, 0:TT], wt[:, k * 128:(k + 1) * 128],
                                      t[f"ZB{k}"][:, sl],
                                      start=(k == 0), stop=(k == NCH - 1))
                return last
            p.op("pe", mm, reads=tuple(f"ZB{k}" for k in range(NCH)) + (wk_,),
                 writes=(pb,))
            xsl = slice(ctx.xs_off + sl.start, ctx.xs_off + sl.stop)
            if bo_col is not None:
                bc = bo_col(c)
                p.op("dve", lambda v, pb=pb, bc=bc, c=c, xsl=xsl:
                     v.scalar_tensor_tensor(
                         t[f"XS{c}"][:, xsl], t[pb][:, 0:TT], bc,
                         t[f"XS{c}"][:, xsl], op0=ALU.add, op1=ALU.add),
                     reads=(pb, f"XS{c}") + param_regions, writes=(f"XS{c}",))
            else:
                p.op("dve", lambda v, pb=pb, c=c, xsl=xsl: v.tensor_tensor(
                    t[f"XS{c}"][:, xsl], t[f"XS{c}"][:, xsl], t[pb][:, 0:TT],
                    op=ALU.add),
                    reads=(f"XS{c}", pb), writes=(f"XS{c}",))


def build_l0ffn(nseq=2):
    nc = bass.Bass("TRN2", target_bir_lowering=False)
    x_hbm = nc.dram_tensor("x", [nseq, D, S], F32, kind="ExternalInput").ap()
    y_hbm = nc.dram_tensor("y", [nseq, D, S], F32, kind="ExternalOutput").ap()
    w1 = nc.dram_tensor("w1", [32, 128, D], F32, kind="ExternalInput").ap()
    b1T = nc.dram_tensor("b1T", [128, 4096 // 128], F32, kind="ExternalInput").ap()
    w2 = nc.dram_tensor("w2", [NCH, 128, 4096], F32, kind="ExternalInput").ap()
    b2T = nc.dram_tensor("b2T", [128, NCH], F32, kind="ExternalInput").ap()

    es = ExitStack()
    with es:
        prog = Prog(nc, es)
        ctx = Ctx(nc, es, prog)
        streams = {"xs": prog.stream("d_xs", "sp"),
                   "setup": prog.stream("d_setup", "sp"),
                   "wg": prog.stream("d_wg", "pool")}
        for i in range(6):
            prog.stream(f"d_W{i}", "sp")
            prog.stream(f"dp_W{i}", "pool")
        make_std(ctx, n_w_tiles=0, w_cols=0)
        for i in range(4):
            ctx.sb(f"W{i}", [128, 1024], BF16)
        for i in range(4, 6):
            ctx.sb(f"W{i}", [128, 2048], BF16)
        for i in range(32):
            ctx.sb(f"H{i}", [128, 1024], BF16)
        bt = ctx.sb("B1T", [128, 4096 // 128], F32)
        prog.op("sp", lambda g: g.dma_start(out=bt[:], in_=b1T),
                reads=("hbm:b1",), writes=("B1T",), stream=streams["setup"])
        b2t = ctx.sb("B2T", [128, NCH], F32)
        prog.op("sp", lambda g: g.dma_start(out=b2t[:], in_=b2T),
                reads=("hbm:b2",), writes=("B2T",), stream=streams["setup"])
        b2h = ctx.sb("B2H", [128, NCH], F32)
        load_constants(ctx, streams)
        prog.freeze_stream("d_setup")
        prog.op("dve", lambda v: v.tensor_scalar(
            b2h[:], b2t[:], 0.5, None, op0=ALU.mult),
            reads=("B2T",), writes=("B2H",))

        for seq in range(nseq):
            emit_load_x(ctx, x_hbm, seq, streams)
            emit_cast_x(ctx)
            emit_ln(ctx)
            emit_ffn_l0(ctx, w1, lambda c4: bt[:, c4:c4 + 1], w2,
                        lambda c: b2h[:, c:c + 1], streams,
                        bias_regions=("B1T", "B2H"))
            emit_store_x(ctx, y_hbm, seq, streams)
        prog.wait_region("sp", tuple(f"hbm:x{s_}" for s_ in range(nseq)))
        prog.finalize()
    return nc




def build_l0attn(nseq=2):
    nc = bass.Bass("TRN2", target_bir_lowering=False)
    x_hbm = nc.dram_tensor("x", [nseq, D, S], F32, kind="ExternalInput").ap()
    y_hbm = nc.dram_tensor("y", [nseq, D, S], F32, kind="ExternalOutput").ap()
    wq = nc.dram_tensor("wq", [NCH, 128, D], F32, kind="ExternalInput").ap()
    wk = nc.dram_tensor("wk", [NCH, 128, D], F32, kind="ExternalInput").ap()
    wv = nc.dram_tensor("wv", [NCH, 128, D], F32, kind="ExternalInput").ap()
    wo = nc.dram_tensor("wo", [NCH, 128, D], F32, kind="ExternalInput").ap()
    wqT = nc.dram_tensor("wqT", [NCH, 128, D], F32, kind="ExternalInput").ap()
    wkT = nc.dram_tensor("wkT", [NCH, 128, D], F32, kind="ExternalInput").ap()
    rexp = nc.dram_tensor("rexp", [128, 1024], F32, kind="ExternalInput").ap()
    bqT = nc.dram_tensor("bqT", [128, NCH], F32, kind="ExternalInput").ap()
    bkT = nc.dram_tensor("bkT", [128, NCH], F32, kind="ExternalInput").ap()
    bvT = nc.dram_tensor("bvT", [128, NCH], F32, kind="ExternalInput").ap()
    boT = nc.dram_tensor("boT", [128, NCH], F32, kind="ExternalInput").ap()

    es = ExitStack()
    with es:
        prog = Prog(nc, es)
        ctx = Ctx(nc, es, prog)
        streams = {"xs": prog.stream("d_xs", "sp"),
                   "setup": prog.stream("d_setup", "sp"),
                   "tr": prog.stream("d_tr", "sp"),
                   "wg": prog.stream("d_wg", "pool")}
        for i in range(6):
            prog.stream(f"d_W{i}", "sp")
            prog.stream(f"dp_W{i}", "pool")
        make_std(ctx, n_w_tiles=6, w_cols=1024, psum=False)
        sb, ps = ctx.sb, ctx.ps
        sb("SROW", [1, TT], F32)
        sb("WRQ", [128, 1024], F32)
        sb("WRK", [128, 1024], F32)
        sb("NCSQ", [1, 128], F32)
        sb("NCSK", [1, 128], F32)
        sb("BQR", [1, 128], F32)
        sb("BKR", [1, 128], F32)
        sb("M16", [128, 16], F32)
        sb("OHSB", [128, 128], BF16)
        sb("TRANSQ", [128, S], BF16)
        sb("TRANSK", [128, S], BF16)
        sb("QP", [128, S], BF16)
        sb("KP", [128, S], BF16)
        sb("VP", [128, S], BF16)
        sb("AUGQ", [72, S], BF16)
        sb("AUGK", [72, S], BF16)
        sb("VTM0", [128, 16 * 65], BF16)
        sb("WTMP", [128, 1024], F32)
        for ch in range(NCH):
            sb(f"ZB{ch}", [128, S], BF16)
        for nm in ("BQ8", "BKC", "BVC", "BOC"):
            sb(nm, [128, NCH], F32)
        sb("REXP", [128, 1024], F32)
        ps("PS0", [128, 512], F32)
        ps("PS1", [128, 512], F32)
        ps("PS2", [128, 512], F32)
        ps("PZ", [65, 2048], F32)
        ps("PTB", [128, 1024], BF16)

        t = ctx.t
        prog.op("sp", lambda g: g.dma_start(out=t["REXP"][:], in_=rexp),
                reads=("hbm:rexp",), writes=("REXP",), stream=streams["setup"])
        for nm, src in (("BQ8", bqT), ("BKC", bkT), ("BVC", bvT), ("BOC", boT)):
            prog.op("sp", lambda g, nm=nm, src=src: g.dma_start(out=t[nm][:], in_=src),
                    reads=(f"hbm:{nm}",), writes=(nm,), stream=streams["setup"])
        load_constants(ctx, streams)
        prog.freeze_stream("d_setup")
        prog.op("dve", lambda v: v.tensor_scalar(
            t["BQ8"][:], t["BQ8"][:], 0.125, None, op0=ALU.mult),
            reads=("BQ8",), writes=("BQ8",))
        ctx.ln_banks = ("PS0", "PS1")
        emit_wr_setup_inline(ctx, wqT, wkT, streams)

        for seq in range(nseq):
            emit_load_x(ctx, x_hbm, seq, streams)
            emit_cast_x(ctx)
            prog.op("dve", lambda v: v.memset(
                bass.AP(ctx.t["VTM0"], 64, [[1040, 128], [65, 16], [1, 1]]), 1.0),
                writes=("VTM0",))
            emit_ln(ctx, srow=True,
                    tile_hook=lambda tt: emit_qr_onehot(ctx, tt, streams))
            emit_full_attn(ctx, streams, bucket=True, causal=False,
                           wq_hbm=wq, wk_hbm=wk, wv_hbm=wv, wo_hbm=wo,
                           bq8_col=lambda c: t["BQ8"][:, c:c + 1],
                           bk_col=lambda c: t["BKC"][:, c:c + 1],
                           bv_col=lambda c: t["BVC"][:, c:c + 1],
                           bo_col=lambda c: t["BOC"][:, c:c + 1],
                           param_regions=("BQ8", "BKC", "BVC", "BOC"), tag="l0a")
            emit_store_x(ctx, y_hbm, seq, streams)
        prog.wait_region("sp", tuple(f"hbm:x{s_}" for s_ in range(nseq)))
        prog.finalize()
    return nc


def emit_wr_setup_inline(ctx, wqT, wkT, streams):
    p, t = ctx.p, ctx.t
    for side, wT, bTk in (("Q", wqT, "BQ8"), ("K", wkT, "BKC")):
        wr = t[f"WR{side}"]
        for m in range(NCH):
            p.op("sp", lambda g, src=wT[m]: g.dma_start(
                out=t["WTMP"][:, 0:1024], in_=src),
                reads=(f"hbm:w{side}T",), writes=("WTMP",), stream=streams["tr"])
            def mm(eng):
                last = None
                for k in range(NCH):
                    last = eng.matmul(t["PS0"][:, 0:128],
                                      t["WTMP"][:, k * 128:(k + 1) * 128],
                                      t["REXP"][:, k * 128:(k + 1) * 128],
                                      start=(k == 0), stop=(k == NCH - 1))
                return last
            p.op("pe", mm, reads=("WTMP", "REXP"), writes=("PS0",))
            p.op("dve", lambda v, m=m, wr=wr, side=side: v.tensor_copy(
                wr[:, m * 128:(m + 1) * 128], t["PS0"][:, 0:128]),
                reads=("PS0",), writes=(f"WR{side}",))
        def mmc(eng, wr=wr):
            last = None
            for k in range(NCH):
                last = eng.matmul(t["PS1"][0:1, 0:128], t["ONESf"][:, 0:1],
                                  wr[:, k * 128:(k + 1) * 128],
                                  start=(k == 0), stop=(k == NCH - 1))
            return last
        p.op("pe", mmc, reads=(f"WR{side}", "ONESf"), writes=("PS1",))
        p.op("dve", lambda v, side=side: v.tensor_scalar(
            t[f"NCS{side}"][:, 0:128], t["PS1"][0:1, 0:128], -1.0, None,
            op0=ALU.mult),
            reads=("PS1",), writes=(f"NCS{side}",))
        bf = 8.0 if side == "Q" else 1.0
        def mmb(eng, bTk=bTk, bf=bf):
            last = None
            for k in range(NCH):
                last = eng.matmul(t["PS1"][0:1, 128:256], t[bTk][:, k:k + 1],
                                  t["REXP"][:, k * 128:(k + 1) * 128],
                                  start=(k == 0), stop=(k == NCH - 1))
            return last
        p.op("pe", mmb, reads=(bTk, "REXP"), writes=("PS1",))
        p.op("dve", lambda v, side=side, bf=bf: v.tensor_scalar(
            t[f"B{side}R"][:, 0:128], t["PS1"][0:1, 128:256], bf, None,
            op0=ALU.mult),
            reads=("PS1",), writes=(f"B{side}R",))


def build_l1attn(nseq=2):
    nc = bass.Bass("TRN2", target_bir_lowering=False)
    x_hbm = nc.dram_tensor("x", [nseq, D, S], F32, kind="ExternalInput").ap()
    y_hbm = nc.dram_tensor("y", [nseq, D, S], F32, kind="ExternalOutput").ap()
    wq = nc.dram_tensor("wq", [NCH, 128, D], F32, kind="ExternalInput").ap()
    wk = nc.dram_tensor("wk", [NCH, 128, D], F32, kind="ExternalInput").ap()
    wv = nc.dram_tensor("wv", [NCH, 128, D], F32, kind="ExternalInput").ap()
    wo = nc.dram_tensor("wo", [NCH, 128, D], F32, kind="ExternalInput").ap()
    triu = nc.dram_tensor("CONST_TRIU", [128, 128], F32, kind="ExternalInput").ap()

    es = ExitStack()
    with es:
        prog = Prog(nc, es)
        ctx = Ctx(nc, es, prog)
        streams = {"xs": prog.stream("d_xs", "sp"),
                   "setup": prog.stream("d_setup", "sp"),
                   "tr": prog.stream("d_tr", "sp"),
                   "wg": prog.stream("d_wg", "pool")}
        for i in range(6):
            prog.stream(f"d_W{i}", "sp")
            prog.stream(f"dp_W{i}", "pool")
        make_std(ctx, n_w_tiles=6, w_cols=1024, psum=False)
        sb, ps = ctx.sb, ctx.ps
        sb("QP", [128, S], BF16)
        sb("KP", [128, S], BF16)
        sb("VP", [128, S], BF16)
        sb("AUGQ", [72, S], BF16)
        sb("AUGK", [72, S], BF16)
        sb("VTM0", [128, 16 * 65], BF16)
        sb("TRIUb", [128, 128], BF16)
        for ch in range(NCH):
            sb(f"ZB{ch}", [128, S], BF16)
        ps("PS0", [128, 512], F32)
        ps("PS1", [128, 512], F32)
        ps("PS2", [128, 512], F32)
        ps("PZ", [65, 2048], F32)
        ps("PTB", [128, 1024], BF16)
        ctx.ln_banks = ("PS0", "PS1")

        t = ctx.t
        prog.op("pool", lambda g: g.dma_start(out=t["TRIUb"][:], in_=triu),
                reads=("c:TRIU",), writes=("TRIUb",), stream=streams["wg"])
        load_constants(ctx, streams)
        prog.freeze_stream("d_setup")

        for seq in range(nseq):
            emit_load_x(ctx, x_hbm, seq, streams)
            emit_cast_x(ctx)
            prog.op("dve", lambda v: v.memset(
                bass.AP(ctx.t["VTM0"], 64, [[1040, 128], [65, 16], [1, 1]]), 1.0),
                writes=("VTM0",))
            emit_ln(ctx)
            emit_full_attn(ctx, streams, bucket=False, causal=True,
                           wq_hbm=wq, wk_hbm=wk, wv_hbm=wv, wo_hbm=wo, tag="l1a")
            emit_store_x(ctx, y_hbm, seq, streams)
        prog.wait_region("sp", tuple(f"hbm:x{s_}" for s_ in range(nseq)))
        prog.finalize()
    return nc


def emit_cast_shift(ctx, off=1):
    p, t = ctx.p, ctx.t
    for c in range(NCH):
        so = off if c < 4 else off - 1
        eng = "pool" if c % 2 == 0 else "dve"
        p.op(eng, lambda v, c=c, so=so: v.tensor_copy(
            t[f"XN{c}"][:, 0:S], t[f"XS{c}"][:, so:so + S]),
            reads=(f"XS{c}",),
            writes=tuple(f"XN{c}@{i}" for i in range(NTT)))


def emit_ffn_gated(ctx, w1_hbm, b1_col, w2_hbm, b2_col, streams, lng_col,
                   lnb_col, param_regions=(), tag=""):
    p, t = ctx.p, ctx.t
    NA = 22
    HW = S // 2
    for half in range(2):
        def w1_mms(c4):
            for part, oc in (("g", 22 + c4), ("a", c4)):
                wk_ = f"W{(2 * c4 + (0 if part == "g" else 1)) % 4}"
                wt = t[wk_]
                ctx.wload(wk_, w1_hbm[oc], 1024)
                for tt in range(2):
                    sl = slice(half * HW + tt * TT, half * HW + (tt + 1) * TT)
                    pb = f"P{tt % 2}" if part == "g" else f"P{2 + tt % 2}"
                    def mm(eng, wt=wt, sl=sl, pb=pb):
                        last = None
                        for k in range(NCH):
                            last = eng.matmul(t[pb][:, 0:TT],
                                              wt[:, k * 128:(k + 1) * 128],
                                              t[f"XN{k}"][:, sl],
                                              start=(k == 0), stop=(k == NCH - 1))
                        return last
                    p.op("pe", mm,
                         reads=tuple(f"XN{k}@{sl.start // TT}" for k in range(NCH)) + (wk_,),
                         writes=(pb,))

        def w1_epi(c4):
            for tt in range(2):
                pbg = f"P{tt % 2}"
                pba = f"P{2 + tt % 2}"
                sq = f"SQ{tt % 2}"
                p.op("act", lambda s_, pbg=pbg, bc=b1_col(22 + c4), sq=sq:
                     s_.activation(t[sq][:, 0:TT], t[pbg][:, 0:TT], AF.Gelu, bias=bc),
                     reads=(pbg,) + param_regions, writes=(sq,))
                p.op("dve", lambda v, pba=pba, bc=b1_col(c4), sq=sq, c4=c4, tt=tt:
                     v.scalar_tensor_tensor(
                         t[f"H{c4}"][:, tt * TT:(tt + 1) * TT],
                         t[pba][:, 0:TT], bc, t[sq][:, 0:TT],
                         op0=ALU.add, op1=ALU.mult),
                     reads=(pba, sq) + param_regions, writes=(f"H{c4}",))

        for c4 in range(NA):
            w1_mms(c4)
            w1_epi(c4)
        for c in range(NCH):
            wk_ = f"W{4 + c % 2}"
            wt = t[wk_]
            ctx.wload(wk_, w2_hbm[c], 2816)
            for tt in range(2):
                sl = slice(half * HW + tt * TT, half * HW + (tt + 1) * TT)
                xsl = slice(ctx.xs_off + half * HW + tt * TT,
                            ctx.xs_off + half * HW + (tt + 1) * TT)
                pb = f"P{2 + tt % 2}"
                def mm(eng, wt=wt, tt=tt, pb=pb):
                    last = None
                    for k in range(NA):
                        last = eng.matmul(t[pb][:, 0:TT], wt[:, k * 128:(k + 1) * 128],
                                          t[f"H{k}"][:, tt * TT:(tt + 1) * TT],
                                          start=(k == 0), stop=(k == NA - 1))
                    return last
                p.op("pe", mm, reads=tuple(f"H{k}" for k in range(NA)) + (wk_,),
                     writes=(pb,))
                bc = b2_col(c)
                p.op("dve", lambda v, pb=pb, bc=bc, c=c, xsl=xsl:
                     v.scalar_tensor_tensor(
                         t[f"XS{c}"][:, xsl], t[pb][:, 0:TT], bc,
                         t[f"XS{c}"][:, xsl], op0=ALU.add, op1=ALU.add),
                     reads=(pb, f"XS{c}") + param_regions, writes=(f"XS{c}",))


def build_l1ffn_fsub(nseq=2):
    nc = bass.Bass("TRN2", target_bir_lowering=False)
    x_hbm = nc.dram_tensor("x", [nseq, D, S], F32, kind="ExternalInput").ap()
    y_hbm = nc.dram_tensor("y", [nseq, D, S], F32, kind="ExternalOutput").ap()
    w1 = nc.dram_tensor("w1", [44, 128, D], F32, kind="ExternalInput").ap()
    w2 = nc.dram_tensor("w2", [NCH, 128, 2816], F32, kind="ExternalInput").ap()
    b1T = nc.dram_tensor("b1T", [128, 44], F32, kind="ExternalInput").ap()
    b2T = nc.dram_tensor("b2T", [128, NCH], F32, kind="ExternalInput").ap()
    lngT = nc.dram_tensor("lngT", [128, NCH], F32, kind="ExternalInput").ap()
    lnbT = nc.dram_tensor("lnbT", [128, NCH], F32, kind="ExternalInput").ap()

    es = ExitStack()
    with es:
        prog = Prog(nc, es)
        ctx = Ctx(nc, es, prog)
        streams = {"xs": prog.stream("d_xs", "sp"),
                   "setup": prog.stream("d_setup", "sp"),
                   "wg": prog.stream("d_wg", "pool")}
        for i in range(6):
            prog.stream(f"d_W{i}", "sp")
            prog.stream(f"dp_W{i}", "pool")
        make_std(ctx, n_w_tiles=0, w_cols=0, psum=True, xs_cols=S + 1)
        ctx.xs_off = 1
        for i in range(4):
            ctx.sb(f"W{i}", [128, 1024], BF16)
        for i in range(4, 6):
            ctx.sb(f"W{i}", [128, 2816], BF16)
        for i in range(22):
            ctx.sb(f"H{i}", [128, 1024], BF16)
        t = ctx.t
        for nm, src_ in (("B1C", b1T), ("B2C", b2T), ("LNG", lngT), ("LNB", lnbT)):
            dd = ctx.sb(nm, [128, src_.shape[-1]], F32)
            prog.op("sp", lambda g, dd=dd, src_=src_: g.dma_start(out=dd[:], in_=src_),
                    reads=(f"hbm:{nm}",), writes=(nm,), stream=streams["setup"])
        load_constants(ctx, streams)
        prog.freeze_stream("d_setup")

        for seq in range(nseq):
            emit_load_x(ctx, x_hbm, seq, streams)
            for c in range(NCH):
                prog.op("dve", lambda v, c=c: v.memset(t[f"XS{c}"][:, 0:1], 0.0),
                        writes=(f"XS{c}",))
            emit_cast_shift(ctx)
            ctx.ln_param_regions = ("LNG", "LNB")
            emit_ln(ctx, g_col=lambda c: t["LNG"][:, c:c + 1],
                    b_col=lambda c: t["LNB"][:, c:c + 1])
            emit_ffn_gated(ctx, w1, lambda oc: t["B1C"][:, oc:oc + 1], w2,
                           lambda c: t["B2C"][:, c:c + 1], streams,
                           None, None, param_regions=("B1C", "B2C"))
            emit_store_x(ctx, y_hbm, seq, streams)
        prog.wait_region("sp", tuple(f"hbm:x{s_}" for s_ in range(nseq)))
        prog.finalize()
    return nc


def emit_tables(ctx, streams, tb_aps, qs_col, qsp_col, ks_col, ksp_col,
                param_regions=()):
    p, t = ctx.p, ctx.t
    for dst, base, col, sc in (("TQ1", "TBQC", qs_col, 8.0),
                               ("TQ2", "TBQS", qsp_col, 8.0),
                               ("TK1", "TBKC", ks_col, 1.0),
                               ("TK2", "TBKS", ksp_col, 1.0)):
        ap = tb_aps[base]
        if ap.dtype == BF16:
            p.op("sp", lambda g, dst=dst, ap=ap: g.dma_start(
                out=t[dst][:], in_=ap),
                reads=(f"c:{base}",), writes=(dst,), stream=streams["tr"])
        else:
            p.op("pool", lambda g, dst=dst, ap=ap: g.dma_start(
                out=t[dst][:], in_=ap),
                reads=(f"c:{base}",), writes=(dst,), stream=streams["wg"])
        p.op("dve", lambda v, dst=dst, col=col, sc=sc: v.tensor_scalar(
            t[dst][:], t[dst][:], col, sc, op0=ALU.mult, op1=ALU.mult),
            reads=(dst,) + param_regions, writes=(dst,))


def emit_attn_sublayer(ctx, streams, wqkv_hbm, wo_hbm, qs_col, qsp_col, ks_col,
                       ksp_col, param_regions=(), tag=""):
    p, t = ctx.p, ctx.t
    W = 256
    for ch in range(NCH):
        for wi, dst in ((0, "QP"), (1, "KP"), (2, "VP")):
            wk_ = f"W{2 * wi + ch % 2}"
            ctx.wload(wk_, wqkv_hbm[8 * wi + ch], 1024)

        def proj(tt, wi, dst):
            sl = slice(tt * TT, (tt + 1) * TT)
            wk_ = f"W{2 * wi + ch % 2}"
            wt = t[wk_]
            pb = f"PS{(wi + 3 * tt) % 2}"
            def mm(eng, wt=wt, sl=sl, pb=pb):
                last = None
                for k in range(NCH):
                    last = eng.matmul(t[pb][:, 0:TT], wt[:, k * 128:(k + 1) * 128],
                                      t[f"XN{k}"][:, sl],
                                      start=(k == 0), stop=(k == NCH - 1))
                return last
            p.op("pe", mm,
                 reads=tuple(f"XN{k}@{tt}" for k in range(NCH)) + (wk_,),
                 writes=(pb,))
            if dst == "VP":
                p.op("act", lambda s_, pb=pb, sl=sl, dst=dst: s_.copy(
                    t[dst][:, sl], t[pb][:, 0:TT]),
                    reads=(pb,), writes=(f"{dst}@{tt}",))
            else:
                p.op("dve", lambda v, pb=pb, sl=sl, dst=dst: v.tensor_copy(
                    t[dst][:, sl], t[pb][:, 0:TT]),
                    reads=(pb,), writes=(f"{dst}@{tt}",))

        rit = 0

        def rotary(tt, side):
            nonlocal rit
            src_, t1, t2 = ("QP", "TQ1", "TQ2") if side == "q" else ("KP", "TK1", "TK2")
            sl = slice(tt * TT, (tt + 1) * TT)
            par = rit % 2
            rit += 1
            pA = "PS2"
            pR = "PS3"
            s0, s1, s2, s3 = (f"SQ{4 * par}", f"SQ{4 * par + 1}",
                              f"SQ{4 * par + 2}", f"SQ{4 * par + 3}")
            rn = f"RN2{par}"
            p.op("pool", lambda g, src_=src_, sl=sl, s0=s0, tt=tt: g.tensor_tensor(
                t[s0][:, 0:TT], t[src_][:, sl], t[src_][:, sl], op=ALU.mult),
                reads=(f"{src_}@{tt}",), writes=(s0,))
            p.op("pe", lambda eng, pA=pA, s0=s0: eng.matmul(
                t[pA][0:2, 0:TT], t["EHD2b"][:, 0:2], t[s0][:, 0:TT],
                start=True, stop=True),
                reads=(s0, "EHD2b"), writes=(pA,))
            p.op("act", lambda s_, pA=pA, rn=rn: s_.activation(
                t[rn][:, 0:TT], t[pA][0:2, 0:TT], AF.Ln,
                bias=t["CONSTS"][0:2, 6:7]),
                reads=(pA, "CONSTS"), writes=(rn,))
            p.op("act", lambda s_, rn=rn: s_.activation(
                t[rn][:, 0:TT], t[rn][:, 0:TT], AF.Exp, scale=-0.5),
                reads=(rn,), writes=(rn,))
            p.op("pe", lambda eng, pA=pA, rn=rn: eng.matmul(
                t[pA][:, 0:TT], t["E2Tf"][:, 0:128], t[rn][:, 0:TT],
                start=True, stop=True),
                reads=(rn, "E2Tf"), writes=(pA,))
            p.op("pe", lambda eng, pR=pR, src_=src_, sl=sl, tt=tt: eng.matmul(
                t[pR][:, 0:TT], t["RHTb"][:, 0:128], t[src_][:, sl],
                start=True, stop=True),
                reads=(f"{src_}@{tt}", "RHTb"), writes=(pR,))
            p.op("pool", lambda g, t1=t1, sl=sl, src_=src_, s2=s2, tt=tt:
                 g.tensor_tensor(
                     t[s2][:, 0:TT], t[src_][:, sl], t[t1][:, sl], op=ALU.mult),
                 reads=(f"{src_}@{tt}", t1), writes=(s2,))
            p.op("dve", lambda v, t2=t2, sl=sl, pR=pR, s3=s3: v.tensor_tensor(
                t[s3][:, 0:TT], t[pR][:, 0:TT], t[t2][:, sl], op=ALU.mult),
                reads=(pR, t2), writes=(s3,))
            p.op("pool", lambda g, s1=s1, s2=s2, s3=s3: g.tensor_tensor(
                t[s1][:, 0:TT], t[s2][:, 0:TT], t[s3][:, 0:TT], op=ALU.add),
                reads=(s2, s3), writes=(s1,))
            p.op("dve", lambda v, src_=src_, sl=sl, pA=pA, s1=s1, tt=tt:
                 v.tensor_tensor(
                     t[src_][:, sl], t[s1][:, 0:TT], t[pA][:, 0:TT], op=ALU.mult),
                 reads=(s1, pA), writes=(f"{src_}@{tt}",))

        for tt in range(NTT + 1):
            if tt < NTT:
                proj(tt, 0, "QP")
                proj(tt, 1, "KP")
                proj(tt, 2, "VP")
            if tt >= 1:
                rotary(tt - 1, "q")
                rotary(tt - 1, "k")
        for hh in range(2):
            b = 64 * hh
            vt = f"VTM{hh}"
            for j0 in range(4):
                pco = 256 * (j0 % 2) + 512 * hh
                def trs(eng, b=b, j0=j0, pco=pco):
                    last = None
                    for jj in range(4):
                        j = 4 * j0 + jj
                        last = eng.transpose(t["PTB"][:, pco + jj * 64:pco + (jj + 1) * 64],
                                             t["VP"][b:b + 64, j * 128:(j + 1) * 128],
                                             t["IDENT2b"][b:b + 64, 0:64])
                    return last
                p.op("pe", trs, reads=(f"VP@{j0}", "IDENT2b"), writes=("PTB",))
                p.op("dve", lambda v, vt=vt, j0=j0, pco=pco: v.tensor_copy(
                    bass.AP(t[vt], 4 * j0 * 65, [[1040, 128], [65, 4], [1, 64]]),
                    bass.AP(t["PTB"], pco, [[1024, 128], [64, 4], [1, 64]])),
                    reads=("PTB",), writes=(vt,))
        for grp in range(4):
            iters = [(hh, qt) for qt in range(4 * grp, 4 * grp + 4)
                     for hh in range(2)]

            def emit_s_l(k):
                hh, qt = iters[k]
                b = 64 * hh
                c_lo = max(0, qt - 2)
                bw = (qt - c_lo + 1) * 128
                pb = f"PS{k % 3}"
                kt0, kt1 = c_lo // 4, qt // 4
                p.op("pe", lambda eng, b=b, qt=qt, c_lo=c_lo, bw=bw, pb=pb:
                     eng.matmul(t[pb][:, 0:bw],
                                t["QP"][b:b + 64, qt * 128:(qt + 1) * 128],
                                t["KP"][b:b + 64, c_lo * 128:c_lo * 128 + bw],
                                start=True, stop=True),
                     reads=(f"QP@{qt // 4}", f"KP@{kt0}", f"KP@{kt1}"),
                     writes=(pb,))

            def emit_m_l(k):
                hh, qt = iters[k]
                c_lo = max(0, qt - 2)
                nb = qt - c_lo + 1
                bw = nb * 128
                pb = f"PS{k % 3}"
                nm = f"NM{k % 2}"
                p.op("dve", lambda v, pb=pb, bw=bw, nm=nm: v.tensor_reduce(
                    t[nm][:, 0:1], t[pb][:, 0:bw], axis=AX.X, op=ALU.max,
                    negate=True),
                    reads=(pb,), writes=(nm,))
                sq = f"SQ{k % 6}"
                p.op("act", lambda s_, pb=pb, sq=sq, bw=bw, nm=nm: s_.activation(
                    t[sq][:, 0:bw], t[pb][:, 0:bw], AF.Exp, bias=t[nm][:, 0:1]),
                    reads=(pb, nm), writes=(sq,))
                if qt >= 2:
                    p.op("pool", lambda g, sq=sq: g.tensor_tensor(
                        t[sq][:, 0:128], t[sq][:, 0:128], t["TRIUb"][:, 0:128],
                        op=ALU.mult),
                        reads=("TRIUb", sq), writes=(sq,))
                p.op("pool", lambda g, sq=sq, bw=bw: g.tensor_tensor(
                    t[sq][:, bw - 128:bw], t[sq][:, bw - 128:bw],
                    t["TRILb"][:, 0:128], op=ALU.mult),
                    reads=("TRILb", sq), writes=(sq,))

            def emit_t_l(k):
                hh, qt = iters[k]
                c_lo = max(0, qt - 2)
                nb = qt - c_lo + 1
                sq = f"SQ{k % 6}"
                pcol = 512 * (k % 2)
                def trs2(eng, sq=sq, nb=nb, pcol=pcol):
                    last = None
                    for cc in range(nb):
                        last = eng.transpose(
                            t["PTB"][:, pcol + cc * 128:pcol + (cc + 1) * 128],
                            t[sq][:, cc * 128:(cc + 1) * 128], t["IDENTb"][:])
                    return last
                p.op("pe", trs2, reads=(sq, "IDENTb"), writes=("PTB",))
                ptt = f"PTT{k % 2}"
                p.op("dve", lambda v, ptt=ptt, nb=nb, pcol=pcol: v.tensor_copy(
                    t[ptt][:, 0:nb * 128], t["PTB"][:, pcol:pcol + nb * 128]),
                    reads=("PTB",), writes=(ptt,))

            def emit_pv_l(k):
                hh, qt = iters[k]
                vt = f"VTM{hh}"
                c_lo = max(0, qt - 2)
                nb = qt - c_lo + 1
                pz = f"PZ{hh}"
                zsl = slice((qt % 4) * 128, (qt % 4) * 128 + 128)
                ptt = f"PTT{k % 2}"
                def pv(eng, vt=vt, c_lo=c_lo, nb=nb, pz=pz, zsl=zsl, ptt=ptt):
                    last = None
                    for cc in range(nb):
                        last = eng.matmul(
                            t[pz][:, zsl],
                            t[vt][:, (c_lo + cc) * 65:(c_lo + cc) * 65 + 65],
                            t[ptt][:, cc * 128:(cc + 1) * 128],
                            start=(cc == 0), stop=(cc == nb - 1))
                    return last
                p.op("pe", pv, reads=(vt, ptt), writes=(pz,))

            for k_ in range(len(iters) + 6):
                if k_ < len(iters):
                    emit_s_l(k_)
                if 2 <= k_ and k_ - 2 < len(iters):
                    emit_m_l(k_ - 2)
                if k_ >= 6:
                    emit_pv_l(k_ - 6)
                if 4 <= k_ and k_ - 4 < len(iters):
                    emit_t_l(k_ - 4)
            for hh in range(2):
                b = 64 * hh
                pz = f"PZ{hh}"
                qsl = slice(grp * TT, (grp + 1) * TT)
                br = f"BROW{hh % 2}"
                p.op("act", lambda s_, pz=pz, br=br: s_.activation(
                    t[br][:, 0:TT], t[pz][64:65, 0:TT], AF.Ln),
                    reads=(pz,), writes=(br,))
                p.op("act", lambda s_, br=br: s_.activation(
                    t[br][:, 0:TT], t[br][:, 0:TT], AF.Exp, scale=-1.0),
                    reads=(br,), writes=(br,))
                pbq = f"PS{hh % 2}"
                p.op("pe", lambda eng, br=br, pbq=pbq: eng.matmul(
                    t[pbq][0:64, 0:TT], t["ONESf"][0:1, 0:64],
                    t[br][0:1, 0:TT], start=True, stop=True),
                    reads=(br, "ONESf"), writes=(pbq,))
                sq = f"SQ{6 + hh}"
                p.op("dve", lambda v, sq=sq, pbq=pbq: v.tensor_copy(
                    t[sq][0:64, 0:TT], t[pbq][0:64, 0:TT]),
                    reads=(pbq,), writes=(sq,))
                p.op("dve", lambda v, ch=ch, b=b, qsl=qsl, sq=sq, pz=pz:
                     v.tensor_tensor(
                         t[f"ZB{ch}"][b:b + 64, qsl], t[pz][0:64, 0:TT],
                         t[sq][0:64, 0:TT], op=ALU.mult),
                     reads=(pz, sq), writes=(f"ZB{ch}",))
    for c in range(NCH):
        wk_ = f"W{c % 2}"
        wt = t[wk_]
        ctx.wload(wk_, wo_hbm[c], 1024)
        for tt in range(NTT):
            sl = slice(tt * TT, (tt + 1) * TT)
            xsl = slice(ctx.xs_off + tt * TT, ctx.xs_off + (tt + 1) * TT)
            pb = f"PS{tt % 2}"
            def mm(eng, wt=wt, sl=sl, pb=pb):
                last = None
                for k in range(NCH):
                    last = eng.matmul(t[pb][:, 0:TT], wt[:, k * 128:(k + 1) * 128],
                                      t[f"ZB{k}"][:, sl],
                                      start=(k == 0), stop=(k == NCH - 1))
                return last
            p.op("pe", mm, reads=tuple(f"ZB{k}" for k in range(NCH)) + (wk_,),
                 writes=(pb,))
            p.op("dve", lambda v, c=c, pb=pb, xsl=xsl: v.tensor_tensor(
                t[f"XS{c}"][:, xsl], t[f"XS{c}"][:, xsl], t[pb][:, 0:TT], op=ALU.add),
                reads=(f"XS{c}", pb), writes=(f"XS{c}",))


def build_l1ffn_asub(nseq=2):
    nc = bass.Bass("TRN2", target_bir_lowering=False)
    x_hbm = nc.dram_tensor("x", [nseq, D, S], F32, kind="ExternalInput").ap()
    y_hbm = nc.dram_tensor("y", [nseq, D, S], F32, kind="ExternalOutput").ap()
    wqkv = nc.dram_tensor("wqkv", [24, 128, D], F32, kind="ExternalInput").ap()
    wo = nc.dram_tensor("wo", [NCH, 128, D], F32, kind="ExternalInput").ap()
    qsT = nc.dram_tensor("qsT", [128, 1], F32, kind="ExternalInput").ap()
    qspT = nc.dram_tensor("qspT", [128, 1], F32, kind="ExternalInput").ap()
    ksT = nc.dram_tensor("ksT", [128, 1], F32, kind="ExternalInput").ap()
    kspT = nc.dram_tensor("kspT", [128, 1], F32, kind="ExternalInput").ap()

    es = ExitStack()
    with es:
        prog = Prog(nc, es)
        ctx = Ctx(nc, es, prog)
        streams = {"xs": prog.stream("d_xs", "sp"),
                   "setup": prog.stream("d_setup", "sp"),
                   "tr": prog.stream("d_tr", "sp"),
                   "wg": prog.stream("d_wg", "pool")}
        for i in range(6):
            prog.stream(f"d_W{i}", "sp")
            prog.stream(f"dp_W{i}", "pool")
        make_std(ctx, n_w_tiles=6, w_cols=1024, psum=False, xs_cols=S + 1)
        ctx.xs_off = 1
        sb, ps = ctx.sb, ctx.ps
        alloc_asub(ctx)
        t = ctx.t
        emit_asub_setup(ctx, streams, qsT, qspT, ksT, kspT)
        load_constants(ctx, streams)
        prog.freeze_stream("d_setup")

        for seq in range(nseq):
            emit_load_x(ctx, x_hbm, seq, streams)
            for c in range(NCH):
                prog.op("dve", lambda v, c=c: v.memset(t[f"XS{c}"][:, 0:1], 0.0),
                        writes=(f"XS{c}",))
            for vtn in ("VTM0", "VTM1"):
                prog.op("dve", lambda v, vtn=vtn: v.memset(
                    bass.AP(t[vtn], 64, [[1040, 128], [65, 16], [1, 1]]), 1.0),
                    writes=(vtn,))
            emit_tables(ctx, streams, ctx.tb_aps,
                        t["QSC"][:, 0:1], t["QSPC"][:, 0:1],
                        t["KSC"][:, 0:1], t["KSPC"][:, 0:1],
                        param_regions=("QSC", "QSPC", "KSC", "KSPC"))
            emit_cast_shift(ctx)
            emit_attn_sublayer(ctx, streams, wqkv, wo,
                               t["QSC"][:, 0:1], t["QSPC"][:, 0:1],
                               t["KSC"][:, 0:1], t["KSPC"][:, 0:1],
                               param_regions=("QSC", "QSPC", "KSC", "KSPC"))
            emit_store_x(ctx, y_hbm, seq, streams)
        prog.wait_region("sp", tuple(f"hbm:x{s_}" for s_ in range(nseq)))
        prog.finalize()
    return nc


def alloc_asub(ctx):
    sb, ps = ctx.sb, ctx.ps
    sb("QP", [128, S], BF16)
    sb("KP", [128, S], BF16)
    sb("VP", [128, S], BF16)
    sb("VTM0", [128, 16 * 65], BF16)
    sb("VTM1", [128, 16 * 65], BF16)
    sb("TRIUb", [128, 128], BF16)
    sb("TRILb", [128, 128], BF16)
    sb("RHTb", [128, 128], BF16)
    sb("EHD2b", [128, 2], BF16)
    sb("E2Tf", [2, 128], F32)
    sb("RN20", [2, TT], F32)
    sb("RN21", [2, TT], F32)
    sb("NM0", [128, 1], F32)
    sb("NM1", [128, 1], F32)
    for i in range(4, 8):
        sb(f"SQ{i}", [128, TT], BF16)
    for i in range(2):
        sb(f"PTT{i}", [128, 384], BF16)
    for nm in ("TQ1", "TQ2", "TK1", "TK2"):
        sb(nm, [128, S], BF16)
    for ch in range(NCH):
        sb(f"ZB{ch}", [128, S], BF16)
    for nm in ("QSC", "QSPC", "KSC", "KSPC"):
        sb(nm, [128, 6], F32)
    ps("PS0", [128, 512], F32)
    ps("PS1", [128, 512], F32)
    ps("PS2", [128, 512], F32)
    ps("PS3", [128, 512], F32)
    ps("PZ0", [65, 512], F32)
    ps("PZ1", [65, 512], F32)
    ps("PTB", [128, 1024], BF16)


def emit_asub_setup(ctx, streams, qsT, qspT, ksT, kspT):
    p, t = ctx.p, ctx.t
    nc = ctx.nc
    for nm, src in (("QSC", qsT), ("QSPC", qspT), ("KSC", ksT), ("KSPC", kspT)):
        p.op("sp", lambda g, nm=nm, src=src: g.dma_start(
            out=t[nm][:, 0:src.shape[-1]], in_=src),
            reads=(f"hbm:{nm}",), writes=(nm,), stream=streams["setup"])
    ctx.tb_aps = {}
    for nm in ("TRIL", "RHT", "EHD2"):
        shape = {"TRIL": [128, 128], "RHT": [128, 128], "EHD2": [128, 2]}[nm]
        ap = nc.dram_tensor("CONST_" + nm, shape, F32, kind="ExternalInput").ap()
        dstn = nm + "b" if nm in ("TRIL", "RHT") else "EHD2b"
        p.op("pool", lambda g, ap=ap, dstn=dstn: g.dma_start(out=t[dstn][:], in_=ap),
             reads=(f"c:{nm}",), writes=(dstn,), stream=streams["wg"])
    for nm in ("TBQC", "TBQS", "TBKC", "TBKS"):
        ctx.tb_aps[nm] = nc.dram_tensor("CONST_" + nm, [128, S], F32,
                                        kind="ExternalInput").ap()
    ap = nc.dram_tensor("CONST_E2T", [2, 128], F32, kind="ExternalInput").ap()
    p.op("sp", lambda g, ap=ap: g.dma_start(out=t["E2Tf"][:], in_=ap),
         reads=("c:E2T",), writes=("E2Tf",), stream=streams["setup"])
    ap2 = nc.dram_tensor("CONST_TRIU", [128, 128], F32, kind="ExternalInput").ap()
    p.op("pool", lambda g: g.dma_start(out=t["TRIUb"][:], in_=ap2),
         reads=("c:TRIU",), writes=("TRIUb",), stream=streams["wg"])


def build_ln_out(nseq=2):
    nc = bass.Bass("TRN2", target_bir_lowering=False)
    x_hbm = nc.dram_tensor("x", [nseq, D, S], F32, kind="ExternalInput").ap()
    y_hbm = nc.dram_tensor("y", [nseq, D, S], F32, kind="ExternalOutput").ap()
    es = ExitStack()
    with es:
        prog = Prog(nc, es)
        ctx = Ctx(nc, es, prog)
        streams = {"xs": prog.stream("d_xs", "sp"),
                   "setup": prog.stream("d_setup", "sp"),
                   "wg": prog.stream("d_wg", "pool")}
        make_std(ctx, n_w_tiles=0, w_cols=0, psum=True)
        load_constants(ctx, streams)
        prog.freeze_stream("d_setup")
        for seq in range(nseq):
            emit_load_x(ctx, x_hbm, seq, streams)
            emit_cast_x(ctx)
            emit_ln(ctx, out_f32="XS")
            emit_store_x(ctx, y_hbm, seq, streams)
        prog.wait_region("sp", tuple(f"hbm:x{s_}" for s_ in range(nseq)))
        prog.finalize()
    return nc


def build_axpy(nseq=2, alpha=0.5):
    nc = bass.Bass("TRN2", target_bir_lowering=False)
    x_hbm = nc.dram_tensor("x", [nseq, D, S], F32, kind="ExternalInput").ap()
    yin_hbm = nc.dram_tensor("yin", [nseq, D, S], F32, kind="ExternalInput").ap()
    y_hbm = nc.dram_tensor("y", [nseq, D, S], F32, kind="ExternalOutput").ap()
    es = ExitStack()
    with es:
        prog = Prog(nc, es)
        ctx = Ctx(nc, es, prog)
        streams = {"xs": prog.stream("d_xs", "sp"), "y": prog.stream("d_y", "sp")}
        t = ctx.t
        for i in range(NCH):
            ctx.sb(f"XS{i}", [128, S], F32)
        for i in range(2):
            ctx.sb(f"YT{i}", [128, S], F32)
        for seq in range(nseq):
            emit_load_x(ctx, x_hbm, seq, streams)
            for c in range(NCH):
                yt = f"YT{c % 2}"
                prog.op("sp", lambda g, c=c, yt=yt, seq=seq: g.dma_start(
                    out=t[yt][:], in_=yin_hbm[seq, c * 128:(c + 1) * 128, :]),
                    reads=(f"hbm:yin{seq}",), writes=(yt,), stream=streams["y"])
                for tt in range(NTT):
                    sl = slice(tt * TT, (tt + 1) * TT)
                    prog.op("dve", lambda v, c=c, yt=yt, sl=sl: v.scalar_tensor_tensor(
                        t[f"XS{c}"][:, sl], t[yt][:, sl], alpha, t[f"XS{c}"][:, sl],
                        op0=ALU.mult, op1=ALU.add),
                        reads=(yt, f"XS{c}"), writes=(f"XS{c}",))
            emit_store_x(ctx, y_hbm, seq, streams)
        prog.wait_region("sp", tuple(f"hbm:x{s_}" for s_ in range(nseq)))
        prog.finalize()
    return nc




_LAST_PROG = None


def build_fused(nseq=2):
    global _LAST_PROG
    nc = bass.Bass("TRN2", target_bir_lowering=False)
    gi = lambda n, shp: nc.dram_tensor(n, shp, F32, kind="ExternalInput").ap()
    x_hbm = gi("x", [nseq, D, S])
    y_hbm = nc.dram_tensor("y", [nseq, D, S], F32, kind="ExternalOutput").ap()
    spill = nc.dram_tensor("spill", [D, S], F32).ap()
    fw1_ = gi("l0f_w1", [32, 128, D]); fw2_ = gi("l0f_w2", [NCH, 128, 4096])
    fb1T = gi("l0f_b1T", [128, 32]); fb2T = gi("l0f_b2T", [128, NCH])
    awq = gi("l0a_wq", [NCH, 128, D]); awk = gi("l0a_wk", [NCH, 128, D])
    awv = gi("l0a_wv", [NCH, 128, D]); awo = gi("l0a_wo", [NCH, 128, D])
    awqT = gi("l0a_wqT", [NCH, 128, D]); awkT = gi("l0a_wkT", [NCH, 128, D])
    arexp = gi("l0a_rexp", [128, 1024])
    abqT = gi("l0a_bqT", [128, NCH]); abkT = gi("l0a_bkT", [128, NCH])
    abvT = gi("l0a_bvT", [128, NCH]); aboT = gi("l0a_boT", [128, NCH])
    cwq = gi("l1a_wq", [NCH, 128, D]); cwk = gi("l1a_wk", [NCH, 128, D])
    cwv = gi("l1a_wv", [NCH, 128, D]); cwo = gi("l1a_wo", [NCH, 128, D])
    swqkv = gi("s_wqkv", [6, 24, 128, D]); swo = gi("s_wo", [6, NCH, 128, D])
    sw1 = gi("s_w1", [6, 44, 128, D]); sw2 = gi("s_w2", [6, NCH, 128, 2816])
    sqsT = gi("s_qsT", [128, 6]); sqspT = gi("s_qspT", [128, 6])
    sksT = gi("s_ksT", [128, 6]); skspT = gi("s_kspT", [128, 6])
    sb1T = gi("s_b1T", [128, 6 * 44]); sb2T = gi("s_b2T", [128, 6 * NCH])
    slngT = gi("s_lngT", [128, 6 * NCH]); slnbT = gi("s_lnbT", [128, 6 * NCH])

    es = ExitStack()
    with es:
        prog = Prog(nc, es)
        ctx = Ctx(nc, es, prog)
        streams = {"xs": prog.stream("d_xs", "sp"),
                   "setup": prog.stream("d_setup", "sp"),
                   "tr": prog.stream("d_tr", "sp"),
                   "wg": prog.stream("d_wg", "pool"),
                   "sw": prog.stream("d_sw", "sp")}
        for i in range(6):
            prog.stream(f"d_W{i}", "sp")
            prog.stream(f"dp_W{i}", "pool")
        make_std(ctx, n_w_tiles=4, w_cols=1024, psum=False, xs_cols=S + 1)
        ctx.xs_off = 1
        t = ctx.t
        sb, ps = ctx.sb, ctx.ps

        sb("TRIUb", [128, 128], BF16)
        sb("TRILb", [128, 128], BF16)
        sb("RHTb", [128, 128], BF16)
        sb("EHD2b", [128, 2], BF16)
        sb("E2Tf", [2, 128], F32)
        sb("B1T", [128, 32], F32)
        sb("B2T", [128, NCH], F32)
        sb("B2H", [128, NCH], F32)
        for nm in ("QSC", "QSPC", "KSC", "KSPC"):
            sb(nm, [128, 6], F32)
        sb("B1C", [128, 6 * 44], F32)
        sb("B2C", [128, 6 * NCH], F32)
        sb("LNG", [128, 6 * NCH], F32)
        sb("LNB", [128, 6 * NCH], F32)
        for nm, src_ in (("B1T", fb1T), ("B2T", fb2T), ("QSC", sqsT),
                         ("QSPC", sqspT), ("KSC", sksT), ("KSPC", skspT),
                         ("B1C", sb1T), ("B2C", sb2T), ("LNG", slngT),
                         ("LNB", slnbT), ("E2Tf", None)):
            if nm == "E2Tf":
                src_ = nc.dram_tensor("CONST_E2T", [2, 128], F32,
                                      kind="ExternalInput").ap()
            prog.op("sp", lambda g, nm=nm, src_=src_: g.dma_start(
                out=t[nm][:], in_=src_),
                reads=(f"hbm:{nm}",), writes=(nm,), stream=streams["setup"])
        load_constants(ctx, streams)
        for nm in ("TRIL", "RHT", "EHD2", "TRIU"):
            shp = {"TRIL": [128, 128], "RHT": [128, 128], "EHD2": [128, 2],
                   "TRIU": [128, 128]}[nm]
            ap = nc.dram_tensor("CONST_" + nm, shp, F32, kind="ExternalInput").ap()
            dstn = nm + "b"
            prog.op("pool", lambda g, ap=ap, dstn=dstn: g.dma_start(
                out=t[dstn][:], in_=ap),
                reads=(f"c:{nm}",), writes=(dstn,), stream=streams["wg"])
        prog.freeze_stream("d_setup")
        tb_f32 = {nm: nc.dram_tensor("CONST_" + nm, [128, S], F32,
                                     kind="ExternalInput").ap()
                  for nm in ("TBQC", "TBQS", "TBKC", "TBKS")}
        def bfdecl(name, shape):
            return nc.dram_tensor(name + "_bf", shape, BF16).ap()
        fw1b = bfdecl("l0f_w1", [32, 128, D])
        fw2b = bfdecl("l0f_w2", [NCH, 128, 4096])
        awqb, awkb, awvb, awob = (bfdecl(n, [NCH, 128, D]) for n in
                                  ("l0a_wq", "l0a_wk", "l0a_wv", "l0a_wo"))
        cwqb, cwkb, cwvb, cwob = (bfdecl(n, [NCH, 128, D]) for n in
                                  ("l1a_wq", "l1a_wk", "l1a_wv", "l1a_wo"))
        swqkvb = bfdecl("s_wqkv", [6, 24, 128, D])
        swob = bfdecl("s_wo", [6, NCH, 128, D])
        sw1b = bfdecl("s_w1", [6, 44, 128, D])
        sw2b = bfdecl("s_w2", [6, NCH, 128, 2816])
        tb_aps = {nm: bfdecl("TB" + nm, [128, S]) for nm in
                  ("TBQC", "TBQS", "TBKC", "TBKS")}
        groups = [
            ("cv_l0f", [(fw1_[i], fw1b[i]) for i in range(32)] +
                       [(fw2_[i], fw2b[i]) for i in range(NCH)]),
            ("cv_l0a", [(a[i], b_[i]) for a, b_ in
                        ((awq, awqb), (awk, awkb), (awv, awvb), (awo, awob))
                        for i in range(NCH)]),
            ("cv_tb", [(tb_f32[nm], tb_aps[nm]) for nm in tb_f32]),
        ]
        for si in range(6):
            groups.append((f"cv_s{si}",
                           [(swqkv[si][i], swqkvb[si][i]) for i in range(24)] +
                           [(sw1[si][i], sw1b[si][i]) for i in range(44)] +
                           [(sw2[si][i], sw2b[si][i]) for i in range(NCH)] +
                           [(swo[si][i], swob[si][i]) for i in range(NCH)]))
        groups.append(("cv_l1a", [(a[i], b_[i]) for a, b_ in
                                  ((cwq, cwqb), (cwk, cwkb), (cwv, cwvb),
                                   (cwo, cwob)) for i in range(NCH)]))
        with ctx.scope():
            for i in range(4):
                sb(f"CV{i}", [128, 4096], BF16)
            cvi = 0
            for gname, pairs in groups:
                sin = prog.stream("din_" + gname, "pool")
                sout = prog.stream("dout_" + gname, "sp")
                for srcap, dstap in pairs:
                    cv = f"CV{cvi % 4}"
                    cvi += 1
                    ncols = srcap.shape[-1]
                    p_ = prog
                    p_.op("pool", lambda g, cv=cv, srcap=srcap, ncols=ncols:
                          g.dma_start(out=t[cv][:, 0:ncols], in_=srcap),
                          reads=("hbm:" + srcap.tensor.name,), writes=(cv,),
                          stream=sin)
                    p_.op("sp", lambda g, cv=cv, dstap=dstap, ncols=ncols:
                          g.dma_start(out=dstap, in_=t[cv][:, 0:ncols]),
                          reads=(cv,), writes=("hbm:" + dstap.tensor.name,),
                          stream=sout)
        prog.op("dve", lambda v: v.tensor_scalar(
            t["B2H"][:], t["B2T"][:], 0.5, None, op0=ALU.mult),
            reads=("B2T",), writes=("B2H",))

        def l0ffn_scope():
            with ctx.scope():
                for i in range(32):
                    sb(f"H{i}", [128, 1024], BF16)
                sb("W4", [128, 2048], BF16)
                sb("W5", [128, 2048], BF16)
                for i in range(4):
                    ps(f"P{i}", [128, 512], F32)
                ps("P4", [128, 512], F32)
                ps("P5", [128, 512], F32)
                ps("P6", [128, 512], F32)
                ps("P7", [128, 512], F32)
                ctx.ln_banks = ("P6", "P7", "P4", "P5")
                emit_cast_x(ctx)
                emit_ln(ctx)
                emit_ffn_l0(ctx, fw1b, lambda c4: t["B1T"][:, c4:c4 + 1], fw2b,
                            lambda c: t["B2H"][:, c:c + 1], streams,
                            bias_regions=("B1T", "B2H"))

        def attn_scope(*, bucket, tag):
            with ctx.scope():
                sb("QP", [128, S], BF16)
                sb("KP", [128, S], BF16)
                sb("VP", [128, S], BF16)
                sb("VTM0", [128, 16 * 65], BF16)
                sb("W4", [128, 1024], BF16)
                sb("W5", [128, 1024], BF16)
                for ch in range(NCH):
                    sb(f"ZB{ch}", [128, S], BF16)
                ps("PS0", [128, 512], F32)
                ps("PS1", [128, 512], F32)
                ps("PS2", [128, 512], F32)
                ps("PZ", [65, 2048], F32)
                ps("PTB", [128, 1024], BF16)
                ctx.ln_banks = ("PS0", "PS1")
                if bucket:
                    sb("SROW", [1, TT], F32)
                    sb("WRQ", [128, 1024], F32)
                    sb("WRK", [128, 1024], F32)
                    sb("NCSQ", [1, 128], F32)
                    sb("NCSK", [1, 128], F32)
                    sb("BQR", [1, 128], F32)
                    sb("BKR", [1, 128], F32)
                    for nm, src_ in (("BQ8", abqT), ("BKC", abkT),
                                     ("BVC", abvT), ("BOC", aboT)):
                        dd = sb(nm, [128, NCH], F32)
                        prog.op("sp", lambda g, dd=dd, src_=src_: g.dma_start(
                            out=dd[:], in_=src_),
                            reads=(f"hbm:{nm}2",), writes=(nm,), stream=streams["tr"])
                    prog.op("dve", lambda v: v.tensor_scalar(
                        t["BQ8"][:], t["BQ8"][:], 0.125, None, op0=ALU.mult),
                        reads=("BQ8",), writes=("BQ8",))
                    with ctx.scope():
                        sb("WTMP", [128, 1024], F32)
                        sb("REXP", [128, 1024], F32)
                        prog.op("sp", lambda g: g.dma_start(
                            out=t["REXP"][:], in_=arexp),
                            reads=("hbm:rexp",), writes=("REXP",), stream=streams["tr"])
                        emit_wr_setup_inline(ctx, awqT, awkT, streams)
                    sb("AUGQ", [72, S], BF16)
                    sb("AUGK", [72, S], BF16)
                    sb("TRANSQ", [128, S], BF16)
                    sb("TRANSK", [128, S], BF16)
                    sb("OHSB", [128, 128], BF16)
                    sb("M16", [128, 16], F32)
                prog.op("dve", lambda v: v.memset(
                    bass.AP(t["VTM0"], 64, [[1040, 128], [65, 16], [1, 1]]), 1.0),
                    writes=("VTM0",))
                emit_cast_x(ctx)
                if bucket:
                    emit_ln(ctx, srow=True,
                            tile_hook=lambda tt: emit_qr_onehot(ctx, tt, streams))
                    emit_full_attn(ctx, streams, bucket=True, causal=False,
                                   wq_hbm=awqb, wk_hbm=awkb, wv_hbm=awvb,
                                   wo_hbm=awob,
                                   bq8_col=lambda c: t["BQ8"][:, c:c + 1],
                                   bk_col=lambda c: t["BKC"][:, c:c + 1],
                                   bv_col=lambda c: t["BVC"][:, c:c + 1],
                                   bo_col=lambda c: t["BOC"][:, c:c + 1],
                                   param_regions=("BQ8", "BKC", "BVC", "BOC"),
                                   tag=tag)
                else:
                    emit_ln(ctx)
                    emit_full_attn(ctx, streams, bucket=False, causal=True,
                                   wq_hbm=cwqb, wk_hbm=cwkb, wv_hbm=cwvb,
                                   wo_hbm=cwob, tag=tag)

        for seq in range(nseq):
            emit_load_x(ctx, x_hbm, seq, streams)
            for c in range(NCH):
                prog.op("dve", lambda v, c=c: v.memset(t[f"XS{c}"][:, 0:1], 0.0),
                        writes=(f"XS{c}",))
            prog.cur_phase = f"l0ffnA{seq}"
            l0ffn_scope()
            prog.cur_phase = f"l0attn{seq}"
            attn_scope(bucket=True, tag=f"l0a{seq}")
            prog.cur_phase = f"l0ffnB{seq}"
            l0ffn_scope()
            for call in range(2):
                for c in range(NCH):
                    prog.op("sp", lambda g, c=c: g.dma_start(
                        out=spill[c * 128:(c + 1) * 128, :],
                        in_=t[f"XS{c}"][:, 1:1 + S]),
                        reads=(f"XS{c}",), writes=("hbm:spill",), stream=streams["sw"])
                prog.cur_phase = f"lnout{seq}_{call}"
                with ctx.scope():
                    ps("P6", [128, 512], F32)
                    ps("P7", [128, 512], F32)
                    ctx.ln_banks = ("P6", "P7")
                    emit_cast_x(ctx)
                    emit_ln(ctx, out_f32="XS")
                with ctx.scope():
                    for nm in ("TQ1", "TQ2", "TK1", "TK2"):
                        sb(nm, [128, S], BF16)
                    for i in range(6):
                        with ctx.scope():
                            sb("QP", [128, S], BF16)
                            sb("KP", [128, S], BF16)
                            sb("VP", [128, S], BF16)
                            sb("VTM0", [128, 16 * 65], BF16)
                            sb("VTM1", [128, 16 * 65], BF16)
                            sb("W4", [128, 1024], BF16)
                            sb("W5", [128, 1024], BF16)
                            sb("RN20", [2, TT], F32)
                            sb("RN21", [2, TT], F32)
                            sb("NM0", [128, 1], F32)
                            sb("NM1", [128, 1], F32)
                            for k_ in range(4, 8):
                                sb(f"SQ{k_}", [128, TT], BF16)
                            for k_ in range(2):
                                sb(f"PTT{k_}", [128, 384], BF16)
                            for ch in range(NCH):
                                sb(f"ZB{ch}", [128, S], BF16)
                            for k_ in range(4):
                                ps(f"PS{k_}", [128, 512], F32)
                            ps("PZ0", [65, 512], F32)
                            ps("PZ1", [65, 512], F32)
                            ps("PTB", [128, 1024], BF16)
                            prog.cur_phase = f"asub{seq}_{call}_{i}"
                            for vtn in ("VTM0", "VTM1"):
                                prog.op("dve", lambda v, vtn=vtn: v.memset(
                                    bass.AP(t[vtn], 64,
                                            [[1040, 128], [65, 16], [1, 1]]), 1.0),
                                    writes=(vtn,))
                            emit_tables(ctx, streams, tb_aps,
                                        t["QSC"][:, i:i + 1], t["QSPC"][:, i:i + 1],
                                        t["KSC"][:, i:i + 1], t["KSPC"][:, i:i + 1],
                                        param_regions=("QSC", "QSPC", "KSC", "KSPC"))
                            emit_cast_shift(ctx)
                            emit_attn_sublayer(
                                ctx, streams, swqkvb[i], swob[i],
                                t["QSC"][:, i:i + 1], t["QSPC"][:, i:i + 1],
                                t["KSC"][:, i:i + 1], t["KSPC"][:, i:i + 1],
                                param_regions=("QSC", "QSPC", "KSC", "KSPC"),
                                tag=f"s{seq}{call}{i}")
                        with ctx.scope():
                            for k_ in range(22):
                                sb(f"H{k_}", [128, 1024], BF16)
                            sb("W4", [128, 2816], BF16)
                            sb("W5", [128, 2816], BF16)
                            for k_ in range(4):
                                ps(f"P{k_}", [128, 512], F32)
                            ps("P4", [128, 512], F32)
                            ps("P5", [128, 512], F32)
                            ps("P6", [128, 512], F32)
                            ps("P7", [128, 512], F32)
                            ctx.ln_banks = ("P6", "P7", "P4", "P5")
                            prog.cur_phase = f"fsub{seq}_{call}_{i}"
                            emit_cast_shift(ctx)
                            ctx.ln_param_regions = ("LNG", "LNB")
                            emit_ln(ctx,
                                    g_col=lambda c, i=i: t["LNG"][:, 8 * i + c:8 * i + c + 1],
                                    b_col=lambda c, i=i: t["LNB"][:, 8 * i + c:8 * i + c + 1])
                            emit_ffn_gated(
                                ctx, sw1b[i],
                                lambda oc, i=i: t["B1C"][:, 44 * i + oc:44 * i + oc + 1],
                                sw2b[i],
                                lambda c, i=i: t["B2C"][:, 8 * i + c:8 * i + c + 1],
                                streams, None, None,
                                param_regions=("B1C", "B2C"), tag=f"f{seq}{call}{i}")
                ctx_sc = ctx.scope()
                ctx_sc.__enter__()
                sb("XOT0", [128, TT], F32)
                sb("XOT1", [128, TT], F32)
                for c in range(NCH):
                    for tt in range(NTT):
                        sl = slice(tt * TT, (tt + 1) * TT)
                        xsl = slice(1 + tt * TT, 1 + (tt + 1) * TT)
                        xo = "XOT0" if tt % 2 == 0 else "XOT1"
                        prog.op("sp", lambda g, c=c, sl=sl, xo=xo: g.dma_start(
                            out=t[xo][:, 0:TT], in_=spill[c * 128:(c + 1) * 128, sl]),
                            reads=("hbm:spill",), writes=(xo,), stream=streams["sw"])
                        prog.op("dve", lambda v, c=c, xsl=xsl, xo=xo:
                                v.scalar_tensor_tensor(
                                    t[f"XS{c}"][:, xsl], t[f"XS{c}"][:, xsl], 0.5,
                                    t[xo][:, 0:TT], op0=ALU.mult, op1=ALU.add),
                                reads=(f"XS{c}", xo), writes=(f"XS{c}",))
                ctx_sc.__exit__(None, None, None)
                if call == 0:
                    prog.cur_phase = f"l1attn{seq}"
                    attn_scope(bucket=False, tag=f"l1a{seq}")
            emit_store_x(ctx, y_hbm, seq, streams)
        prog.wait_region("sp", tuple(f"hbm:x{s_}" for s_ in range(nseq)))
        prog.finalize()
    _LAST_PROG = prog
    return nc


def host_pack_all(inputs):
    gi = lambda k: np.ascontiguousarray(np.asarray(inputs[k], np.float32))
    consts = host_constants()
    bT = lambda b, n: np.ascontiguousarray(np.asarray(b, np.float32).reshape(n, 128).T)

    m = {}
    m["l0ffn"] = dict(w1=pack_w(gi("L0_ffn_w1")), w2=pack_w(gi("L0_ffn_w2")),
                      b1T=bT(inputs["L0_ffn_b1"], 32), b2T=bT(inputs["L0_ffn_b2"], 8),
                      **consts)
    Wq, Wk = gi("L0_attn_Wq"), gi("L0_attn_Wk")
    R = gi("L0_attn_R")
    R_exp = np.zeros((1024, 128), np.float32)
    for h in range(16):
        R_exp[64 * h:64 * h + 64, 8 * h:8 * h + 8] = R.T
    rexp_host = np.ascontiguousarray(
        R_exp.reshape(8, 128, 128).transpose(1, 0, 2).reshape(128, 1024))
    m["l0attn"] = dict(
        wq=pack_w(Wq), wk=pack_w(Wk), wv=pack_w(gi("L0_attn_Wv")),
        wo=pack_w(gi("L0_attn_Wo")),
        wqT=pack_w(np.ascontiguousarray(Wq.T)),
        wkT=pack_w(np.ascontiguousarray(Wk.T)), rexp=rexp_host,
        bqT=bT(inputs["L0_attn_bq"], 8), bkT=bT(inputs["L0_attn_bk"], 8),
        bvT=bT(inputs["L0_attn_bv"], 8), boT=bT(inputs["L0_attn_bo"], 8),
        **consts)
    tohm = lambda W: np.ascontiguousarray(
        np.transpose(np.asarray(W, np.float32), (2, 0, 1)).reshape(1024, 1024))
    m["l1attn"] = dict(
        wq=pack_w(tohm(gi("L1_attn_W_Q"))), wk=pack_w(tohm(gi("L1_attn_W_K"))),
        wv=pack_w(tohm(gi("L1_attn_W_V"))),
        wo=pack_w(np.ascontiguousarray(gi("L1_attn_W_O").T)), **consts)
    Wqkv = gi("L1_ffn_Wqkv")
    Wol = gi("L1_ffn_Wo")
    qsc = gi("L1_ffn_q_scale")
    ksc = gi("L1_ffn_k_scale")
    W1 = gi("L1_ffn_W1")
    b1 = gi("L1_ffn_b1")
    W2 = gi("L1_ffn_W2")
    b2 = gi("L1_ffn_b2")
    lng = gi("L1_ffn_ln_g")
    lnb = gi("L1_ffn_ln_b")
    DI = 2730
    m["asub"] = []
    m["fsub"] = []
    st = lambda v: np.ascontiguousarray(np.concatenate([v, v])[:, None])
    for i in range(6):
        qs, ks = qsc[i], ksc[i]
        m["asub"].append(dict(
            wqkv=pack_w(Wqkv[i]), wo=pack_w(Wol[i]),
            qsT=st(qs), qspT=st(np.concatenate([qs[32:], qs[:32]])),
            ksT=st(ks), kspT=st(np.concatenate([ks[32:], ks[:32]])), **consts))
        W1p = np.zeros((1024, 5632), np.float32)
        W1p[:, 0:DI] = W1[i][:, 0:DI]
        W1p[:, 2816:2816 + DI] = W1[i][:, DI:]
        b1p = np.zeros(5632, np.float32)
        b1p[0:DI] = b1[i][:DI]
        b1p[2816:2816 + DI] = b1[i][DI:]
        W2p = np.zeros((2816, 1024), np.float32)
        W2p[0:DI] = W2[i]
        m["fsub"].append(dict(
            w1=pack_w(W1p), w2=pack_w(W2p), b1T=bT(b1p, 44), b2T=bT(b2[i], 8),
            lngT=bT(lng[i], 8), lnbT=bT(lnb[i], 8), **consts))
    m["lnout"] = dict(**consts)
    m["axpy"] = {}
    return m




def host_pack_fused(inputs):
    m = host_pack_all(inputs)
    consts = host_constants()
    out = {}
    out.update({"l0f_w1": m["l0ffn"]["w1"], "l0f_w2": m["l0ffn"]["w2"],
                "l0f_b1T": m["l0ffn"]["b1T"], "l0f_b2T": m["l0ffn"]["b2T"]})
    a = m["l0attn"]
    out.update({"l0a_wq": a["wq"], "l0a_wk": a["wk"], "l0a_wv": a["wv"],
                "l0a_wo": a["wo"], "l0a_wqT": a["wqT"], "l0a_wkT": a["wkT"],
                "l0a_rexp": a["rexp"], "l0a_bqT": a["bqT"], "l0a_bkT": a["bkT"],
                "l0a_bvT": a["bvT"], "l0a_boT": a["boT"]})
    c = m["l1attn"]
    out.update({"l1a_wq": c["wq"], "l1a_wk": c["wk"], "l1a_wv": c["wv"],
                "l1a_wo": c["wo"]})
    out["s_wqkv"] = np.stack([m["asub"][i]["wqkv"] for i in range(6)])
    out["s_wo"] = np.stack([m["asub"][i]["wo"] for i in range(6)])
    out["s_w1"] = np.stack([m["fsub"][i]["w1"] for i in range(6)])
    out["s_w2"] = np.stack([m["fsub"][i]["w2"] for i in range(6)])
    out["s_qsT"] = np.concatenate([m["asub"][i]["qsT"] for i in range(6)], 1)
    out["s_qspT"] = np.concatenate([m["asub"][i]["qspT"] for i in range(6)], 1)
    out["s_ksT"] = np.concatenate([m["asub"][i]["ksT"] for i in range(6)], 1)
    out["s_kspT"] = np.concatenate([m["asub"][i]["kspT"] for i in range(6)], 1)
    out["s_b1T"] = np.concatenate([m["fsub"][i]["b1T"] for i in range(6)], 1)
    out["s_b2T"] = np.concatenate([m["fsub"][i]["b2T"] for i in range(6)], 1)
    out["s_lngT"] = np.concatenate([m["fsub"][i]["lngT"] for i in range(6)], 1)
    out["s_lnbT"] = np.concatenate([m["fsub"][i]["lnbT"] for i in range(6)], 1)
    out.update(consts)
    return out


def kernel_fused(**inputs):
    x = np.asarray(inputs["L0_attn_x"], np.float32)
    wm = host_pack_fused(inputs)
    nc, innames = _get_prog("fused", build_fused)
    ims = []
    for c in range(NCORES):
        im = {k: v for k, v in wm.items() if k in innames}
        im["x"] = np.ascontiguousarray(x[2 * c:2 * c + 2].transpose(0, 2, 1))
        ims.append(im)
    res = run_bass_kernel_spmd(nc, ims, core_ids=list(range(NCORES)))
    out = np.empty((16, S, D), np.float32)
    for c in range(NCORES):
        out[2 * c:2 * c + 2] = res.results[c]["y"].transpose(0, 2, 1)
    return out


_PROG_CACHE = {}


def _get_prog(name, builder):
    if name not in _PROG_CACHE:
        nc = builder()
        innames = set()
        for alloc in nc.m.functions[0].allocations:
            if isinstance(alloc, mybir.MemoryLocationSet) and alloc.kind == "ExternalInput":
                innames.add(alloc.memorylocations[0].name)
        _PROG_CACHE[name] = (nc, innames)
    return _PROG_CACHE[name]


def kernel(**inputs):
    return kernel_fused(**inputs)


def kernel_unfused(**inputs):
    x = np.asarray(inputs["L0_attn_x"], np.float32)
    m = host_pack_all(inputs)
    nseq = 2
    cores = list(range(NCORES))
    xs = [np.ascontiguousarray(x[2 * c:2 * c + 2].transpose(0, 2, 1))
          for c in range(NCORES)]

    def launch(name, builder, wm, xkey="x", extra=None):
        nc, innames = _get_prog(name, builder)
        ims = []
        for c in range(NCORES):
            im = {k: v for k, v in wm.items() if k in innames}
            im[xkey] = xs[c]
            if extra is not None:
                im["yin"] = extra[c]
            ims.append(im)
        res = run_bass_kernel_spmd(nc, ims, core_ids=cores)
        return [res.results[c]["y"] for c in range(NCORES)]

    def set_x(out):
        for c in range(NCORES):
            xs[c] = out[c]

    set_x(launch("l0ffn", build_l0ffn, m["l0ffn"]))
    set_x(launch("l0attn", build_l0attn, m["l0attn"]))
    set_x(launch("l0ffn", build_l0ffn, m["l0ffn"]))
    for call in range(2):
        inner = launch("lnout", build_ln_out, m["lnout"])
        saved = xs
        xs = inner
        for i in range(6):
            set_x(launch("asub", build_l1ffn_asub, m["asub"][i]))
            set_x(launch("fsub", build_l1ffn_fsub, m["fsub"][i]))
        inner = xs
        xs = saved
        set_x(launch("axpy", build_axpy, m["axpy"], extra=inner))
        if call == 0:
            set_x(launch("l1attn", build_l1attn, m["l1attn"]))
    out = np.empty((16, S, D), np.float32)
    for c in range(NCORES):
        out[2 * c:2 * c + 2] = xs[c].transpose(0, 2, 1)
    return out

